# version 05
# baseline (speedup 1.0000x reference)
import numpy as np
from contextlib import ExitStack
import concourse.bass as bass
import concourse.mybir as mybir
from concourse.bass_utils import run_bass_kernel_spmd
from concourse.aot_env import allow_ici_all_to_all
import ml_dtypes

F32 = mybir.dt.float32
BF16 = mybir.dt.bfloat16
I32 = mybir.dt.int32
AF = mybir.ActivationFunctionType
ALU = mybir.AluOpType
AX = mybir.AxisListType
BF = ml_dtypes.bfloat16

NCORE = 8
B = 2
T = 8192
D = 1024
TOK = 2048
TT = 512
NT = TOK // TT
ND = D // 128
EPS = 1e-5
SLOT = 262144
RG = [[0, 1, 2, 3, 4, 5, 6, 7]]


class KB:
    def __init__(self):
        self.nc = bass.Bass("TRN2", target_bir_lowering=False)
        nc = self.nc
        self.es = ExitStack()
        self.ps_es = self.es
        self.V, self.S, self.T, self.SY, self.G = nc.vector, nc.scalar, nc.tensor, nc.sync, nc.gpsimd
        self.counts = {}
        self.sems = {}
        self.waited = {}
        self.tag = "g"
        self.cur = self.es
        for s in ("ld", "st", "gd", "t", "v", "s", "g", "cc"):
            self.sems[s] = self.es.enter_context(nc.semaphore("sem_" + s))
            self.counts[s] = 0

    def inc(self, ins, name, k=1):
        self.counts[name] += k
        ins.then_inc(self.sems[name], k)
        return self.counts[name]

    def n(self, name):
        return self.counts[name]

    def incw(self, eng, ins, name, k=1):
        self.inc(ins, name, k)
        self.wait(eng, name)

    def wait(self, eng, name, val=None):
        if val is None:
            val = self.counts[name]
        if val <= 0:
            return
        key = (id(eng), name)
        if self.waited.get(key, 0) >= val:
            return
        eng.wait_ge(self.sems[name], val)
        self.waited[key] = val

    def sb(self, name, shape, dt):
        return self.cur.enter_context(self.nc.sbuf_tensor(self.tag + "_" + name, shape, dt))

    def ps(self, name, shape, dt=F32):
        return self.cur.enter_context(self.nc.psum_tensor(self.tag + "_" + name, shape, dt))

    def din(self, name, shape, dt):
        return self.nc.dram_tensor(name, shape, dt, kind="ExternalInput").ap()

    def dout(self, name, shape, dt):
        return self.nc.dram_tensor(name, shape, dt, kind="ExternalOutput").ap()

    def dint(self, name, shape, dt):
        return self.nc.dram_tensor(name, shape, dt).ap()

    def barrier(self):
        for eng in (self.SY, self.V, self.S, self.T, self.G):
            for s in self.counts:
                self.wait(eng, s)

    def phase(self, tag):
        self.barrier()
        if self.cur is not self.es:
            self.cur.close()
        self.cur = ExitStack()
        self.tag = tag

    def collective(self, kind, src_ap, dst_ap):
        G = self.G
        self.wait(G, "st")
        self.wait(G, "gd")
        if kind == "AllToAll":
            with allow_ici_all_to_all():
                ins = G.collective_compute(kind, mybir.AluOpType.bypass,
                                           replica_groups=RG, ins=[src_ap.opt()],
                                           outs=[dst_ap.opt()])
        else:
            ins = G.collective_compute(kind, mybir.AluOpType.bypass,
                                       replica_groups=RG, ins=[src_ap.opt()],
                                       outs=[dst_ap.opt()])
        self.inc(ins, "cc", 1)

    def fin(self):
        self.barrier()
        if self.cur is not self.es:
            self.cur.close()
            self.cur = self.es
        self.es.close()
        return self.nc

    _fc_n = 0
    def fconst(self, val, parts=1):
        key = ("fc", val, parts)
        if not hasattr(self, "_fcache"):
            self._fcache = {}
        if key not in self._fcache:
            KB._fc_n += 1
            sv = self.cur
            self.cur = self.es
            tns = self.sb(f"fc{KB._fc_n}_{parts}", [parts, 1], F32)
            self.cur = sv
            self.inc(self.G.memset(tns[:], float(val)), "g")
            self._fcache[key] = tns
        return self._fcache[key][:]


def qk_slot(X, s, k):
    return X[s:s + 1, k:k + 1, :].rearrange("a b (p c) -> (a b p) c", p=256)


def v_slot(X, s):
    return X[s:s + 1, 2:3, :].rearrange("a b (t f) -> (a b t) f", t=1024)


def y_slot(X, s):
    return X[s:s + 1, :].rearrange("a (p c) -> (a p) c", p=256)


def emit_consts(kb):
    c = {}
    sv = kb.cur
    kb.cur = kb.es
    c["ones_col"] = kb.sb("c_ones_col", [128, 1], BF16)
    c["ones_row"] = kb.sb("c_ones_row", [1, 128], BF16)
    c["idn2"] = kb.sb("c_idn2", [128, 64], BF16)
    kb.cur = sv
    G = kb.G
    kb.inc(G.memset(c["ones_col"][:], 1.0), "g")
    kb.inc(G.memset(c["ones_row"][:], 1.0), "g")
    kb.inc(G.memset(c["idn2"][0:64, :], 1.0), "g")
    kb.inc(G.affine_select(out=c["idn2"][0:64, :], in_=c["idn2"][0:64, :], compare_op=ALU.is_equal,
                           fill=0.0, base=0, pattern=[[-1, 64]], channel_multiplier=1), "g")
    kb.wait(kb.V, "g")
    kb.inc(kb.V.tensor_copy(c["idn2"][64:128, :], c["idn2"][0:64, :]), "v")
    return c


def emit_diag_masks(kb, with_anti):
    G = kb.G
    dm = kb.sb("dmask", [128, 4 * 512], BF16)
    kb.inc(G.memset(dm[:], 1.0), "g")
    for r in range(4):
        kb.inc(G.affine_select(out=dm[:, r * 512:(r + 1) * 512], in_=dm[:, r * 512:(r + 1) * 512],
                               compare_op=ALU.is_ge, fill=0.0, base=-128 * r,
                               pattern=[[1, 512]], channel_multiplier=-1), "g")
    am = None
    if with_anti:
        am = kb.sb("amask", [128, 4 * 512], BF16)
        kb.inc(G.memset(am[:], 1.0), "g")
        for r in range(4):
            kb.inc(G.affine_select(out=am[:, r * 512:(r + 1) * 512], in_=am[:, r * 512:(r + 1) * 512],
                                   compare_op=ALU.is_ge, fill=0.0, base=128 * r - 1,
                                   pattern=[[-1, 512]], channel_multiplier=1), "g")
    return dm, am


class LNW:
    def __init__(self, kb, tag):
        self.xb = kb.sb(tag + "_xb", [128, 512], BF16)
        self.xsq = kb.sb(tag + "_xsq", [128, 512], BF16)
        self.nm = kb.sb(tag + "_nm", [1, 512], F32)
        self.ex2 = kb.sb(tag + "_ex2", [1, 512], F32)
        self.m2 = kb.sb(tag + "_m2", [1, 512], F32)
        self.var = kb.sb(tag + "_var", [1, 512], F32)
        self.rs = kb.sb(tag + "_rs", [1, 512], F32)
        self.nmb = kb.sb(tag + "_nmb", [1, 512], BF16)
        self.rsb = kb.sb(tag + "_rsb", [1, 512], BF16)
        self.t1 = kb.sb(tag + "_t1", [128, 512], F32)


def emit_stats(kb, w, get_src, c, P_s, P_q, rms=False, src_bf16=False, ncols=512):
    V, S, Tn = kb.V, kb.S, kb.T
    kb.wait(Tn, "v")
    kb.wait(Tn, "s")
    kb.wait(S, "v")
    for kc in range(ND):
        src = get_src(kc)
        if not rms:
            if src_bf16:
                xb = src
            else:
                kb.wait(V, "t")
                kb.inc(V.tensor_copy(w.xb[:, :ncols], src), "v")
                xb = w.xb[:, :ncols]
                kb.wait(Tn, "v")
            kb.wait(Tn, "g")
            kb.inc(Tn.matmul(P_s[0:1, :ncols], c["ones_col"][:], xb,
                             start=(kc == 0), stop=(kc == ND - 1)), "t")
        kb.wait(S, "t")
        kb.inc(S.activation(w.xsq[:, :ncols], src, AF.Square), "s")
        kb.wait(Tn, "s")
        kb.wait(Tn, "g")
        kb.inc(Tn.matmul(P_q[0:1, :ncols], c["ones_col"][:], w.xsq[:, :ncols],
                         start=(kc == 0), stop=(kc == ND - 1)), "t")
    kb.wait(V, "t")
    kb.inc(V.tensor_scalar_mul(w.ex2[:, :ncols], P_q[0:1, :ncols], 1.0 / D), "v")
    if rms:
        kb.wait(S, "v")
        kb.wait(S, "g")
        kb.inc(S.activation(w.var[:, :ncols], w.ex2[:, :ncols], AF.Ln, bias=kb.fconst(EPS)), "s")
        kb.inc(S.activation(w.rs[:, :ncols], w.var[:, :ncols], AF.Exp, scale=-0.5), "s")
        kb.wait(V, "s")
        kb.inc(V.tensor_copy(w.rsb[:, :ncols], w.rs[:, :ncols]), "v")
    else:
        kb.inc(V.tensor_scalar_mul(w.nm[:, :ncols], P_s[0:1, :ncols], -1.0 / D), "v")
        kb.inc(V.tensor_mul(w.m2[:, :ncols], w.nm[:, :ncols], w.nm[:, :ncols]), "v")
        kb.inc(V.tensor_sub(w.var[:, :ncols], w.ex2[:, :ncols], w.m2[:, :ncols]), "v")
        kb.wait(S, "v")
        kb.wait(S, "g")
        kb.inc(S.activation(w.m2[:, :ncols], w.var[:, :ncols], AF.Ln, bias=kb.fconst(EPS)), "s")
        kb.inc(S.activation(w.rs[:, :ncols], w.m2[:, :ncols], AF.Exp, scale=-0.5), "s")
        kb.wait(V, "s")
        kb.inc(V.tensor_copy(w.nmb[:, :ncols], w.nm[:, :ncols]), "v")
        kb.inc(V.tensor_copy(w.rsb[:, :ncols], w.rs[:, :ncols]), "v")


def emit_bcast(kb, c, row_b, P_dst, ncols=512):
    kb.wait(kb.T, "v")
    kb.wait(kb.T, "g")
    kb.inc(kb.T.matmul(P_dst[:, :ncols], c["ones_row"][:], row_b[0:1, :ncols],
                       start=True, stop=True), "t")


def load_x_fm(kb, dst_sb, src_dram, sem="ld", eng=None):
    ncols = src_dram.shape[1]
    src = src_dram.rearrange("(kc p) t -> p kc t", p=128)
    dst = dst_sb[:].rearrange("p (kc t) -> p kc t", t=ncols)
    kb.inc((eng or kb.SY).dma_start(out=dst, in_=src), sem, 16)


def emit_A1(kb, c, x, Wa, ba, Rq, X1i):
    kb.phase("a1")
    xsb = kb.sb("xsb", [128, ND * TOK], F32)
    Wsb = kb.sb("Wsb", [128, ND * 3 * D], BF16)
    basb = kb.sb("basb", [128, 24], F32)
    Rsb = kb.sb("Rsb", [128, D], BF16)
    xnb = kb.sb("xnb", [128, ND * 512], BF16)
    qtmp = kb.sb("qtmp", [128, 512], BF16)
    stg = [kb.sb(f"stg{i}", [128, 512], BF16) for i in range(2)]
    vt = [kb.sb(f"vt{i}", [128, 4 * D], BF16) for i in range(2)]
    w = LNW(kb, "ln")

    P_s = kb.ps("P_s", [1, 512])
    P_q = kb.ps("P_q", [1, 512])
    P_b1 = kb.ps("P_b1", [128, 512])
    P_b2 = kb.ps("P_b2", [128, 512])
    P_m = [kb.ps(f"P_m{i}", [128, 512]) for i in range(2)]
    P_r = kb.ps("P_r", [128, 512])
    P_t = kb.ps("P_t", [128, 64], BF16)

    SY, V, S, Tn, G = kb.SY, kb.V, kb.S, kb.T, kb.G
    load_x_fm(kb, xsb, x)
    kb.inc(SY.dma_start(out=basb[:], in_=ba), "ld", 16)
    kb.inc(SY.dma_start(out=Rsb[:], in_=Rq), "ld", 16)
    for kc in range(ND):
        kb.inc(G.dma_start(out=Wsb[:, kc * 3 * D:(kc + 1) * 3 * D],
                           in_=Wa[kc * 128:(kc + 1) * 128, :]), "gd", 16)
    kb.wait(V, "ld"); kb.wait(S, "ld")

    st_marks = {}
    for t in range(NT):
        xt = lambda kc: xsb[:, kc * TOK + t * 512: kc * TOK + (t + 1) * 512]
        emit_stats(kb, w, xt, c, P_s, P_q)
        emit_bcast(kb, c, w.nmb, P_b1)
        emit_bcast(kb, c, w.rsb, P_b2)
        for kc in range(ND):
            kb.wait(V, "t")
            kb.inc(V.tensor_add(w.t1[:], xt(kc), P_b1[:]), "v")
            kb.inc(V.tensor_mul(xnb[:, kc * 512:(kc + 1) * 512], w.t1[:], P_b2[:]), "v")
        kb.wait(Tn, "v")
        kb.wait(Tn, "gd")
        for oc in range(24):
            pm = P_m[oc % 2]
            kb.wait(Tn, "s")
            for kc in range(ND):
                kb.inc(Tn.matmul(pm[:], Wsb[:, kc * 3 * D + oc * 128: kc * 3 * D + (oc + 1) * 128],
                                 xnb[:, kc * 512:(kc + 1) * 512],
                                 start=(kc == 0), stop=(kc == ND - 1)), "t")
            kb.wait(S, "t")
            if oc < 16:
                kb.inc(S.activation(qtmp[:], pm[:], AF.Identity, bias=basb[:, oc:oc + 1]), "s")
                kb.wait(Tn, "s")
                kb.wait(Tn, "v")
                kb.inc(Tn.matmul(P_r[:], Rsb[:, (oc % 8) * 128:(oc % 8 + 1) * 128], qtmp[:],
                                 start=True, stop=True), "t")
                sg = stg[oc % 2]
                if id(sg) in st_marks:
                    kb.wait(V, "st", st_marks[id(sg)])
                kb.wait(V, "t")
                kb.inc(V.tensor_copy(sg[:], P_r[:]), "v")
                kb.wait(SY, "v")
                k = 0 if oc < 8 else 1
                j = oc % 8
                s = 4 * (t // 2) + j // 2
                slot = qk_slot(X1i, s, k)
                kb.inc(SY.dma_start(out=slot[128 * (j % 2):128 * (j % 2) + 128,
                                             512 * (t % 2):512 * (t % 2) + 512], in_=sg[:]), "st", 16)
                st_marks[id(sg)] = kb.n("st")
            else:
                vc = oc - 16
                vbuf = vt[t % 2]
                if (vc == 0) and (id(vbuf) in st_marks):
                    kb.wait(S, "st", st_marks[id(vbuf)])
                    kb.wait(V, "st", st_marks[id(vbuf)])
                kb.inc(S.activation(qtmp[:], pm[:], AF.Identity, bias=basb[:, oc:oc + 1]), "s")
                for hh in range(2):
                    for p4 in range(4):
                        kb.wait(Tn, "s")
                        kb.wait(Tn, "v")
                        kb.inc(Tn.transpose(P_t[:], qtmp[64 * hh:64 * hh + 64, 128 * p4:128 * (p4 + 1)],
                                            c["idn2"][64 * hh:64 * hh + 64, :]), "t")
                        kb.wait(V, "t")
                        col = 1024 * p4 + 64 * (2 * vc + hh)
                        kb.inc(V.tensor_copy(vbuf[:, col:col + 64], P_t[:]), "v")
                if vc == 7:
                    kb.wait(SY, "v")
                    for p4 in range(4):
                        for q in range(4):
                            s = 4 * (t // 2) + q
                            r0 = 512 * (t % 2) + 128 * p4
                            kb.inc(SY.dma_start(out=v_slot(X1i, s)[r0:r0 + 128, :],
                                                in_=vbuf[:, 1024 * p4 + 256 * q: 1024 * p4 + 256 * q + 256]),
                                   "st", 16)
                    st_marks[id(vbuf)] = kb.n("st")


def emit_flash(kb, c, Xo, Yi, l1_mode, tag):
    kb.phase(tag)
    dm, am = emit_diag_masks(kb, with_anti=l1_mode)

    Qsb = kb.sb("Qsb", [128, 2 * T], BF16)
    Ksb = kb.sb("Ksb", [128, 2 * T], BF16)
    Vsb = [kb.sb(f"Vsb{i}", [128, 64 * 65], BF16) for i in range(4)]
    Ysb = kb.sb("Ysb", [128, 2 * T], BF16)
    esT = [kb.sb(f"esT{i}", [128, 4 * 512], BF16) for i in range(2)]
    dr = kb.sb("dr", [1, 512], F32)
    drb = kb.sb("drb", [1, 512], BF16)
    bcs = kb.sb("bcs", [64, 512], F32)
    yt = kb.sb("yt", [65, 512], F32) if l1_mode else None
    tsb = kb.sb("tsb", [65, 64], F32) if l1_mode else None
    tstb = kb.sb("tstb", [64, 65], F32) if l1_mode else None
    utri = kb.sb("utri", [64, 64], F32) if l1_mode else None
    sufsb = kb.sb("sufsb", [65, 64], F32) if l1_mode else None
    idn65 = kb.sb("idn65", [65, 65], F32) if l1_mode else None

    P_S = [kb.ps(f"P_S{i}", [128, 3 * 512]) for i in range(2)]
    P_y = kb.ps("P_y", [128, 512])
    P_bc = kb.ps("P_bc", [128, 512])
    P_ts = P_bc[0:65, 0:64]
    P_tt = P_y[0:64, 0:65]

    SY, V, S, Tn, G = kb.SY, kb.V, kb.S, kb.T, kb.G
    if l1_mode:
        kb.inc(G.memset(utri[:], 1.0), "g")
        kb.inc(G.affine_select(out=utri[:], in_=utri[:], compare_op=ALU.is_ge,
                               fill=0.0, base=0, pattern=[[-1, 64]], channel_multiplier=1), "g")
        kb.inc(G.memset(idn65[:], 1.0), "g")
        kb.inc(G.affine_select(out=idn65[:], in_=idn65[:], compare_op=ALU.is_equal,
                               fill=0.0, base=0, pattern=[[-1, 65]], channel_multiplier=1), "g")

    kb.wait(SY, "cc")
    for pr in range(2):
        off = 131072 * pr
        kb.inc(SY.dma_start(out=Qsb[0:128, pr * T:(pr + 1) * T].rearrange("p (s cc) -> p s cc", cc=1024),
                            in_=Xo[:, 0:1, off:off + 131072].rearrange("s b (p cc) -> p s (b cc)", p=128)),
               "ld", 16)
        kb.inc(SY.dma_start(out=Ksb[0:128, pr * T:(pr + 1) * T].rearrange("p (s cc) -> p s cc", cc=1024),
                            in_=Xo[:, 1:2, off:off + 131072].rearrange("s b (p cc) -> p s (b cc)", p=128)),
               "ld", 16)
    for hh in range(4):
        for s in range(8):
            kb.inc(SY.dma_start(
                out=Vsb[hh][:].rearrange("p (j cc) -> p j cc", cc=65)[:, 8 * s:8 * s + 8, 0:64],
                in_=v_slot(Xo, s)[:, 64 * hh:64 * hh + 64].rearrange("(j p) cc -> p j cc", p=128)),
                "ld", 16)
    kb.wait(V, "ld")
    for hh in range(4):
        onescol = Vsb[hh][:].rearrange("p (j cc) -> p j cc", cc=65)[:, :, 64:65]
        kb.inc(V.memset(onescol, 1.0), "v")

    epi_mark = 0
    exp_mark = [0, 0]
    av_mark = [0, 0]
    for hh in range(4):
        pr = hh // 2
        qrow = 64 * (hh % 2)
        kb.wait(Tn, "ld")
        kb.wait(Tn, "v")
        if l1_mode:
            for j in range(64):
                kb.inc(Tn.matmul(P_ts[:, j:j + 1], Vsb[hh][:, j * 65:(j + 1) * 65],
                                 c["ones_col"][:], start=(j == 0), stop=(j == 63)), "t")
            kb.wait(V, "t")
            kb.inc(V.tensor_copy(tsb[:], P_ts[:, :]), "v")
            kb.wait(Tn, "v")
            kb.inc(Tn.transpose(P_tt[:, :], tsb[:], idn65[:]), "t")
            kb.wait(V, "t")
            kb.inc(V.tensor_copy(tstb[:], P_tt[:, :]), "v")
            kb.wait(Tn, "v")
            kb.inc(Tn.matmul(P_ts[:, :], tstb[:], utri[:], start=True, stop=True), "t")
            kb.wait(V, "t")
            kb.inc(V.tensor_copy(sufsb[:], P_ts[:, :]), "v")
            kb.wait(Tn, "v")
        for qt in range(16):
            g0 = 512 * qt
            nkt = 4 * (qt + 1)
            qap = Qsb[qrow:qrow + 64, pr * T + g0: pr * T + g0 + 512]
            groups = []
            k0 = 0
            while k0 < nkt:
                gs = min(3, nkt - k0)
                groups.append((k0, gs))
                k0 += gs
            def emit_qk(gi):
                k0, gs = groups[gi]
                par = gi % 2
                ps = P_S[par]
                kb.wait(Tn, "s", exp_mark[par])
                for i4 in range(gs):
                    kt = k0 + i4
                    ins = Tn.matmul(ps[:, i4 * 512:(i4 + 1) * 512],
                                    Ksb[qrow:qrow + 64, pr * T + kt * 128: pr * T + (kt + 1) * 128],
                                    qap, start=True, stop=True)
                    if i4 == gs - 1:
                        kb.inc(ins, "t")
                qk_marks[gi] = kb.n("t")
            qk_marks = {}
            emit_qk(0)
            for gi, (k0, gs) in enumerate(groups):
                par = gi % 2
                eb = esT[par]
                ps = P_S[par]
                kb.wait(S, "t", qk_marks[gi])
                kb.inc(S.activation(eb[:, 0:gs * 512], ps[:, 0:gs * 512], AF.Exp, scale=0.125), "s")
                exp_mark[par] = kb.n("s")
                need_v = False
                for i4 in range(gs):
                    kt = k0 + i4
                    r = kt - (nkt - 4)
                    if r >= 0:
                        if not need_v:
                            kb.wait(V, "s", exp_mark[par])
                            kb.wait(V, "g")
                            need_v = True
                        kb.inc(V.tensor_mul(eb[:, i4 * 512:(i4 + 1) * 512],
                                            eb[:, i4 * 512:(i4 + 1) * 512],
                                            dm[:, r * 512:(r + 1) * 512]), "v")
                        if l1_mode:
                            kb.inc(V.tensor_add(eb[:, i4 * 512:(i4 + 1) * 512],
                                                eb[:, i4 * 512:(i4 + 1) * 512],
                                                am[:, r * 512:(r + 1) * 512]), "v")
                if gi + 1 < len(groups):
                    emit_qk(gi + 1)
                if need_v:
                    kb.wait(Tn, "v", kb.n("v"))
                else:
                    kb.wait(Tn, "s", exp_mark[par])
                if gi == 0:
                    kb.wait(Tn, "v", epi_mark)
                for i4 in range(gs):
                    kt = k0 + i4
                    ins = Tn.matmul(P_y[0:65, :], Vsb[hh][:, kt * 65:(kt + 1) * 65],
                                    eb[:, i4 * 512:(i4 + 1) * 512],
                                    start=(kt == 0), stop=(kt == nkt - 1))
                    if i4 == gs - 1:
                        kb.inc(ins, "t")
                av_mark[par] = kb.n("t")
            kb.wait(V, "t")
            if l1_mode:
                jq = (g0 + 512) // 128
                if jq < 64:
                    kb.inc(V.tensor_scalar(yt[:], P_y[0:65, :], sufsb[:, jq:jq + 1], None, ALU.add), "v")
                else:
                    kb.inc(V.tensor_copy(yt[:], P_y[0:65, :]), "v")
                ysrc = yt
                kb.inc(V.reciprocal(dr[:], yt[64:65, :]), "v")
            else:
                ysrc = P_y
                kb.inc(V.reciprocal(dr[:], P_y[64:65, :]), "v")
            kb.inc(V.tensor_copy(drb[:], dr[:]), "v")
            kb.wait(Tn, "v")
            kb.inc(Tn.matmul(P_bc[0:64, :], c["ones_row"][0:1, 0:64], drb[:], start=True, stop=True), "t")
            kb.wait(V, "t")
            kb.inc(V.tensor_copy(bcs[:], P_bc[0:64, :]), "v")
            kb.inc(V.tensor_mul(Ysb[qrow:qrow + 64, pr * T + g0: pr * T + g0 + 512],
                                ysrc[0:64, :], bcs[:]), "v")
            epi_mark = kb.n("v")
    kb.wait(SY, "v")
    for hh in range(4):
        for s in range(8):
            kb.inc(SY.dma_start(out=y_slot(Yi, s)[64 * hh:64 * hh + 64, :],
                                in_=Ysb[64 * (hh % 2):64 * (hh % 2) + 64,
                                        (hh // 2) * T + 1024 * s: (hh // 2) * T + 1024 * s + 1024]),
                   "st", 16)


def emit_PS(kb, c, x, X2o, Wp, bp, xo):
    kb.phase("ps")
    xsb = kb.sb("xsb", [128, ND * TOK], F32)
    ysb = kb.sb("ysb", [128, ND * TOK], BF16)
    Wsb = kb.sb("Wsb", [128, ND * D], BF16)
    bpsb = kb.sb("bpsb", [128, 8], F32)
    zb = kb.sb("zb", [128, ND * 512], BF16)
    w = LNW(kb, "ln")

    P_s = kb.ps("P_s", [1, 512])
    P_q = kb.ps("P_q", [1, 512])
    P_b1 = kb.ps("P_b1", [128, 512])
    P_b2 = kb.ps("P_b2", [128, 512])
    P_m = [kb.ps(f"P_m{i}", [128, 512]) for i in range(2)]

    SY, V, S, Tn, G = kb.SY, kb.V, kb.S, kb.T, kb.G
    load_x_fm(kb, xsb, x)
    kb.wait(SY, "cc")
    for r in range(8):
        ys = y_slot(X2o, r)
        for half in range(2):
            kc = 2 * (r % 4) + half
            kb.inc(SY.dma_start(out=ysb[:, kc * TOK + 1024 * (r // 4): kc * TOK + 1024 * (r // 4) + 1024],
                                in_=ys[128 * half:128 * half + 128, :]), "ld", 16)
    for kc in range(ND):
        kb.inc(G.dma_start(out=Wsb[:, kc * D:(kc + 1) * D], in_=Wp[kc * 128:(kc + 1) * 128, :]), "gd", 16)
    kb.inc(SY.dma_start(out=bpsb[:], in_=bp), "ld", 16)
    kb.wait(Tn, "ld"); kb.wait(Tn, "gd"); kb.wait(S, "ld"); kb.wait(V, "ld")

    for t in range(NT):
        for oc in range(ND):
            pm = P_m[oc % 2]
            kb.wait(Tn, "s")
            for kc in range(ND):
                kb.inc(Tn.matmul(pm[:], Wsb[:, kc * D + oc * 128: kc * D + (oc + 1) * 128],
                                 ysb[:, kc * TOK + t * 512: kc * TOK + (t + 1) * 512],
                                 start=(kc == 0), stop=(kc == ND - 1)), "t")
            kb.wait(S, "t")
            kb.wait(S, "v")
            kb.inc(S.activation(zb[:, oc * 512:(oc + 1) * 512], pm[:], AF.Identity,
                                bias=bpsb[:, oc:oc + 1]), "s")
        kb.wait(V, "s")
        zt = lambda kc: zb[:, kc * 512:(kc + 1) * 512]
        emit_stats(kb, w, zt, c, P_s, P_q, src_bf16=True)
        emit_bcast(kb, c, w.nmb, P_b1)
        emit_bcast(kb, c, w.rsb, P_b2)
        for kc in range(ND):
            kb.wait(V, "t")
            xap = xsb[:, kc * TOK + t * 512: kc * TOK + (t + 1) * 512]
            kb.inc(V.tensor_add(w.t1[:], zt(kc), P_b1[:]), "v")
            kb.inc(V.tensor_mul(w.t1[:], w.t1[:], P_b2[:]), "v")
            kb.inc(V.tensor_add(xap, xap, w.t1[:]), "v")
        kb.wait(SY, "v")
        dst = xo.rearrange("(kc p) t -> p kc t", p=128)[:, :, t * 512:(t + 1) * 512]
        src = xsb[:].rearrange("p (kc t) -> p kc t", t=TOK)[:, :, t * 512:(t + 1) * 512]
        kb.inc(SY.dma_start(out=dst, in_=src), "st", 16)


def emit_B1(kb, c, x, Wq, bq, anw, ctab_d, stab_d, Psw_d, xno, Qo, Ko, Vo, X3i):
    kb.phase("b1")
    xsb = kb.sb("xsb", [128, ND * TOK], F32)
    Wsb = kb.sb("Wsb", [128, ND * 1536], BF16)
    bqsb = kb.sb("bqsb", [128, 12], F32)
    anwsb = kb.sb("anwsb", [128, 8], F32)
    ctab = kb.sb("ctab_s", [128, TOK], BF16)
    stab = kb.sb("stab_s", [128, TOK], BF16)
    Psw = kb.sb("Psw_s", [128, 128], BF16)
    xnf = kb.sb("xnf", [128, ND * 512], F32)
    hb = kb.sb("hb", [128, ND * 512], BF16)
    qtmp = kb.sb("qtmp", [128, 512], BF16)
    psb = kb.sb("psb", [128, 512], BF16)
    t2b = kb.sb("t2b", [128, 512], BF16)
    stg = [kb.sb(f"stg{i}", [128, 512], BF16) for i in range(2)]
    vt = [kb.sb(f"vt{i}", [128, 4 * 256], BF16) for i in range(2)]
    hal = kb.sb("hal", [128, 1024], BF16)
    w = LNW(kb, "ln")

    P_s = kb.ps("P_s", [1, 512])
    P_q = kb.ps("P_q", [1, 512])
    P_b1 = kb.ps("P_b1", [128, 512])
    P_b2 = kb.ps("P_b2", [128, 512])
    P_m = [kb.ps(f"P_m{i}", [128, 512]) for i in range(2)]
    P_r = kb.ps("P_r", [128, 512])
    P_t = kb.ps("P_t", [128, 64], BF16)

    SY, V, S, Tn, G = kb.SY, kb.V, kb.S, kb.T, kb.G
    load_x_fm(kb, xsb, x)
    for kc in range(ND):
        kb.inc(G.dma_start(out=Wsb[:, kc * 1536:(kc + 1) * 1536],
                           in_=Wq[kc * 128:(kc + 1) * 128, :]), "gd", 16)
    kb.inc(SY.dma_start(out=bqsb[:], in_=bq), "ld", 16)
    kb.inc(SY.dma_start(out=anwsb[:], in_=anw), "ld", 16)
    kb.inc(SY.dma_start(out=ctab[:], in_=ctab_d), "ld", 16)
    kb.inc(SY.dma_start(out=stab[:], in_=stab_d), "ld", 16)
    kb.inc(SY.dma_start(out=Psw[:], in_=Psw_d), "ld", 16)
    kb.wait(V, "ld"); kb.wait(S, "ld")

    st_marks = {}
    for t in range(NT):
        xt = lambda kc: xsb[:, kc * TOK + t * 512: kc * TOK + (t + 1) * 512]
        emit_stats(kb, w, xt, c, P_s, P_q)
        emit_bcast(kb, c, w.nmb, P_b1)
        emit_bcast(kb, c, w.rsb, P_b2)
        for kc in range(ND):
            kb.wait(V, "t")
            kb.inc(V.tensor_add(w.t1[:], xt(kc), P_b1[:]), "v")
            kb.inc(V.tensor_mul(xnf[:, kc * 512:(kc + 1) * 512], w.t1[:], P_b2[:]), "v")
            kb.inc(V.tensor_copy(hb[:, kc * 512:(kc + 1) * 512], xnf[:, kc * 512:(kc + 1) * 512]), "v")
        kb.wait(SY, "v")
        dst = xno.rearrange("(kc p) t -> p kc t", p=128)[:, :, t * 512:(t + 1) * 512]
        kb.inc(SY.dma_start(out=dst, in_=hb[:].rearrange("p (kc t) -> p kc t", t=512)), "st", 16)
        xn_st_mark = kb.n("st")
        xnt = lambda kc: xnf[:, kc * 512:(kc + 1) * 512]
        emit_stats(kb, w, xnt, c, P_s, P_q, rms=True)
        emit_bcast(kb, c, w.rsb, P_b2)
        kb.wait(V, "st", xn_st_mark)
        for kc in range(ND):
            kb.wait(V, "t")
            kb.inc(V.tensor_mul(w.t1[:], xnt(kc), P_b2[:]), "v")
            kb.inc(V.tensor_scalar_mul(hb[:, kc * 512:(kc + 1) * 512], w.t1[:], anwsb[:, kc:kc + 1]), "v")
        kb.wait(Tn, "v")
        for oc in range(12):
            pm = P_m[oc % 2]
            kb.wait(Tn, "s")
            for kc in range(ND):
                kb.inc(Tn.matmul(pm[:], Wsb[:, kc * 1536 + oc * 128: kc * 1536 + (oc + 1) * 128],
                                 hb[:, kc * 512:(kc + 1) * 512],
                                 start=(kc == 0), stop=(kc == ND - 1)), "t")
            kb.wait(S, "t")
            kb.inc(S.activation(qtmp[:], pm[:], AF.Identity, bias=bqsb[:, oc:oc + 1]), "s")
            if oc < 10:
                kb.wait(Tn, "s")
                kb.wait(Tn, "v")
                kb.inc(Tn.matmul(P_r[:], Psw[:], qtmp[:], start=True, stop=True), "t")
                sg = stg[oc % 2]
                if id(sg) in st_marks:
                    kb.wait(V, "st", st_marks[id(sg)])
                kb.wait(V, "t")
                kb.inc(V.tensor_copy(psb[:], P_r[:]), "v")
                kb.wait(V, "s")
                kb.inc(V.tensor_mul(t2b[:], qtmp[:], ctab[:, t * 512:(t + 1) * 512]), "v")
                kb.inc(V.tensor_mul(psb[:], psb[:], stab[:, t * 512:(t + 1) * 512]), "v")
                kb.inc(V.tensor_add(sg[:], t2b[:], psb[:]), "v")
                kb.wait(SY, "v")
                if oc < 8:
                    kb.inc(SY.dma_start(out=Qo[oc * 128:(oc + 1) * 128, t * 512:(t + 1) * 512],
                                        in_=sg[:]), "st", 16)
                else:
                    kb.inc(SY.dma_start(out=Ko[(oc - 8) * 128:(oc - 7) * 128, t * 512:(t + 1) * 512],
                                        in_=sg[:]), "st", 16)
                st_marks[id(sg)] = kb.n("st")
            else:
                vc = oc - 10
                vbuf = vt[t % 2]
                if (vc == 0) and (id(vbuf) in st_marks):
                    kb.wait(S, "st", st_marks[id(vbuf)])
                for hh in range(2):
                    for p4 in range(4):
                        kb.wait(Tn, "s")
                        kb.wait(Tn, "v")
                        kb.inc(Tn.transpose(P_t[:], qtmp[64 * hh:64 * hh + 64, 128 * p4:128 * (p4 + 1)],
                                            c["idn2"][64 * hh:64 * hh + 64, :]), "t")
                        kb.wait(V, "t")
                        col = 256 * p4 + 64 * (2 * vc + hh)
                        kb.inc(V.tensor_copy(vbuf[:, col:col + 64], P_t[:]), "v")
                if vc == 1:
                    kb.wait(SY, "v")
                    vdst = Vo[t * 512:(t + 1) * 512, :].rearrange("(p4 p) ch -> p p4 ch", p=128)
                    kb.inc(SY.dma_start(out=vdst,
                                        in_=vbuf[:].rearrange("p (p4 ch) -> p p4 ch", ch=256)), "st", 16)
                    st_marks[id(vbuf)] = kb.n("st")
    kb.wait(SY, "st")
    for ch in range(2):
        koff = ch * 512
        kb.inc(SY.dma_start(out=hal[:, koff:koff + 256].rearrange("p (h tt) -> p h tt", h=2),
                            in_=Ko[:, 1024 * ch + 896:1024 * ch + 1024]
                            .rearrange("(h p) tt -> p h tt", p=128)), "ld", 16)
        kb.inc(SY.dma_start(out=hal[:, koff + 256:koff + 512],
                            in_=Vo[1024 * ch + 896:1024 * ch + 1024, :]), "ld", 16)
    kb.wait(SY, "ld")
    for ch in range(2):
        koff = ch * 512
        kb.inc(SY.dma_start(out=X3i[2 * ch:2 * ch + 1, :].rearrange("a (h p tt) -> p (a h) tt", p=128, tt=128),
                            in_=hal[:, koff:koff + 256].rearrange("p (h tt) -> p h tt", h=2)), "st", 16)
        kb.inc(SY.dma_start(out=X3i[2 * ch + 1:2 * ch + 2, :].rearrange("a (p f) -> p (a f)", p=128),
                            in_=hal[:, koff + 256:koff + 512]), "st", 16)


def emit_B2(kb, c, Qd, Kod, Vod, X3o, msel_d, bmd, snd, xnd, Wpd, bpd, fnwd, gwd, gbd,
            xmo, h2o, ceo):
    kb.phase("b2")
    Qsb = kb.sb("Qsb", [128, ND * TOK], BF16)
    Ksb = kb.sb("Ksb2", [128, 8 * 1152], BF16)
    Vsl = kb.sb("Vsl2", [128, 8 * 9 * 65], BF16)
    bm = kb.sb("bm_s", [128, 7 * 512], BF16)
    snsb = kb.sb("snsb", [1, 16], F32)
    snexp = kb.sb("snexp", [1, 16], F32)
    xnsb = kb.sb("xnsb", [128, ND * 512], BF16)
    Wsb = kb.sb("Wsb2", [128, ND * D], BF16)
    bpsb = kb.sb("bpsb2", [128, 8], F32)
    fnwsb = kb.sb("fnwsb", [128, 8], F32)
    gwsb = kb.sb("gwsb", [128, ND * 32], BF16)
    gbsb = kb.sb("gbsb", [32, 1], F32)
    Ysb = kb.sb("Ysb2", [128, ND * TOK], BF16)
    esT = [kb.sb(f"esT{i}", [128, 3 * 512], BF16) for i in range(2)]
    dr = kb.sb("dr", [1, 512], F32)
    drb = kb.sb("drb", [1, 512], BF16)
    bcs = kb.sb("bcs", [64, 512], F32)
    zb = kb.sb("zb", [128, ND * 512], BF16)
    xm_f = kb.sb("xm_f", [128, ND * 512], F32)
    h2b = kb.sb("h2b", [128, ND * 512], BF16)
    glf = kb.sb("glf", [32, 512], F32)
    I32f = kb.sb("I32f", [32, 32], F32)
    I128f = kb.sb("I128f", [128, 128], F32)
    gtm = kb.sb("gtm", [128, 32], F32)
    cur = kb.sb("cur", [128, 32], F32)
    eqb = kb.sb("eqb", [128, 32], F32)
    mrow = kb.sb("mrow", [128, 5], F32)
    nm0 = kb.sb("nm0", [128, 1], F32)
    rd_ = kb.sb("rd_", [128, 1], F32)
    et = kb.sb("et", [128, 32], F32)
    sel = kb.sb("sel", [128, 32], F32)
    wgt = kb.sb("wgt", [128, 32], F32)
    cetm = kb.sb("cetm", [128, 32], F32)
    cef = kb.sb("cef", [32, 512], F32)
    tsel = kb.sb("tsel", [128, 8 * 256], BF16)
    Ks = kb.sb("Ks", [128, 2 * 256], BF16)
    Vs = kb.sb("Vs", [128, 2 * 256], BF16)
    tmpsel = kb.sb("tmpsel", [128, 256], BF16)
    mselsb = kb.sb("mselsb", [128, 8], F32)
    w = LNW(kb, "ln")

    P_S = [kb.ps(f"P_Sw{i}", [128, 3 * 512]) for i in range(2)]
    P_y = kb.ps("P_y", [128, 512])
    P_x = kb.ps("P_x", [128, 512])
    P_m = [P_S[0][:, 0:512], P_S[1][:, 0:512]]
    P_a = [P_S[0][0:1, 1024:1536], P_S[1][0:1, 1024:1536]]
    P_t32 = P_S[0][:, 512:1024]

    SY, V, S, Tn, G = kb.SY, kb.V, kb.S, kb.T, kb.G
    kb.inc(G.memset(I32f[:], 1.0), "g")
    kb.inc(G.affine_select(out=I32f[:], in_=I32f[:], compare_op=ALU.is_equal,
                           fill=0.0, base=0, pattern=[[-1, 32]], channel_multiplier=1), "g")
    kb.inc(G.memset(I128f[:], 1.0), "g")
    kb.inc(G.affine_select(out=I128f[:], in_=I128f[:], compare_op=ALU.is_equal,
                           fill=0.0, base=0, pattern=[[-1, 128]], channel_multiplier=1), "g")
    load_x_fm(kb, Qsb, Qd)
    kb.inc(SY.dma_start(out=bm[:], in_=bmd), "ld", 16)
    kb.inc(SY.dma_start(out=snsb[:], in_=snd), "ld", 16)
    kb.inc(SY.dma_start(out=bpsb[:], in_=bpd), "ld", 16)
    kb.inc(SY.dma_start(out=fnwsb[:], in_=fnwd), "ld", 16)
    kb.inc(SY.dma_start(out=gbsb[:], in_=gbd), "ld", 16)
    kb.inc(SY.dma_start(out=mselsb[:], in_=msel_d), "ld", 16)
    for kc in range(ND):
        kb.inc(G.dma_start(out=Wsb[:, kc * D:(kc + 1) * D], in_=Wpd[kc * 128:(kc + 1) * 128, :]), "gd", 16)
        kb.inc(G.dma_start(out=gwsb[:, kc * 32:(kc + 1) * 32], in_=gwd[kc * 128:(kc + 1) * 128, :]), "gd", 16)
    for ch in range(2):
        for kh in range(4):
            bi = ch * 4 + kh
            src = Kod[64 * kh:64 * kh + 64, 1024 * ch:1024 * ch + 1024]
            for rh in range(2):
                kb.inc(SY.dma_start(out=Ksb[64 * rh:64 * rh + 64, bi * 1152 + 128: bi * 1152 + 1152],
                                    in_=src), "ld", 16)
            kb.inc(SY.dma_start(
                out=Vsl[:, bi * 585:(bi + 1) * 585].rearrange("p (j cc) -> p j cc", cc=65)[:, 1:9, 0:64],
                in_=Vod[1024 * ch:1024 * ch + 1024, 64 * kh:64 * kh + 64].rearrange("(j p) cc -> p j cc", p=128)),
                "ld", 16)
    kb.wait(SY, "cc")
    for ch in range(2):
        for kind in range(2):
            kb.wait(SY, "v")
            for s in range(8):
                if kind == 0:
                    kb.inc(SY.dma_start(
                        out=tsel[:, 256 * s:256 * (s + 1)].rearrange("p (h tt) -> p h tt", h=2),
                        in_=X3o[s:s + 1, 2 * ch:2 * ch + 1, :].rearrange("a b (h p tt) -> p (a h) tt",
                                                                        p=128, tt=128)), "ld", 16)
                else:
                    kb.inc(SY.dma_start(
                        out=tsel[:, 256 * s:256 * (s + 1)],
                        in_=X3o[s:s + 1, 2 * ch + 1:2 * ch + 2, :].rearrange("a b (p f) -> p (a b f)",
                                                                            p=128)), "ld", 16)
            kb.wait(V, "ld")
            dst = (Ks if kind == 0 else Vs)
            for s in range(8):
                blk = tsel[:, 256 * s:256 * (s + 1)]
                if s == 0:
                    kb.inc(V.tensor_scalar(dst[:, 256 * ch:256 * (ch + 1)], blk, mselsb[:, 0:1], None, ALU.mult), "v")
                else:
                    kb.inc(V.tensor_scalar(tmpsel[:], blk, mselsb[:, s:s + 1], None, ALU.mult), "v")
                    kb.inc(V.tensor_add(dst[:, 256 * ch:256 * (ch + 1)], dst[:, 256 * ch:256 * (ch + 1)],
                                        tmpsel[:]), "v")
    for ch in range(2):
        for kh in range(4):
            bi = ch * 4 + kh
            ksrc = Ks[64 * (kh % 2):64 * (kh % 2) + 64, 256 * ch + 128 * (kh // 2): 256 * ch + 128 * (kh // 2) + 128]
            for rh in range(2):
                kb.inc(V.tensor_copy(Ksb[64 * rh:64 * rh + 64, bi * 1152: bi * 1152 + 128], ksrc), "v")
            kb.inc(V.tensor_copy(Vsl[:, bi * 585: bi * 585 + 64],
                                 Vs[:, 256 * ch + 64 * kh: 256 * ch + 64 * kh + 64]), "v")
    onescol = Vsl[:].rearrange("p (j cc) -> p j cc", cc=65)[:, :, 64:65]
    kb.inc(V.memset(onescol, 1.0), "v")
    kb.wait(S, "ld")
    kb.inc(S.activation(snexp[:], snsb[:], AF.Exp), "s")

    epi_mark = 0
    exp_last = [0, 0]
    kb.wait(Tn, "ld")
    kb.wait(Tn, "v")
    kb.wait(Tn, "g")
    for h in range(16):
        kh = h // 4
        qrow = 64 * (h % 2)
        qpr = h // 2
        for qt in range(NT):
            chx = qt // 2
            qoff = (qt % 2) * 512
            qap = Qsb[qrow:qrow + 64, qpr * TOK + qt * 512: qpr * TOK + (qt + 1) * 512]
            wgroups = [(0, 3), (3, 2)]
            def emit_qkw(gi):
                k0, gs = wgroups[gi]
                par = gi % 2
                ps = P_S[par]
                kb.wait(Tn, "s", exp_last[par])
                for i4 in range(gs):
                    kt = k0 + i4
                    kcol = (chx * 4 + kh) * 1152 + qoff + kt * 128
                    ins = Tn.matmul(ps[:, i4 * 512:(i4 + 1) * 512],
                                    Ksb[qrow:qrow + 64, kcol:kcol + 128],
                                    qap, start=True, stop=True)
                    if i4 == gs - 1:
                        kb.inc(ins, "t")
                qkw_marks[gi] = kb.n("t")
            qkw_marks = {}
            emit_qkw(0)
            for gi, (k0, gs) in enumerate(wgroups):
                par = gi % 2
                ps = P_S[par]
                eb = esT[par]
                kb.wait(S, "t", qkw_marks[gi])
                kb.inc(S.activation(eb[:, 0:gs * 512], ps[:, 0:gs * 512], AF.Exp, scale=0.125), "s")
                exp_last[par] = kb.n("s")
                kb.wait(V, "s", exp_last[par])
                kb.wait(V, "g")
                for i4 in range(gs):
                    kt = k0 + i4
                    if qoff == 0 and kt == 0:
                        mi = 5 if chx == 0 else 6
                    else:
                        mi = kt
                    kb.inc(V.tensor_mul(eb[:, i4 * 512:(i4 + 1) * 512],
                                        eb[:, i4 * 512:(i4 + 1) * 512],
                                        bm[:, mi * 512:(mi + 1) * 512]), "v")
                if gi + 1 < len(wgroups):
                    emit_qkw(gi + 1)
                kb.wait(Tn, "v", kb.n("v"))
                if gi == 0:
                    kb.wait(Tn, "v", epi_mark)
                for i4 in range(gs):
                    kt = k0 + i4
                    vcol = (chx * 4 + kh) * 585 + (qoff // 128 + kt) * 65
                    ins = Tn.matmul(P_y[0:65, :], Vsl[:, vcol:vcol + 65],
                                    eb[:, i4 * 512:(i4 + 1) * 512],
                                    start=(kt == 0), stop=(kt == 4))
                    if i4 == gs - 1:
                        kb.inc(ins, "t")
            kb.wait(V, "t")
            kb.inc(V.tensor_scalar(dr[:], P_y[64:65, :], snexp[0:1, h:h + 1], None, ALU.add), "v")
            kb.inc(V.reciprocal(dr[:], dr[:]), "v")
            kb.inc(V.tensor_copy(drb[:], dr[:]), "v")
            kb.wait(Tn, "v")
            kb.inc(Tn.matmul(P_x[0:64, :], c["ones_row"][0:1, 0:64], drb[:], start=True, stop=True), "t")
            kb.wait(V, "t")
            kb.inc(V.tensor_copy(bcs[:], P_x[0:64, :]), "v")
            kb.inc(V.tensor_mul(Ysb[qrow:qrow + 64, qpr * TOK + qt * 512: qpr * TOK + (qt + 1) * 512],
                                P_y[0:64, :], bcs[:]), "v")
            epi_mark = kb.n("v")

    kb.wait(Tn, "gd")
    for t in range(NT):
        for oc in range(ND):
            pm = P_m[oc % 2]
            kb.wait(Tn, "s")
            kb.wait(Tn, "v")
            for kc in range(ND):
                kb.inc(Tn.matmul(pm[:], Wsb[:, kc * D + oc * 128: kc * D + (oc + 1) * 128],
                                 Ysb[:, kc * TOK + t * 512: kc * TOK + (t + 1) * 512],
                                 start=(kc == 0), stop=(kc == ND - 1)), "t")
            kb.wait(S, "t")
            kb.wait(S, "v")
            kb.inc(S.activation(zb[:, oc * 512:(oc + 1) * 512], pm[:], AF.Identity,
                                bias=bpsb[:, oc:oc + 1]), "s")
        kb.wait(SY, "v")
        xnsrc = xnd.rearrange("(kc p) tt -> p kc tt", p=128)[:, :, t * 512:(t + 1) * 512]
        kb.inc(SY.dma_start(out=xnsb[:].rearrange("p (kc tt) -> p kc tt", tt=512), in_=xnsrc), "ld", 16)
        xn_mark = kb.n("ld")
        kb.wait(V, "ld", xn_mark)
        for kc in range(ND):
            kb.wait(V, "s")
            kb.wait(V, "st")
            xnap = xnsb[:, kc * 512:(kc + 1) * 512]
            kb.inc(V.tensor_add(xm_f[:, kc * 512:(kc + 1) * 512], xnap, zb[:, kc * 512:(kc + 1) * 512]), "v")
        kb.wait(G, "v")
        dstx = xmo.rearrange("(kc p) tt -> p kc tt", p=128)[:, :, t * 512:(t + 1) * 512]
        kb.inc(G.dma_start(out=dstx, in_=xm_f[:].rearrange("p (kc tt) -> p kc tt", tt=512)), "st", 16)
        xmt = lambda kc: xm_f[:, kc * 512:(kc + 1) * 512]
        emit_stats(kb, w, xmt, c, P_a[0], P_a[1], rms=True)
        emit_bcast(kb, c, w.rsb, P_y)
        for kc in range(ND):
            kb.wait(V, "t")
            kb.wait(V, "st")
            kb.inc(V.tensor_mul(w.t1[:], xmt(kc), P_y[:]), "v")
            kb.inc(V.tensor_scalar_mul(h2b[:, kc * 512:(kc + 1) * 512],
                                       w.t1[:], fnwsb[:, kc:kc + 1]), "v")
        kb.wait(SY, "v")
        dsth = h2o.rearrange("(kc p) tt -> p kc tt", p=128)[:, :, t * 512:(t + 1) * 512]
        kb.inc(SY.dma_start(out=dsth, in_=h2b[:].rearrange("p (kc tt) -> p kc tt", tt=512)), "st", 16)
        kb.wait(Tn, "v")
        for kc in range(ND):
            kb.inc(Tn.matmul(P_x[0:32, :], gwsb[:, kc * 32:(kc + 1) * 32],
                             h2b[:, kc * 512:(kc + 1) * 512],
                             start=(kc == 0), stop=(kc == ND - 1)), "t")
        kb.wait(S, "t")
        kb.inc(S.activation(glf[:], P_x[0:32, :], AF.Identity, bias=gbsb[:, 0:1]), "s")
        for p4 in range(4):
            kb.wait(Tn, "s")
            kb.wait(Tn, "v")
            kb.inc(Tn.transpose(P_m[0][:, 0:32], glf[0:32, 128 * p4:128 * (p4 + 1)], I32f[:]), "t")
            kb.wait(V, "t")
            kb.incw(V, V.tensor_copy(gtm[:], P_m[0][:, 0:32]), "v")
            kb.incw(V, V.tensor_copy(cur[:], gtm[:]), "v")
            for r in range(4):
                kb.incw(V, V.reduce_max(mrow[:, r:r + 1], cur[:], axis=AX.X), "v")
                if r < 3:
                    kb.incw(V, V.tensor_scalar(eqb[:], cur[:], mrow[:, r:r + 1], None, ALU.is_ge), "v")
                    kb.incw(V, V.scalar_tensor_tensor(cur[:], in0=eqb[:], scalar=-1e30, in1=cur[:],
                                                      op0=ALU.mult, op1=ALU.add), "v")
            kb.incw(V, V.tensor_scalar_mul(nm0[:], mrow[:, 0:1], -1.0), "v")
            kb.wait(S, "v")
            kb.inc(S.activation(et[:], gtm[:], AF.Exp, bias=nm0[:]), "s")
            kb.incw(V, V.tensor_scalar(sel[:], gtm[:], mrow[:, 3:4], None, ALU.is_ge), "v")
            kb.wait(V, "s")
            kb.incw(V, V.tensor_mul(wgt[:], et[:], sel[:]), "v")
            kb.incw(V, V.reduce_sum(mrow[:, 4:5], wgt[:], axis=AX.X), "v")
            kb.incw(V, V.reciprocal(rd_[:], mrow[:, 4:5]), "v")
            kb.incw(V, V.tensor_scalar_mul(cetm[:], wgt[:], rd_[:]), "v")
            kb.wait(Tn, "v")
            kb.inc(Tn.transpose(P_t32[0:32, 0:128], cetm[:], I128f[:]), "t")
            kb.wait(V, "t")
            kb.inc(V.tensor_copy(cef[0:32, 128 * p4: 128 * (p4 + 1)],
                                 P_t32[0:32, 0:128]), "v")
        kb.wait(SY, "v")
        kb.inc(SY.dma_start(out=ceo[:, t * 512:(t + 1) * 512], in_=cef[0:32, :]), "st", 16)


def emit_B3(kb, c, h2d, ced, xmd, xd, Wud, Wdd, xo):
    kb.phase("b3")
    h2sb = kb.sb("h2sb", [128, ND * TOK], BF16)
    cefb = kb.sb("cefb", [32, TOK], BF16)
    Ebig = kb.sb("Ebig", [32, 32 * 128], BF16)
    acc = kb.sb("acc", [128, ND * TOK], F32)
    wes = ExitStack()
    _wsb = lambda n, sh, dt: wes.enter_context(kb.nc.sbuf_tensor("b3_" + n, sh, dt))
    cb_all = _wsb("cb_all", [128, 4 * 512], F32)
    sg_sb = [_wsb(f"sg_sb{i}", [128, 512], F32) for i in range(2)]
    mdf = [_wsb(f"mdf{i}", [128, 512], F32) for i in range(2)]
    mdb = _wsb("mdb", [128, 4 * ND * 512], BF16)
    Wusb = [_wsb(f"Wusb{i}", [128, ND * 1024], BF16) for i in range(2)]
    Wdsb = _wsb("Wdsb", [128, ND * D], BF16)

    P_u = [kb.ps(f"P_u{i}", [128, 512]) for i in range(2)]
    P_g = [kb.ps(f"P_g{i}", [128, 512]) for i in range(2)]
    P_d = [kb.ps(f"P_d{i}", [128, 512]) for i in range(2)]
    P_c = kb.ps("P_c", [128, 512])

    SY, V, S, Tn, G = kb.SY, kb.V, kb.S, kb.T, kb.G
    gd0 = kb.n("gd")
    kb.inc(G.memset(Ebig[:], 1.0), "g")
    kb.inc(G.affine_select(out=Ebig[:].rearrange("p (e cc) -> p e cc", cc=128),
                           in_=Ebig[:].rearrange("p (e cc) -> p e cc", cc=128),
                           compare_op=ALU.is_equal, fill=0.0, base=0,
                           pattern=[[-1, 32], [0, 128]], channel_multiplier=1), "g")
    load_x_fm(kb, h2sb, h2d)
    load_x_fm(kb, acc, xmd, eng=G, sem="gd")
    kb.inc(G.dma_start(out=cefb[:], in_=ced), "gd", 16)

    wu_ready = {}
    wu_free = {}
    wd_ready = {}
    wd_free = 0

    def prefetch_wu(unit):
        e, ha = divmod(unit, 2)
        if e >= 32:
            return
        par = unit % 2
        kb.wait(G, "t", wu_free.get(par, 0))
        wsl = Wud[e * D:(e + 1) * D, :].rearrange("(kc p) cc -> p kc cc", p=128)
        dstu = Wusb[par][:].rearrange("p (kc two cc) -> p kc two cc", two=2, cc=512)
        kb.inc(G.dma_start(out=dstu[:, :, 0, :], in_=wsl[:, :, 512 * ha:512 * (ha + 1)]), "gd", 16)
        kb.inc(G.dma_start(out=dstu[:, :, 1, :], in_=wsl[:, :, 1024 + 512 * ha:1024 + 512 * (ha + 1)]), "gd", 16)
        wu_ready[unit] = kb.n("gd")

    def prefetch_wd(e):
        if e >= 32:
            return
        kb.wait(G, "t", wd_free)
        wsl = Wdd[e * D:(e + 1) * D, :].rearrange("(kc p) cc -> p kc cc", p=128)
        kb.inc(G.dma_start(out=Wdsb[:].rearrange("p (kc cc) -> p kc cc", cc=D), in_=wsl), "gd", 16)
        wd_ready[e] = kb.n("gd")

    prefetch_wu(0)
    prefetch_wu(1)
    prefetch_wd(0)
    pu_last = [0, 0]
    pg_last = [0, 0]
    pd_last = [0, 0]
    for e in range(32):
        kb.wait(Tn, "g")
        kb.wait(Tn, "gd", gd0 + 32)
        kb.wait(S, "v")
        for t in range(NT):
            kb.wait(Tn, "s")
            kb.inc(Tn.matmul(P_c[:], Ebig[:, e * 128:(e + 1) * 128],
                             cefb[0:32, t * 512:(t + 1) * 512], start=True, stop=True), "t")
            kb.wait(S, "t")
            kb.inc(S.activation(cb_all[:, t * 512:(t + 1) * 512], P_c[:], AF.Identity), "s")
        for ha in range(2):
            unit = 2 * e + ha
            par = unit % 2
            kb.wait(Tn, "gd", wu_ready[unit])
            for t in range(NT):
                h2t = lambda kc: h2sb[:, kc * TOK + t * 512: kc * TOK + (t + 1) * 512]
                for jj in range(4):
                    j = 4 * ha + jj
                    jpar = jj % 2
                    kb.wait(Tn, "v", pu_last[jpar])
                    for kc in range(ND):
                        kb.inc(Tn.matmul(P_u[jpar][:], Wusb[par][:, kc * 1024 + jj * 128: kc * 1024 + (jj + 1) * 128],
                                         h2t(kc), start=(kc == 0), stop=(kc == ND - 1)), "t")
                    kb.wait(Tn, "s", pg_last[jpar])
                    for kc in range(ND):
                        kb.inc(Tn.matmul(P_g[jpar][:], Wusb[par][:, kc * 1024 + 512 + jj * 128: kc * 1024 + 512 + (jj + 1) * 128],
                                         h2t(kc), start=(kc == 0), stop=(kc == ND - 1)), "t")
                    g_mark = kb.n("t")
                    kb.wait(S, "t", g_mark)
                    kb.wait(S, "v", pu_last[jpar])
                    kb.inc(S.activation(sg_sb[jpar][:], P_g[jpar][:], AF.Silu), "s")
                    pg_last[jpar] = kb.n("s")
                    kb.wait(V, "s", pg_last[jpar])
                    kb.inc(V.tensor_mul(mdf[jpar][:], sg_sb[jpar][:], P_u[jpar][:]), "v")
                    kb.inc(V.tensor_mul(mdb[:, (t * ND + j) * 512:(t * ND + j + 1) * 512],
                                        mdf[jpar][:], cb_all[:, t * 512:(t + 1) * 512]), "v")
                    pu_last[jpar] = kb.n("v")
            wu_free[par] = kb.n("t")
            if unit + 2 <= 63:
                prefetch_wu(unit + 2)
        kb.wait(Tn, "gd", wd_ready[e])
        kb.wait(Tn, "v")
        kb.wait(V, "gd", gd0 + 32)
        for t in range(NT):
            for oc in range(ND):
                pdp = oc % 2
                pd = P_d[pdp]
                kb.wait(Tn, "v", pd_last[pdp])
                for j in range(ND):
                    kb.inc(Tn.matmul(pd[:], Wdsb[:, j * D + oc * 128: j * D + (oc + 1) * 128],
                                     mdb[:, (t * ND + j) * 512:(t * ND + j + 1) * 512],
                                     start=(j == 0), stop=(j == ND - 1)), "t")
                kb.wait(V, "t")
                aap = acc[:, oc * TOK + t * 512: oc * TOK + (t + 1) * 512]
                kb.inc(V.tensor_add(aap, aap, pd[:]), "v")
                pd_last[pdp] = kb.n("v")
        wd_free = kb.n("t")
        prefetch_wd(e + 1)
    wes.close()
    kb.wait(SY, "t"); kb.wait(SY, "v"); kb.wait(SY, "s")
    xsb = kb.sb("xsb3", [128, ND * TOK], F32)
    w = LNW(kb, "ln3")
    load_x_fm(kb, xsb, xd)
    kb.wait(V, "ld")
    P_b1 = P_u[0]
    P_b2 = P_u[1]
    for t in range(NT):
        at = lambda kc: acc[:, kc * TOK + t * 512: kc * TOK + (t + 1) * 512]
        emit_stats(kb, w, at, c, P_g[0][0:1, :], P_g[1][0:1, :])
        emit_bcast(kb, c, w.nmb, P_b1)
        emit_bcast(kb, c, w.rsb, P_b2)
        for kc in range(ND):
            kb.wait(V, "t")
            xap = xsb[:, kc * TOK + t * 512: kc * TOK + (t + 1) * 512]
            kb.inc(V.tensor_add(w.t1[:], at(kc), P_b1[:]), "v")
            kb.inc(V.tensor_mul(w.t1[:], w.t1[:], P_b2[:]), "v")
            kb.inc(V.tensor_add(xap, xap, w.t1[:]), "v")
        kb.wait(SY, "v")
        dst = xo.rearrange("(kc p) tt -> p kc tt", p=128)[:, :, t * 512:(t + 1) * 512]
        src2 = xsb[:].rearrange("p (kc tt) -> p kc tt", tt=TOK)[:, :, t * 512:(t + 1) * 512]
        kb.inc(SY.dma_start(out=dst, in_=src2), "st", 16)


def emit_C1(kb, c, x, Wq, bq, w1, b1, xno, X4i):
    kb.phase("c1")
    xsb = kb.sb("xsb", [128, ND * TOK], F32)
    Wsb = kb.sb("Wsb", [128, ND * 3 * D], BF16)
    bqsb = kb.sb("bqsb", [128, 24], F32)
    w1sb = kb.sb("w1sb", [128, 8], F32)
    b1sb = kb.sb("b1sb", [128, 8], F32)
    xnf = kb.sb("xnf", [128, ND * 512], F32)
    xlb = kb.sb("xlb", [128, ND * 512], BF16)
    qtmp = kb.sb("qtmp", [128, 512], BF16)
    stg = [kb.sb(f"stg{i}", [128, 512], BF16) for i in range(2)]
    vt = [kb.sb(f"vt{i}", [128, 4 * D], BF16) for i in range(2)]
    w = LNW(kb, "ln")

    P_s = kb.ps("P_s", [1, 512])
    P_q = kb.ps("P_q", [1, 512])
    P_b1 = kb.ps("P_b1", [128, 512])
    P_b2 = kb.ps("P_b2", [128, 512])
    P_m = [kb.ps(f"P_m{i}", [128, 512]) for i in range(2)]
    P_t = kb.ps("P_t", [128, 64], BF16)

    SY, V, S, Tn, G = kb.SY, kb.V, kb.S, kb.T, kb.G
    load_x_fm(kb, xsb, x)
    for kc in range(ND):
        kb.inc(G.dma_start(out=Wsb[:, kc * 3 * D:(kc + 1) * 3 * D],
                           in_=Wq[kc * 128:(kc + 1) * 128, :]), "gd", 16)
    kb.inc(SY.dma_start(out=bqsb[:], in_=bq), "ld", 16)
    kb.inc(SY.dma_start(out=w1sb[:], in_=w1), "ld", 16)
    kb.inc(SY.dma_start(out=b1sb[:], in_=b1), "ld", 16)
    kb.wait(V, "ld"); kb.wait(S, "ld")

    st_marks = {}
    for t in range(NT):
        xt = lambda kc: xsb[:, kc * TOK + t * 512: kc * TOK + (t + 1) * 512]
        emit_stats(kb, w, xt, c, P_s, P_q)
        emit_bcast(kb, c, w.nmb, P_b1)
        emit_bcast(kb, c, w.rsb, P_b2)
        for kc in range(ND):
            kb.wait(V, "t")
            kb.inc(V.tensor_add(w.t1[:], xt(kc), P_b1[:]), "v")
            kb.inc(V.tensor_mul(xnf[:, kc * 512:(kc + 1) * 512], w.t1[:], P_b2[:]), "v")
            kb.inc(V.tensor_copy(xlb[:, kc * 512:(kc + 1) * 512], xnf[:, kc * 512:(kc + 1) * 512]), "v")
        kb.wait(SY, "v")
        dst = xno.rearrange("(kc p) tt -> p kc tt", p=128)[:, :, t * 512:(t + 1) * 512]
        kb.inc(SY.dma_start(out=dst, in_=xlb[:].rearrange("p (kc tt) -> p kc tt", tt=512)), "st", 16)
        xn_st = kb.n("st")
        xnt = lambda kc: xnf[:, kc * 512:(kc + 1) * 512]
        emit_stats(kb, w, xnt, c, P_s, P_q)
        emit_bcast(kb, c, w.nmb, P_b1)
        emit_bcast(kb, c, w.rsb, P_b2)
        kb.wait(V, "st", xn_st)
        for kc in range(ND):
            kb.wait(V, "t")
            kb.inc(V.tensor_add(w.t1[:], xnt(kc), P_b1[:]), "v")
            kb.inc(V.tensor_mul(w.t1[:], w.t1[:], P_b2[:]), "v")
            kb.inc(V.tensor_scalar(xlb[:, kc * 512:(kc + 1) * 512], w.t1[:],
                                   w1sb[:, kc:kc + 1], b1sb[:, kc:kc + 1], ALU.mult, ALU.add), "v")
        kb.wait(Tn, "v")
        kb.wait(Tn, "gd")
        for oc in range(24):
            pm = P_m[oc % 2]
            kb.wait(Tn, "s")
            for kc in range(ND):
                kb.inc(Tn.matmul(pm[:], Wsb[:, kc * 3 * D + oc * 128: kc * 3 * D + (oc + 1) * 128],
                                 xlb[:, kc * 512:(kc + 1) * 512],
                                 start=(kc == 0), stop=(kc == ND - 1)), "t")
            kb.wait(S, "t")
            kb.inc(S.activation(qtmp[:], pm[:], AF.Identity, bias=bqsb[:, oc:oc + 1]), "s")
            if oc < 16:
                sg = stg[oc % 2]
                if id(sg) in st_marks:
                    kb.wait(V, "st", st_marks[id(sg)])
                kb.wait(V, "s")
                kb.inc(V.tensor_copy(sg[:], qtmp[:]), "v")
                kb.wait(SY, "v")
                k = 0 if oc < 8 else 1
                j = oc % 8
                s = 4 * (t // 2) + j // 2
                slot = qk_slot(X4i, s, k)
                kb.inc(SY.dma_start(out=slot[128 * (j % 2):128 * (j % 2) + 128,
                                             512 * (t % 2):512 * (t % 2) + 512], in_=sg[:]), "st", 16)
                st_marks[id(sg)] = kb.n("st")
            else:
                vc = oc - 16
                vbuf = vt[t % 2]
                if (vc == 0) and (id(vbuf) in st_marks):
                    kb.wait(V, "st", st_marks[id(vbuf)])
                for hh in range(2):
                    for p4 in range(4):
                        kb.wait(Tn, "s")
                        kb.wait(Tn, "v")
                        kb.inc(Tn.transpose(P_t[:], qtmp[64 * hh:64 * hh + 64, 128 * p4:128 * (p4 + 1)],
                                            c["idn2"][64 * hh:64 * hh + 64, :]), "t")
                        kb.wait(V, "t")
                        col = 1024 * p4 + 64 * (2 * vc + hh)
                        kb.inc(V.tensor_copy(vbuf[:, col:col + 64], P_t[:]), "v")
                if vc == 7:
                    kb.wait(SY, "v")
                    for p4 in range(4):
                        for q in range(4):
                            s = 4 * (t // 2) + q
                            r0 = 512 * (t % 2) + 128 * p4
                            kb.inc(SY.dma_start(out=v_slot(X4i, s)[r0:r0 + 128, :],
                                                in_=vbuf[:, 1024 * p4 + 256 * q: 1024 * p4 + 256 * q + 256]),
                                   "st", 16)
                    st_marks[id(vbuf)] = kb.n("st")


def emit_C3(kb, c, x, X5o, xnd, Wo, bo, w2, b2, xo, xn3o):
    kb.phase("c3")
    xsb = kb.sb("xsb", [128, ND * TOK], F32)
    ysb = kb.sb("ysb", [128, ND * TOK], BF16)
    xnsb = kb.sb("xnsb", [128, ND * TOK], BF16)
    Wsb = kb.sb("Wsb", [128, ND * D], BF16)
    bosb = kb.sb("bosb", [128, 8], F32)
    w2sb = kb.sb("w2sb", [128, 8], F32)
    b2sb = kb.sb("b2sb", [128, 8], F32)
    zb = kb.sb("zb", [128, ND * 512], BF16)
    rm_f = kb.sb("rm_f", [128, ND * 512], F32)
    rp_f = kb.sb("rp_f", [128, ND * 512], F32)
    w = LNW(kb, "ln")

    P_s = kb.ps("P_s", [1, 512])
    P_q = kb.ps("P_q", [1, 512])
    P_b1 = kb.ps("P_b1", [128, 512])
    P_b2 = kb.ps("P_b2", [128, 512])
    P_m = [kb.ps(f"P_m{i}", [128, 512]) for i in range(2)]

    SY, V, S, Tn, G = kb.SY, kb.V, kb.S, kb.T, kb.G
    load_x_fm(kb, xsb, x)
    load_x_fm(kb, xnsb, xnd)
    kb.wait(SY, "cc")
    for r in range(8):
        ys = y_slot(X5o, r)
        for half in range(2):
            kc = 2 * (r % 4) + half
            kb.inc(SY.dma_start(out=ysb[:, kc * TOK + 1024 * (r // 4): kc * TOK + 1024 * (r // 4) + 1024],
                                in_=ys[128 * half:128 * half + 128, :]), "ld", 16)
    for kc in range(ND):
        kb.inc(G.dma_start(out=Wsb[:, kc * D:(kc + 1) * D], in_=Wo[kc * 128:(kc + 1) * 128, :]), "gd", 16)
    kb.inc(SY.dma_start(out=bosb[:], in_=bo), "ld", 16)
    kb.inc(SY.dma_start(out=w2sb[:], in_=w2), "ld", 16)
    kb.inc(SY.dma_start(out=b2sb[:], in_=b2), "ld", 16)
    kb.wait(Tn, "ld"); kb.wait(Tn, "gd"); kb.wait(S, "ld"); kb.wait(V, "ld")

    for t in range(NT):
        for oc in range(ND):
            pm = P_m[oc % 2]
            kb.wait(Tn, "s")
            for kc in range(ND):
                kb.inc(Tn.matmul(pm[:], Wsb[:, kc * D + oc * 128: kc * D + (oc + 1) * 128],
                                 ysb[:, kc * TOK + t * 512: kc * TOK + (t + 1) * 512],
                                 start=(kc == 0), stop=(kc == ND - 1)), "t")
            kb.wait(S, "t")
            kb.wait(S, "v")
            kb.inc(S.activation(zb[:, oc * 512:(oc + 1) * 512], pm[:], AF.Identity,
                                bias=bosb[:, oc:oc + 1]), "s")
        for kc in range(ND):
            kb.wait(V, "s")
            xnap = xnsb[:, kc * TOK + t * 512: kc * TOK + (t + 1) * 512]
            kb.inc(V.tensor_add(rm_f[:, kc * 512:(kc + 1) * 512], xnap, zb[:, kc * 512:(kc + 1) * 512]), "v")
        rmt = lambda kc: rm_f[:, kc * 512:(kc + 1) * 512]
        emit_stats(kb, w, rmt, c, P_s, P_q)
        emit_bcast(kb, c, w.nmb, P_b1)
        emit_bcast(kb, c, w.rsb, P_b2)
        for kc in range(ND):
            kb.wait(V, "t")
            kb.inc(V.tensor_add(w.t1[:], rmt(kc), P_b1[:]), "v")
            kb.inc(V.tensor_mul(w.t1[:], w.t1[:], P_b2[:]), "v")
            kb.inc(V.tensor_scalar(w.t1[:], w.t1[:], w2sb[:, kc:kc + 1], b2sb[:, kc:kc + 1],
                                   ALU.mult, ALU.add), "v")
            kb.inc(V.tensor_add(rp_f[:, kc * 512:(kc + 1) * 512], rmt(kc), w.t1[:]), "v")
        rpt = lambda kc: rp_f[:, kc * 512:(kc + 1) * 512]
        emit_stats(kb, w, rpt, c, P_s, P_q)
        emit_bcast(kb, c, w.nmb, P_b1)
        emit_bcast(kb, c, w.rsb, P_b2)
        for kc in range(ND):
            kb.wait(V, "t")
            xap = xsb[:, kc * TOK + t * 512: kc * TOK + (t + 1) * 512]
            kb.inc(V.tensor_add(w.t1[:], rpt(kc), P_b1[:]), "v")
            kb.inc(V.tensor_mul(w.t1[:], w.t1[:], P_b2[:]), "v")
            kb.inc(V.tensor_add(xap, xap, w.t1[:]), "v")
        kb.wait(SY, "v")
        dst = xo.rearrange("(kc p) tt -> p kc tt", p=128)[:, :, t * 512:(t + 1) * 512]
        src = xsb[:].rearrange("p (kc tt) -> p kc tt", tt=TOK)[:, :, t * 512:(t + 1) * 512]
        kb.inc(SY.dma_start(out=dst, in_=src), "st", 16)
        xt3 = lambda kc: xsb[:, kc * TOK + t * 512: kc * TOK + (t + 1) * 512]
        emit_stats(kb, w, xt3, c, P_s, P_q)
        emit_bcast(kb, c, w.nmb, P_b1)
        emit_bcast(kb, c, w.rsb, P_b2)
        for kc in range(ND):
            kb.wait(V, "t")
            kb.inc(V.tensor_add(w.t1[:], xt3(kc), P_b1[:]), "v")
            kb.inc(V.tensor_mul(rp_f[:, kc * 512:(kc + 1) * 512], w.t1[:], P_b2[:]), "v")
        kb.wait(SY, "v")
        dst2 = xn3o.rearrange("(kc p) tt -> p kc tt", p=128)[:, :, t * 512:(t + 1) * 512]
        kb.inc(SY.dma_start(out=dst2, in_=rp_f[:].rearrange("p (kc tt) -> p kc tt", tt=512)), "st", 16)


def emit_D(kb, c, knum, h_in, qA, qB, srA, srB, bA, bB, laA, laB, lbA, lbB, gam, bet,
           final, out_ap, xd=None):
    kb.phase(f"d{knum}")
    hsb = kb.sb("hsb", [128, ND * TOK], F32)
    hbb = kb.sb("hbb", [128, ND * TOK], BF16)
    yA = kb.sb("yA", [128, ND * TOK], BF16)
    lasb = kb.sb("lasb", [128, ND * 64], BF16)
    lbsb = kb.sb("lbsb", [32, 2 * D], BF16)
    bAsb = kb.sb("bAsb", [128, 8], F32)
    bBsb = kb.sb("bBsb", [128, 8], F32)
    gamsb = kb.sb("gamsb", [128, 8], F32)
    betsb = kb.sb("betsb", [128, 8], F32)
    lsf = kb.sb("lsf", [32, D], F32)
    qf = [kb.sb(f"qf{i}", [128, D], F32) for i in range(2)]
    srf = [kb.sb(f"srf{i}", [128, D], F32) for i in range(2)]
    qaf = kb.sb("qaf", [128, D], F32)
    wsb = kb.sb("wsb_d", [128, ND * D], BF16)
    xrb = kb.sb("xrb", [32, 512], BF16)
    w = LNW(kb, "ln")

    P_s = kb.ps("P_s", [1, 512])
    P_q = kb.ps("P_q", [1, 512])
    P_b1 = kb.ps("P_b1", [128, 512])
    P_b2 = kb.ps("P_b2", [128, 512])
    P_m = [kb.ps(f"P_m{i}", [128, 512]) for i in range(2)]
    P_l = kb.ps("P_l", [32, 512])

    SY, V, S, Tn, G = kb.SY, kb.V, kb.S, kb.T, kb.G
    load_x_fm(kb, hsb, h_in)
    kb.inc(SY.dma_start(out=bAsb[:], in_=bA), "ld", 16)
    kb.inc(SY.dma_start(out=bBsb[:], in_=bB), "ld", 16)
    kb.inc(SY.dma_start(out=gamsb[:], in_=gam), "ld", 16)
    kb.inc(SY.dma_start(out=betsb[:], in_=bet), "ld", 16)
    for kc in range(ND):
        kb.inc(G.dma_start(out=lasb[:, kc * 32:(kc + 1) * 32], in_=laA[kc * 128:(kc + 1) * 128, :]), "gd", 16)
        kb.inc(G.dma_start(out=lasb[:, 256 + kc * 32:256 + (kc + 1) * 32],
                           in_=laB[kc * 128:(kc + 1) * 128, :]), "gd", 16)
    kb.inc(SY.dma_start(out=lsf[:], in_=lbA), "ld", 16)
    kb.wait(V, "ld")
    kb.inc(V.tensor_copy(lbsb[:, 0:D], lsf[:]), "v")
    lb_mark = kb.n("v")
    kb.wait(SY, "v", lb_mark)
    kb.inc(SY.dma_start(out=lsf[:], in_=lbB), "ld", 16)
    kb.wait(V, "ld")
    kb.inc(V.tensor_copy(lbsb[:, D:2 * D], lsf[:]), "v")
    kb.wait(V, "ld")
    for kc in range(ND):
        kb.inc(V.tensor_copy(hbb[:, kc * TOK:(kc + 1) * TOK], hsb[:, kc * TOK:(kc + 1) * TOK]), "v")

    def dequant(qd, srd):
        for kc in range(ND):
            pqf = qf[kc % 2]
            psr = srf[kc % 2]
            kb.wait(G, "s")
            kb.wait(G, "v")
            kb.inc(G.dma_start(out=pqf[:], in_=qd[kc * 128:(kc + 1) * 128, :]), "gd", 16)
            kb.inc(G.dma_start(out=psr[:], in_=srd[kc * 128:(kc + 1) * 128, :]), "gd", 16)
            bias_c = kb.fconst(-1.0, 128)
            kb.wait(S, "gd")
            kb.wait(S, "v")
            kb.wait(S, "g")
            kb.inc(S.activation(qaf[:], pqf[:], AF.Identity, scale=1.0 / 7.5, bias=bias_c), "s")
            kb.wait(V, "s")
            kb.wait(V, "t")
            kb.inc(V.tensor_mul(wsb[:, kc * D:(kc + 1) * D], qaf[:], psr[:]), "v")

    def qlora(src_sb, bias_sb, la_off, lb_off, out_writer):
        for t in range(NT):
            kb.wait(Tn, "v")
            kb.wait(Tn, "gd")
            kb.wait(Tn, "s")
            for kc in range(ND):
                kb.inc(Tn.matmul(P_l[:], lasb[:, la_off + kc * 32: la_off + (kc + 1) * 32],
                                 src_sb[:, kc * TOK + t * 512: kc * TOK + (t + 1) * 512],
                                 start=(kc == 0), stop=(kc == ND - 1)), "t")
            kb.wait(S, "t")
            kb.inc(S.activation(xrb[:], P_l[0:32, :], AF.Identity), "s")
            kb.wait(Tn, "s")
            for oc in range(ND):
                pm = P_m[oc % 2]
                kb.wait(Tn, "s")
                kb.wait(Tn, "v")
                kb.inc(Tn.matmul(pm[:], lbsb[0:32, lb_off + oc * 128: lb_off + (oc + 1) * 128],
                                 xrb[:], start=True, stop=False), "t")
                for kc in range(ND):
                    kb.inc(Tn.matmul(pm[:], wsb[:, kc * D + oc * 128: kc * D + (oc + 1) * 128],
                                     src_sb[:, kc * TOK + t * 512: kc * TOK + (t + 1) * 512],
                                     start=False, stop=(kc == ND - 1)), "t")
                out_writer(oc, t, pm, bias_sb)

    dequant(qA, srA)
    def wA(oc, t, pm, bias_sb):
        kb.wait(S, "t")
        kb.inc(S.activation(yA[:, oc * TOK + t * 512: oc * TOK + (t + 1) * 512], pm[:],
                            AF.Identity, bias=bias_sb[:, oc:oc + 1]), "s")
    qlora(hbb, bAsb, 0, 0, wA)
    gb_sb = yA
    for t in range(NT):
        yt = lambda kc: yA[:, kc * TOK + t * 512: kc * TOK + (t + 1) * 512]
        emit_stats(kb, w, yt, c, P_s, P_q, src_bf16=True)
        emit_bcast(kb, c, w.nmb, P_b1)
        emit_bcast(kb, c, w.rsb, P_b2)
        for kc in range(ND):
            kb.wait(V, "t")
            kb.wait(V, "s")
            kb.inc(V.tensor_add(w.t1[:], yt(kc), P_b1[:]), "v")
            kb.inc(V.tensor_mul(w.t1[:], w.t1[:], P_b2[:]), "v")
            kb.inc(V.tensor_scalar(w.t1[:], w.t1[:], gamsb[:, kc:kc + 1], betsb[:, kc:kc + 1],
                                   ALU.mult, ALU.add), "v")
            kb.wait(S, "v")
            kb.inc(S.activation(yA[:, kc * TOK + t * 512: kc * TOK + (t + 1) * 512],
                                w.t1[:], AF.Gelu), "s")
    dequant(qB, srB)
    def wB(oc, t, pm, bias_sb):
        kb.wait(V, "t")
        hap = hsb[:, oc * TOK + t * 512: oc * TOK + (t + 1) * 512]
        kb.inc(V.tensor_scalar_add(w.t1[:], pm[:], bias_sb[:, oc:oc + 1]), "v")
        kb.inc(V.tensor_add(hap, hap, w.t1[:]), "v")
    qlora(gb_sb, bBsb, 256, D, wB)

    if not final:
        kb.wait(SY, "v")
        dst = out_ap.rearrange("(kc p) tt -> p kc tt", p=128)
        kb.inc(SY.dma_start(out=dst, in_=hsb[:].rearrange("p (kc tt) -> p kc tt", tt=TOK)), "st", 16)
    else:
        xtile = kb.sb("xtileD", [128, ND * 512], F32)
        for t in range(NT):
            kb.wait(SY, "v")
            xsrc = xd.rearrange("(kc p) tt -> p kc tt", p=128)[:, :, t * 512:(t + 1) * 512]
            kb.inc(SY.dma_start(out=xtile[:].rearrange("p (kc tt) -> p kc tt", tt=512), in_=xsrc), "ld", 16)
            ht = lambda kc: hsb[:, kc * TOK + t * 512: kc * TOK + (t + 1) * 512]
            emit_stats(kb, w, ht, c, P_s, P_q)
            emit_bcast(kb, c, w.nmb, P_b1)
            emit_bcast(kb, c, w.rsb, P_b2)
            kb.wait(V, "ld")
            for kc in range(ND):
                kb.wait(V, "t")
                kb.inc(V.tensor_add(w.t1[:], ht(kc), P_b1[:]), "v")
                kb.inc(V.tensor_mul(w.t1[:], w.t1[:], P_b2[:]), "v")
                kb.inc(V.tensor_add(ht(kc), xtile[:, kc * 512:(kc + 1) * 512], w.t1[:]), "v")
            kb.wait(SY, "v")
            dst = out_ap.rearrange("(kc p) tt -> p kc tt", p=128)[:, :, t * 512:(t + 1) * 512]
            hsrc = hsb[:].rearrange("p (kc tt) -> p kc tt", tt=TOK)[:, :, t * 512:(t + 1) * 512]
            kb.inc(SY.dma_start(out=dst, in_=hsrc), "st", 16)


DEBUG_TAPS = False


def build_mega():
    kb = KB()
    dtap = (lambda n, s, d: kb.dout(n, s, d)) if DEBUG_TAPS else (lambda n, s, d: kb.dint(n, s, d))
    x_in = kb.din("x", [D, TOK], F32)
    Wa = kb.din("Wa", [D, 3 * D], F32)
    ba = kb.din("ba", [128, 24], F32)
    Rq = kb.din("Rq", [128, D], BF16)
    Wp_a = kb.din("Wp_a", [D, D], F32)
    bp_a = kb.din("bp_a", [128, 8], F32)
    WqB = kb.din("WqB", [D, 1536], F32)
    bqB = kb.din("bqB", [128, 12], F32)
    anwB = kb.din("anwB", [128, 8], F32)
    ctabB = kb.din("ctabB", [128, TOK], BF16)
    stabB = kb.din("stabB", [128, TOK], BF16)
    PswB = kb.din("PswB", [128, 128], BF16)
    mselB = kb.din("mselB", [128, 8], F32)
    bmB = kb.din("bmB", [128, 7 * 512], BF16)
    snB = kb.din("snB", [1, 16], F32)
    WpB = kb.din("WpB", [D, D], F32)
    bpB = kb.din("bpB", [128, 8], F32)
    fnwB = kb.din("fnwB", [128, 8], F32)
    gwB = kb.din("gwB", [D, 32], F32)
    gbB = kb.din("gbB", [32, 1], F32)
    WuB = kb.din("WuB", [32 * D, 2048], F32)
    WdB = kb.din("WdB", [32 * D, D], F32)
    WqC = kb.din("WqC", [D, 3 * D], F32)
    bqC = kb.din("bqC", [128, 24], F32)
    w1C = kb.din("w1C", [128, 8], F32)
    b1C = kb.din("b1C", [128, 8], F32)
    WoC = kb.din("WoC", [D, D], F32)
    boC = kb.din("boC", [128, 8], F32)
    w2C = kb.din("w2C", [128, 8], F32)
    b2C = kb.din("b2C", [128, 8], F32)
    Dk = []
    for knum in range(3):
        Dk.append({nm: kb.din(f"{nm}{knum}",
                              [D, D] if nm in ("qA", "qB") else
                              ([D, D] if nm in ("srA", "srB") else
                               ([128, 8] if nm in ("bA", "bB", "gam", "bet") else
                                ([D, 32] if nm in ("laA", "laB") else [32, D]))),
                              I32 if nm in ("qA", "qB") else F32)
                   for nm in ("qA", "qB", "srA", "srB", "bA", "bB", "laA", "laB",
                              "lbA", "lbB", "gam", "bet")})
    xo_final = kb.dout("xo", [D, TOK], F32)

    X1i = kb.dint("X1i", [8, 3, SLOT], BF16)
    X1o = kb.dint("X1o", [8, 3, SLOT], BF16)
    X2i = kb.dint("X2i", [8, SLOT], BF16)
    X2o = kb.dint("X2o", [8, SLOT], BF16)
    X3i = kb.dint("X3i", [4, 32768], BF16)
    X3o = kb.dint("X3o", [8, 4, 32768], BF16)
    X4i = kb.dint("X4i", [8, 3, SLOT], BF16)
    X4o = kb.dint("X4o", [8, 3, SLOT], BF16)
    X5i = kb.dint("X5i", [8, SLOT], BF16)
    X5o = kb.dint("X5o", [8, SLOT], BF16)
    x1d = dtap("x1d", [D, TOK], F32)
    x2d = dtap("x2d", [D, TOK], F32)
    x3d = dtap("x3d", [D, TOK], F32)
    xn3d = dtap("xn3d", [D, TOK], F32)
    xnoB = dtap("xnoB", [D, TOK], BF16)
    QfB = dtap("QfB", [D, TOK], BF16)
    KoB = dtap("KoB", [256, TOK], BF16)
    VoB = dtap("VoB", [TOK, 256], BF16)
    xmB = dtap("xmB", [D, TOK], BF16)
    h2B = dtap("h2B", [D, TOK], BF16)
    ceB = dtap("ceB", [32, TOK], F32)
    xnoC = dtap("xnoC", [D, TOK], BF16)
    hD = [dtap(f"hD{i}", [D, TOK], F32) for i in range(2)]

    c = emit_consts(kb)
    kb.fconst(EPS)
    kb.fconst(-1.0, 128)

    emit_A1(kb, c, x_in, Wa, ba, Rq, X1i)
    kb.collective("AllToAll", X1i, X1o)
    emit_flash(kb, c, X1o, X2i, l1_mode=False, tag="f0")
    kb.collective("AllToAll", X2i, X2o)
    emit_PS(kb, c, x_in, X2o, Wp_a, bp_a, x1d)
    emit_B1(kb, c, x1d, WqB, bqB, anwB, ctabB, stabB, PswB, xnoB, QfB, KoB, VoB, X3i)
    kb.collective("AllGather", X3i, X3o)
    emit_B2(kb, c, QfB, KoB, VoB, X3o, mselB, bmB, snB, xnoB, WpB, bpB, fnwB, gwB, gbB,
            xmB, h2B, ceB)
    emit_B3(kb, c, h2B, ceB, xmB, x1d, WuB, WdB, x2d)
    emit_C1(kb, c, x2d, WqC, bqC, w1C, b1C, xnoC, X4i)
    kb.collective("AllToAll", X4i, X4o)
    emit_flash(kb, c, X4o, X5i, l1_mode=True, tag="f1")
    kb.collective("AllToAll", X5i, X5o)
    emit_C3(kb, c, x2d, X5o, xnoC, WoC, boC, w2C, b2C, x3d, xn3d)
    emit_D(kb, c, 0, xn3d, **{k: Dk[0][k] for k in Dk[0]}, final=False, out_ap=hD[0])
    emit_D(kb, c, 1, hD[0], **{k: Dk[1][k] for k in Dk[1]}, final=False, out_ap=hD[1])
    emit_D(kb, c, 2, hD[1], **{k: Dk[2][k] for k in Dk[2]}, final=True, out_ap=xo_final, xd=x3d)
    return kb.fin()


def rope_mats_l0():
    hd = 64
    theta = 1.0 / 10000.0 ** (np.arange(0, hd, 2, dtype=np.float64) / hd)
    R = np.zeros((128, D), np.float32)
    for ch in range(8):
        for hh in range(2):
            h = 2 * ch + hh
            base = 64 * hh
            for j in range(32):
                cth = np.cos(h * theta[j]); sth = np.sin(h * theta[j])
                R[base + 2 * j, ch * 128 + base + 2 * j] = cth
                R[base + 2 * j + 1, ch * 128 + base + 2 * j] = -sth
                R[base + 2 * j, ch * 128 + base + 2 * j + 1] = sth
                R[base + 2 * j + 1, ch * 128 + base + 2 * j + 1] = cth
    return R.astype(BF)


def l0f_rope_tables(positions):
    DH = 64
    pos = positions.astype(np.float64) / 32.0
    invf = 1.0 / (150000.0 ** (np.arange(0, DH, 2, dtype=np.float64) / DH))
    fr = pos[:, None] * invf[None, :]
    sn, cs = np.sin(fr), np.cos(fr)
    ct = np.zeros((64, len(positions)), np.float32)
    st = np.zeros((64, len(positions)), np.float32)
    for i in range(32):
        ct[2 * i] = cs[:, i]; ct[2 * i + 1] = cs[:, i]
        st[2 * i] = -sn[:, i]; st[2 * i + 1] = sn[:, i]
    return (np.ascontiguousarray(np.tile(ct, (2, 1))).astype(BF),
            np.ascontiguousarray(np.tile(st, (2, 1))).astype(BF))


def pair_swap_mat():
    P = np.zeros((128, 128), np.float32)
    for i in range(64):
        P[2 * i, 2 * i + 1] = 1.0
        P[2 * i + 1, 2 * i] = 1.0
    return P.astype(BF)


def bm_masks(core):
    bmm = np.zeros((128, 7 * 512), np.float32)
    ql = np.arange(512)[None, :]
    kl = np.arange(128)[:, None]
    for kt in range(5):
        dmm = ql - kl + 128 - 128 * kt
        bmm[:, kt * 512:(kt + 1) * 512] = ((dmm >= 0) & (dmm <= 128)).astype(np.float32)
    if core == 0:
        bmm[:, 5 * 512:] = 0.0
    else:
        bmm[:, 5 * 512:6 * 512] = bmm[:, 0:512]
        bmm[:, 6 * 512:7 * 512] = bmm[:, 0:512]
    return bmm.astype(BF)


_nc_cache = []


def kernel(**I):
    x = np.ascontiguousarray(I["L0_attn_x"].astype(np.float32).transpose(0, 2, 1))
    common = {}
    common["Wa"] = np.ascontiguousarray(I["L0_attn_W_attn"].astype(np.float32))
    common["ba"] = np.ascontiguousarray(I["L0_attn_b_attn"].astype(np.float32).reshape(24, 128).T)
    common["Rq"] = rope_mats_l0()
    common["Wp_a"] = np.ascontiguousarray(I["L0_attn_W_proj"].astype(np.float32))
    common["bp_a"] = np.ascontiguousarray(I["L0_attn_b_proj"].astype(np.float32).reshape(8, 128).T)
    common["WqB"] = np.ascontiguousarray(I["L0_ffn_qkv_w"].astype(np.float32))
    common["bqB"] = np.ascontiguousarray(I["L0_ffn_qkv_b"].astype(np.float32).reshape(12, 128).T)
    common["anwB"] = np.ascontiguousarray(I["L0_ffn_attn_norm_w"].astype(np.float32).reshape(8, 128).T)
    common["PswB"] = pair_swap_mat()
    common["snB"] = np.ascontiguousarray(I["L0_ffn_attn_sink"].astype(np.float32).reshape(1, 16))
    common["WpB"] = np.ascontiguousarray(I["L0_ffn_proj_w"].astype(np.float32))
    common["bpB"] = np.ascontiguousarray(I["L0_ffn_proj_b"].astype(np.float32).reshape(8, 128).T)
    common["fnwB"] = np.ascontiguousarray(I["L0_ffn_ffn_norm_w"].astype(np.float32).reshape(8, 128).T)
    common["gwB"] = np.ascontiguousarray(I["L0_ffn_gate_w"].astype(np.float32))
    common["gbB"] = np.ascontiguousarray(I["L0_ffn_gate_b"].astype(np.float32).reshape(32, 1))
    common["WuB"] = np.ascontiguousarray(I["L0_ffn_W_up"].astype(np.float32).reshape(32 * D, 2048))
    common["WdB"] = np.ascontiguousarray(I["L0_ffn_W_down"].astype(np.float32).reshape(32 * D, D))
    WQ = np.ascontiguousarray(I["L1_attn_W_Q"].astype(np.float32).transpose(1, 0, 2).reshape(D, D))
    WK = np.ascontiguousarray(I["L1_attn_W_K"].astype(np.float32).transpose(1, 0, 2).reshape(D, D))
    WV = np.ascontiguousarray(I["L1_attn_W_V"].astype(np.float32).transpose(1, 0, 2).reshape(D, D))
    common["WqC"] = np.ascontiguousarray(np.concatenate([WQ, WK, WV], axis=1))
    bqC = np.concatenate([I["L1_attn_b_Q"].reshape(-1), I["L1_attn_b_K"].reshape(-1),
                          I["L1_attn_b_V"].reshape(-1)]).astype(np.float32)
    common["bqC"] = np.ascontiguousarray(bqC.reshape(24, 128).T)
    common["w1C"] = np.ascontiguousarray(I["L1_attn_ln1_w"].astype(np.float32).reshape(8, 128).T)
    common["b1C"] = np.ascontiguousarray(I["L1_attn_ln1_b"].astype(np.float32).reshape(8, 128).T)
    common["WoC"] = np.ascontiguousarray(I["L1_attn_W_O"].astype(np.float32).reshape(D, D))
    common["boC"] = np.ascontiguousarray(I["L1_attn_b_O"].astype(np.float32).reshape(8, 128).T)
    common["w2C"] = np.ascontiguousarray(I["L1_attn_ln2_w"].astype(np.float32).reshape(8, 128).T)
    common["b2C"] = np.ascontiguousarray(I["L1_attn_ln2_b"].astype(np.float32).reshape(8, 128).T)
    for knum in range(3):
        i0, i1 = 2 * knum, 2 * knum + 1
        common[f"qA{knum}"] = np.ascontiguousarray(I["L1_ffn_qweights"][i0].astype(np.int32).T)
        common[f"qB{knum}"] = np.ascontiguousarray(I["L1_ffn_qweights"][i1].astype(np.int32).T)
        common[f"srA{knum}"] = np.ascontiguousarray(
            np.repeat(I["L1_ffn_scales"][i0].astype(np.float32), 16, axis=1).T)
        common[f"srB{knum}"] = np.ascontiguousarray(
            np.repeat(I["L1_ffn_scales"][i1].astype(np.float32), 16, axis=1).T)
        common[f"bA{knum}"] = np.ascontiguousarray(I["L1_ffn_biases"][i0].astype(np.float32).reshape(8, 128).T)
        common[f"bB{knum}"] = np.ascontiguousarray(I["L1_ffn_biases"][i1].astype(np.float32).reshape(8, 128).T)
        common[f"laA{knum}"] = np.ascontiguousarray(I["L1_ffn_lora_a"][i0].astype(np.float32).T)
        common[f"laB{knum}"] = np.ascontiguousarray(I["L1_ffn_lora_a"][i1].astype(np.float32).T)
        common[f"lbA{knum}"] = np.ascontiguousarray(I["L1_ffn_lora_b"][i0].astype(np.float32).T)
        common[f"lbB{knum}"] = np.ascontiguousarray(I["L1_ffn_lora_b"][i1].astype(np.float32).T)
        common[f"gam{knum}"] = np.ascontiguousarray(I["L1_ffn_ln_gamma"][knum].astype(np.float32).reshape(8, 128).T)
        common[f"bet{knum}"] = np.ascontiguousarray(I["L1_ffn_ln_beta"][knum].astype(np.float32).reshape(8, 128).T)

    in_maps = []
    for cc in range(NCORE):
        cols = slice(1024 * cc, 1024 * (cc + 1))
        xs = np.concatenate([x[0][:, cols], x[1][:, cols]], axis=1)
        pos = np.concatenate([np.arange(1024 * cc, 1024 * (cc + 1))] * 2)
        ct, st = l0f_rope_tables(pos)
        msel = np.zeros((128, 8), np.float32)
        if cc > 0:
            msel[:, cc - 1] = 1.0
        m = dict(common)
        m["x"] = np.ascontiguousarray(xs)
        m["ctabB"] = ct
        m["stabB"] = st
        m["mselB"] = msel
        m["bmB"] = bm_masks(cc)
        in_maps.append(m)

    if not _nc_cache:
        _nc_cache.append(build_mega())
    nc = _nc_cache[0]
    res = run_bass_kernel_spmd(nc, in_maps, core_ids=list(range(NCORE)))
    globals()["LAST"] = res
    out = np.zeros((B, D, T), np.float32)
    for cc in range(NCORE):
        r = res.results[cc]["xo"]
        out[0][:, 1024 * cc:1024 * (cc + 1)] = r[:, 0:1024]
        out[1][:, 1024 * cc:1024 * (cc + 1)] = r[:, 1024:2048]
    return np.ascontiguousarray(out.transpose(0, 2, 1))


# version 06
# speedup vs baseline: 1.0043x; 1.0043x over previous
import numpy as np
from contextlib import ExitStack
import concourse.bass as bass
import concourse.mybir as mybir
from concourse.bass_utils import run_bass_kernel_spmd
from concourse.aot_env import allow_ici_all_to_all
import ml_dtypes

F32 = mybir.dt.float32
BF16 = mybir.dt.bfloat16
I32 = mybir.dt.int32
AF = mybir.ActivationFunctionType
ALU = mybir.AluOpType
AX = mybir.AxisListType
BF = ml_dtypes.bfloat16

NCORE = 8
B = 2
T = 8192
D = 1024
TOK = 2048
TT = 512
NT = TOK // TT
ND = D // 128
EPS = 1e-5
SLOT = 262144
RG = [[0, 1, 2, 3, 4, 5, 6, 7]]


class KB:
    def __init__(self):
        self.nc = bass.Bass("TRN2", target_bir_lowering=False)
        nc = self.nc
        self.es = ExitStack()
        self.ps_es = self.es
        self.V, self.S, self.T, self.SY, self.G = nc.vector, nc.scalar, nc.tensor, nc.sync, nc.gpsimd
        self.counts = {}
        self.sems = {}
        self.waited = {}
        self.tag = "g"
        self.cur = self.es
        for s in ("ld", "st", "gd", "t", "v", "s", "g", "cc"):
            self.sems[s] = self.es.enter_context(nc.semaphore("sem_" + s))
            self.counts[s] = 0

    def inc(self, ins, name, k=1):
        self.counts[name] += k
        ins.then_inc(self.sems[name], k)
        return self.counts[name]

    def n(self, name):
        return self.counts[name]

    def incw(self, eng, ins, name, k=1):
        self.inc(ins, name, k)
        self.wait(eng, name)

    def wait(self, eng, name, val=None):
        if val is None:
            val = self.counts[name]
        if val <= 0:
            return
        key = (id(eng), name)
        if self.waited.get(key, 0) >= val:
            return
        eng.wait_ge(self.sems[name], val)
        self.waited[key] = val

    def sb(self, name, shape, dt):
        return self.cur.enter_context(self.nc.sbuf_tensor(self.tag + "_" + name, shape, dt))

    def ps(self, name, shape, dt=F32):
        return self.cur.enter_context(self.nc.psum_tensor(self.tag + "_" + name, shape, dt))

    def din(self, name, shape, dt):
        return self.nc.dram_tensor(name, shape, dt, kind="ExternalInput").ap()

    def dout(self, name, shape, dt):
        return self.nc.dram_tensor(name, shape, dt, kind="ExternalOutput").ap()

    def dint(self, name, shape, dt):
        return self.nc.dram_tensor(name, shape, dt).ap()

    def barrier(self):
        for eng in (self.SY, self.V, self.S, self.T, self.G):
            for s in self.counts:
                self.wait(eng, s)

    def phase(self, tag):
        self.barrier()
        if self.cur is not self.es:
            self.cur.close()
        self.cur = ExitStack()
        self.tag = tag

    def collective(self, kind, src_ap, dst_ap):
        G = self.G
        self.wait(G, "st")
        self.wait(G, "gd")
        if kind == "AllToAll":
            with allow_ici_all_to_all():
                ins = G.collective_compute(kind, mybir.AluOpType.bypass,
                                           replica_groups=RG, ins=[src_ap.opt()],
                                           outs=[dst_ap.opt()])
        else:
            ins = G.collective_compute(kind, mybir.AluOpType.bypass,
                                       replica_groups=RG, ins=[src_ap.opt()],
                                       outs=[dst_ap.opt()])
        self.inc(ins, "cc", 1)

    def fin(self):
        self.barrier()
        if self.cur is not self.es:
            self.cur.close()
            self.cur = self.es
        self.es.close()
        return self.nc

    _fc_n = 0
    def fconst(self, val, parts=1):
        key = ("fc", val, parts)
        if not hasattr(self, "_fcache"):
            self._fcache = {}
        if key not in self._fcache:
            KB._fc_n += 1
            sv = self.cur
            self.cur = self.es
            tns = self.sb(f"fc{KB._fc_n}_{parts}", [parts, 1], F32)
            self.cur = sv
            self.inc(self.G.memset(tns[:], float(val)), "g")
            self._fcache[key] = tns
        return self._fcache[key][:]


def qk_slot(X, s, k):
    return X[s:s + 1, k:k + 1, :].rearrange("a b (p c) -> (a b p) c", p=256)


def v_slot(X, s):
    return X[s:s + 1, 2:3, :].rearrange("a b (t f) -> (a b t) f", t=1024)


def y_slot(X, s):
    return X[s:s + 1, :].rearrange("a (p c) -> (a p) c", p=256)


def emit_consts(kb):
    c = {}
    sv = kb.cur
    kb.cur = kb.es
    c["ones_col"] = kb.sb("c_ones_col", [128, 1], BF16)
    c["ones_row"] = kb.sb("c_ones_row", [1, 128], BF16)
    c["idn2"] = kb.sb("c_idn2", [128, 64], BF16)
    kb.cur = sv
    G = kb.G
    kb.inc(G.memset(c["ones_col"][:], 1.0), "g")
    kb.inc(G.memset(c["ones_row"][:], 1.0), "g")
    kb.inc(G.memset(c["idn2"][0:64, :], 1.0), "g")
    kb.inc(G.affine_select(out=c["idn2"][0:64, :], in_=c["idn2"][0:64, :], compare_op=ALU.is_equal,
                           fill=0.0, base=0, pattern=[[-1, 64]], channel_multiplier=1), "g")
    kb.wait(kb.V, "g")
    kb.inc(kb.V.tensor_copy(c["idn2"][64:128, :], c["idn2"][0:64, :]), "v")
    return c


def emit_diag_masks(kb, with_anti):
    G = kb.G
    dm = kb.sb("dmask", [128, 4 * 512], BF16)
    kb.inc(G.memset(dm[:], 1.0), "g")
    for r in range(4):
        kb.inc(G.affine_select(out=dm[:, r * 512:(r + 1) * 512], in_=dm[:, r * 512:(r + 1) * 512],
                               compare_op=ALU.is_ge, fill=0.0, base=-128 * r,
                               pattern=[[1, 512]], channel_multiplier=-1), "g")
    am = None
    if with_anti:
        am = kb.sb("amask", [128, 4 * 512], BF16)
        kb.inc(G.memset(am[:], 1.0), "g")
        for r in range(4):
            kb.inc(G.affine_select(out=am[:, r * 512:(r + 1) * 512], in_=am[:, r * 512:(r + 1) * 512],
                                   compare_op=ALU.is_ge, fill=0.0, base=128 * r - 1,
                                   pattern=[[-1, 512]], channel_multiplier=1), "g")
    return dm, am


class LNW:
    def __init__(self, kb, tag):
        self.xb = kb.sb(tag + "_xb", [128, 512], BF16)
        self.xsq = kb.sb(tag + "_xsq", [128, 512], BF16)
        self.nm = kb.sb(tag + "_nm", [1, 512], F32)
        self.ex2 = kb.sb(tag + "_ex2", [1, 512], F32)
        self.m2 = kb.sb(tag + "_m2", [1, 512], F32)
        self.var = kb.sb(tag + "_var", [1, 512], F32)
        self.rs = kb.sb(tag + "_rs", [1, 512], F32)
        self.nmb = kb.sb(tag + "_nmb", [1, 512], BF16)
        self.rsb = kb.sb(tag + "_rsb", [1, 512], BF16)
        self.t1 = kb.sb(tag + "_t1", [128, 512], F32)


def emit_stats(kb, w, get_src, c, P_s, P_q, rms=False, src_bf16=False, ncols=512):
    V, S, Tn = kb.V, kb.S, kb.T
    kb.wait(Tn, "v")
    kb.wait(Tn, "s")
    kb.wait(S, "v")
    for kc in range(ND):
        src = get_src(kc)
        if not rms:
            if src_bf16:
                xb = src
            else:
                kb.wait(V, "t")
                kb.inc(V.tensor_copy(w.xb[:, :ncols], src), "v")
                xb = w.xb[:, :ncols]
                kb.wait(Tn, "v")
            kb.wait(Tn, "g")
            kb.inc(Tn.matmul(P_s[0:1, :ncols], c["ones_col"][:], xb,
                             start=(kc == 0), stop=(kc == ND - 1)), "t")
        kb.wait(S, "t")
        kb.inc(S.activation(w.xsq[:, :ncols], src, AF.Square), "s")
        kb.wait(Tn, "s")
        kb.wait(Tn, "g")
        kb.inc(Tn.matmul(P_q[0:1, :ncols], c["ones_col"][:], w.xsq[:, :ncols],
                         start=(kc == 0), stop=(kc == ND - 1)), "t")
    kb.wait(V, "t")
    kb.inc(V.tensor_scalar_mul(w.ex2[:, :ncols], P_q[0:1, :ncols], 1.0 / D), "v")
    if rms:
        kb.wait(S, "v")
        kb.wait(S, "g")
        kb.inc(S.activation(w.var[:, :ncols], w.ex2[:, :ncols], AF.Ln, bias=kb.fconst(EPS)), "s")
        kb.inc(S.activation(w.rs[:, :ncols], w.var[:, :ncols], AF.Exp, scale=-0.5), "s")
        kb.wait(V, "s")
        kb.inc(V.tensor_copy(w.rsb[:, :ncols], w.rs[:, :ncols]), "v")
    else:
        kb.inc(V.tensor_scalar_mul(w.nm[:, :ncols], P_s[0:1, :ncols], -1.0 / D), "v")
        kb.inc(V.tensor_mul(w.m2[:, :ncols], w.nm[:, :ncols], w.nm[:, :ncols]), "v")
        kb.inc(V.tensor_sub(w.var[:, :ncols], w.ex2[:, :ncols], w.m2[:, :ncols]), "v")
        kb.wait(S, "v")
        kb.wait(S, "g")
        kb.inc(S.activation(w.m2[:, :ncols], w.var[:, :ncols], AF.Ln, bias=kb.fconst(EPS)), "s")
        kb.inc(S.activation(w.rs[:, :ncols], w.m2[:, :ncols], AF.Exp, scale=-0.5), "s")
        kb.wait(V, "s")
        kb.inc(V.tensor_copy(w.nmb[:, :ncols], w.nm[:, :ncols]), "v")
        kb.inc(V.tensor_copy(w.rsb[:, :ncols], w.rs[:, :ncols]), "v")


def emit_bcast(kb, c, row_b, P_dst, ncols=512):
    kb.wait(kb.T, "v")
    kb.wait(kb.T, "g")
    kb.inc(kb.T.matmul(P_dst[:, :ncols], c["ones_row"][:], row_b[0:1, :ncols],
                       start=True, stop=True), "t")


def load_x_fm(kb, dst_sb, src_dram, sem="ld", eng=None):
    ncols = src_dram.shape[1]
    src = src_dram.rearrange("(kc p) t -> p kc t", p=128)
    dst = dst_sb[:].rearrange("p (kc t) -> p kc t", t=ncols)
    kb.inc((eng or kb.SY).dma_start(out=dst, in_=src), sem, 16)


def emit_A1(kb, c, x, Wa, ba, Rq, X1i):
    kb.phase("a1")
    xsb = kb.sb("xsb", [128, ND * TOK], F32)
    Wsb = kb.sb("Wsb", [128, ND * 3 * D], BF16)
    basb = kb.sb("basb", [128, 24], F32)
    Rsb = kb.sb("Rsb", [128, D], BF16)
    xnb = kb.sb("xnb", [128, ND * 512], BF16)
    qtmp = kb.sb("qtmp", [128, 512], BF16)
    stg = [kb.sb(f"stg{i}", [128, 512], BF16) for i in range(2)]
    vt = [kb.sb(f"vt{i}", [128, 4 * D], BF16) for i in range(2)]
    w = LNW(kb, "ln")

    P_s = kb.ps("P_s", [1, 512])
    P_q = kb.ps("P_q", [1, 512])
    P_b1 = kb.ps("P_b1", [128, 512])
    P_b2 = kb.ps("P_b2", [128, 512])
    P_m = [kb.ps(f"P_m{i}", [128, 512]) for i in range(2)]
    P_r = kb.ps("P_r", [128, 512])
    P_t = kb.ps("P_t", [128, 64], BF16)

    SY, V, S, Tn, G = kb.SY, kb.V, kb.S, kb.T, kb.G
    load_x_fm(kb, xsb, x)
    kb.inc(SY.dma_start(out=basb[:], in_=ba), "ld", 16)
    kb.inc(SY.dma_start(out=Rsb[:], in_=Rq), "ld", 16)
    for kc in range(ND):
        kb.inc(G.dma_start(out=Wsb[:, kc * 3 * D:(kc + 1) * 3 * D],
                           in_=Wa[kc * 128:(kc + 1) * 128, :]), "gd", 16)
    kb.wait(V, "ld"); kb.wait(S, "ld")

    st_marks = {}
    for t in range(NT):
        xt = lambda kc: xsb[:, kc * TOK + t * 512: kc * TOK + (t + 1) * 512]
        emit_stats(kb, w, xt, c, P_s, P_q)
        emit_bcast(kb, c, w.nmb, P_b1)
        emit_bcast(kb, c, w.rsb, P_b2)
        for kc in range(ND):
            kb.wait(V, "t")
            kb.inc(V.tensor_add(w.t1[:], xt(kc), P_b1[:]), "v")
            kb.inc(V.tensor_mul(xnb[:, kc * 512:(kc + 1) * 512], w.t1[:], P_b2[:]), "v")
        kb.wait(Tn, "v")
        kb.wait(Tn, "gd")
        for oc in range(24):
            pm = P_m[oc % 2]
            kb.wait(Tn, "s")
            for kc in range(ND):
                kb.inc(Tn.matmul(pm[:], Wsb[:, kc * 3 * D + oc * 128: kc * 3 * D + (oc + 1) * 128],
                                 xnb[:, kc * 512:(kc + 1) * 512],
                                 start=(kc == 0), stop=(kc == ND - 1)), "t")
            kb.wait(S, "t")
            if oc < 16:
                kb.inc(S.activation(qtmp[:], pm[:], AF.Identity, bias=basb[:, oc:oc + 1]), "s")
                kb.wait(Tn, "s")
                kb.wait(Tn, "v")
                kb.inc(Tn.matmul(P_r[:], Rsb[:, (oc % 8) * 128:(oc % 8 + 1) * 128], qtmp[:],
                                 start=True, stop=True), "t")
                sg = stg[oc % 2]
                if id(sg) in st_marks:
                    kb.wait(V, "st", st_marks[id(sg)])
                kb.wait(V, "t")
                kb.inc(V.tensor_copy(sg[:], P_r[:]), "v")
                kb.wait(SY, "v")
                k = 0 if oc < 8 else 1
                j = oc % 8
                s = 4 * (t // 2) + j // 2
                slot = qk_slot(X1i, s, k)
                kb.inc(SY.dma_start(out=slot[128 * (j % 2):128 * (j % 2) + 128,
                                             512 * (t % 2):512 * (t % 2) + 512], in_=sg[:]), "st", 16)
                st_marks[id(sg)] = kb.n("st")
            else:
                vc = oc - 16
                vbuf = vt[t % 2]
                if (vc == 0) and (id(vbuf) in st_marks):
                    kb.wait(S, "st", st_marks[id(vbuf)])
                    kb.wait(V, "st", st_marks[id(vbuf)])
                kb.inc(S.activation(qtmp[:], pm[:], AF.Identity, bias=basb[:, oc:oc + 1]), "s")
                for hh in range(2):
                    for p4 in range(4):
                        kb.wait(Tn, "s")
                        kb.wait(Tn, "v")
                        kb.inc(Tn.transpose(P_t[:], qtmp[64 * hh:64 * hh + 64, 128 * p4:128 * (p4 + 1)],
                                            c["idn2"][64 * hh:64 * hh + 64, :]), "t")
                        kb.wait(V, "t")
                        col = 1024 * p4 + 64 * (2 * vc + hh)
                        kb.inc(V.tensor_copy(vbuf[:, col:col + 64], P_t[:]), "v")
                if vc == 7:
                    kb.wait(SY, "v")
                    for p4 in range(4):
                        for q in range(4):
                            s = 4 * (t // 2) + q
                            r0 = 512 * (t % 2) + 128 * p4
                            kb.inc(SY.dma_start(out=v_slot(X1i, s)[r0:r0 + 128, :],
                                                in_=vbuf[:, 1024 * p4 + 256 * q: 1024 * p4 + 256 * q + 256]),
                                   "st", 16)
                    st_marks[id(vbuf)] = kb.n("st")


def emit_flash(kb, c, Xo, Yi, l1_mode, tag):
    kb.phase(tag)
    dm, am = emit_diag_masks(kb, with_anti=l1_mode)

    Qsb = kb.sb("Qsb", [128, 2 * T], BF16)
    Ksb = kb.sb("Ksb", [128, 2 * T], BF16)
    Vsb = [kb.sb(f"Vsb{i}", [128, 64 * 65], BF16) for i in range(4)]
    Ysb = kb.sb("Ysb", [128, 2 * T], BF16)
    esT = [kb.sb(f"esT{i}", [128, 4 * 512], BF16) for i in range(2)]
    Yu = kb.sb("Yu", [65, 16 * 512], BF16)
    drS = kb.sb("drS", [1, 16 * 512], BF16)
    bcs = kb.sb("bcs", [64, 512], F32)
    tsb = kb.sb("tsb", [65, 64], F32) if l1_mode else None
    tstb = kb.sb("tstb", [64, 65], F32) if l1_mode else None
    utri = kb.sb("utri", [64, 64], F32) if l1_mode else None
    sufsb = kb.sb("sufsb", [65, 64], F32) if l1_mode else None
    idn65 = kb.sb("idn65", [65, 65], F32) if l1_mode else None

    P_S = [kb.ps(f"P_S{i}", [128, 3 * 512]) for i in range(2)]
    P_y = kb.ps("P_y", [128, 512])
    P_bc = kb.ps("P_bc", [128, 512])
    P_ts = P_bc[0:65, 0:64]
    P_tt = P_y[0:64, 0:65]

    SY, V, S, Tn, G = kb.SY, kb.V, kb.S, kb.T, kb.G
    if l1_mode:
        kb.inc(G.memset(utri[:], 1.0), "g")
        kb.inc(G.affine_select(out=utri[:], in_=utri[:], compare_op=ALU.is_ge,
                               fill=0.0, base=0, pattern=[[-1, 64]], channel_multiplier=1), "g")
        kb.inc(G.memset(idn65[:], 1.0), "g")
        kb.inc(G.affine_select(out=idn65[:], in_=idn65[:], compare_op=ALU.is_equal,
                               fill=0.0, base=0, pattern=[[-1, 65]], channel_multiplier=1), "g")

    kb.wait(SY, "cc")
    for pr in range(2):
        off = 131072 * pr
        kb.inc(SY.dma_start(out=Qsb[0:128, pr * T:(pr + 1) * T].rearrange("p (s cc) -> p s cc", cc=1024),
                            in_=Xo[:, 0:1, off:off + 131072].rearrange("s b (p cc) -> p s (b cc)", p=128)),
               "ld", 16)
        kb.inc(SY.dma_start(out=Ksb[0:128, pr * T:(pr + 1) * T].rearrange("p (s cc) -> p s cc", cc=1024),
                            in_=Xo[:, 1:2, off:off + 131072].rearrange("s b (p cc) -> p s (b cc)", p=128)),
               "ld", 16)
    for hh in range(4):
        for s in range(8):
            kb.inc(SY.dma_start(
                out=Vsb[hh][:].rearrange("p (j cc) -> p j cc", cc=65)[:, 8 * s:8 * s + 8, 0:64],
                in_=v_slot(Xo, s)[:, 64 * hh:64 * hh + 64].rearrange("(j p) cc -> p j cc", p=128)),
                "ld", 16)
    kb.wait(V, "ld")
    for hh in range(4):
        onescol = Vsb[hh][:].rearrange("p (j cc) -> p j cc", cc=65)[:, :, 64:65]
        kb.inc(V.memset(onescol, 1.0), "v")

    yu_mark = 0
    exp_mark = [0, 0]
    av_mark = [0, 0]
    for hh in range(4):
        pr = hh // 2
        qrow = 64 * (hh % 2)
        kb.wait(Tn, "ld")
        kb.wait(Tn, "v")
        if l1_mode:
            for j in range(64):
                kb.inc(Tn.matmul(P_ts[:, j:j + 1], Vsb[hh][:, j * 65:(j + 1) * 65],
                                 c["ones_col"][:], start=(j == 0), stop=(j == 63)), "t")
            kb.wait(V, "t")
            kb.inc(V.tensor_copy(tsb[:], P_ts[:, :]), "v")
            kb.wait(Tn, "v")
            kb.inc(Tn.transpose(P_tt[:, :], tsb[:], idn65[:]), "t")
            kb.wait(V, "t")
            kb.inc(V.tensor_copy(tstb[:], P_tt[:, :]), "v")
            kb.wait(Tn, "v")
            kb.inc(Tn.matmul(P_ts[:, :], tstb[:], utri[:], start=True, stop=True), "t")
            kb.wait(V, "t")
            kb.inc(V.tensor_copy(sufsb[:], P_ts[:, :]), "v")
            kb.wait(Tn, "v")
        for qt in range(16):
            g0 = 512 * qt
            nkt = 4 * (qt + 1)
            qap = Qsb[qrow:qrow + 64, pr * T + g0: pr * T + g0 + 512]
            groups = []
            k0 = 0
            while k0 < nkt:
                gs = min(3, nkt - k0)
                groups.append((k0, gs))
                k0 += gs
            def emit_qk(gi):
                k0, gs = groups[gi]
                par = gi % 2
                ps = P_S[par]
                kb.wait(Tn, "s", exp_mark[par])
                for i4 in range(gs):
                    kt = k0 + i4
                    ins = Tn.matmul(ps[:, i4 * 512:(i4 + 1) * 512],
                                    Ksb[qrow:qrow + 64, pr * T + kt * 128: pr * T + (kt + 1) * 128],
                                    qap, start=True, stop=True)
                    if i4 == gs - 1:
                        kb.inc(ins, "t")
                qk_marks[gi] = kb.n("t")
            qk_marks = {}
            emit_qk(0)
            for gi, (k0, gs) in enumerate(groups):
                par = gi % 2
                eb = esT[par]
                ps = P_S[par]
                kb.wait(S, "t", qk_marks[gi])
                kb.inc(S.activation(eb[:, 0:gs * 512], ps[:, 0:gs * 512], AF.Exp, scale=0.125), "s")
                exp_mark[par] = kb.n("s")
                need_v = False
                for i4 in range(gs):
                    kt = k0 + i4
                    r = kt - (nkt - 4)
                    if r >= 0:
                        if not need_v:
                            kb.wait(V, "s", exp_mark[par])
                            kb.wait(V, "g")
                            need_v = True
                        kb.inc(V.tensor_mul(eb[:, i4 * 512:(i4 + 1) * 512],
                                            eb[:, i4 * 512:(i4 + 1) * 512],
                                            dm[:, r * 512:(r + 1) * 512]), "v")
                        if l1_mode:
                            kb.inc(V.tensor_add(eb[:, i4 * 512:(i4 + 1) * 512],
                                                eb[:, i4 * 512:(i4 + 1) * 512],
                                                am[:, r * 512:(r + 1) * 512]), "v")
                if gi + 1 < len(groups):
                    emit_qk(gi + 1)
                if need_v:
                    kb.wait(Tn, "v", kb.n("v"))
                else:
                    kb.wait(Tn, "s", exp_mark[par])
                if gi == 0:
                    kb.wait(Tn, "v", yu_mark)
                for i4 in range(gs):
                    kt = k0 + i4
                    ins = Tn.matmul(P_y[0:65, :], Vsb[hh][:, kt * 65:(kt + 1) * 65],
                                    eb[:, i4 * 512:(i4 + 1) * 512],
                                    start=(kt == 0), stop=(kt == nkt - 1))
                    if i4 == gs - 1:
                        kb.inc(ins, "t")
                av_mark[par] = kb.n("t")
            kb.wait(V, "t")
            ys = Yu[:, qt * 512:(qt + 1) * 512]
            if l1_mode:
                jq = (g0 + 512) // 128
                if jq < 64:
                    kb.inc(V.tensor_scalar(ys, P_y[0:65, :], sufsb[:, jq:jq + 1], None, ALU.add), "v")
                else:
                    kb.inc(V.tensor_copy(ys, P_y[0:65, :]), "v")
            else:
                kb.inc(V.tensor_copy(ys, P_y[0:65, :]), "v")
            yu_mark = kb.n("v")
        with kb.nc.allow_low_precision(reason="bf16 denominator reciprocal feeding bf16 bcast matmul"):
            for qt in range(16):
                kb.inc(V.reciprocal(drS[:, qt * 512:(qt + 1) * 512],
                                    Yu[64:65, qt * 512:(qt + 1) * 512]), "v")
        rec_mark = kb.n("v")
        bc_banks = [P_y[0:64, :], P_bc[0:64, :]]
        for qt in range(16):
            pb = bc_banks[qt % 2]
            kb.wait(Tn, "v", rec_mark)
            if qt >= 2:
                kb.wait(Tn, "v")
            kb.inc(Tn.matmul(pb, c["ones_row"][0:1, 0:64],
                             drS[0:1, qt * 512:(qt + 1) * 512], start=True, stop=True), "t")
            kb.wait(V, "t")
            kb.inc(V.tensor_copy(bcs[:], pb), "v")
            kb.inc(V.tensor_mul(Ysb[qrow:qrow + 64, pr * T + qt * 512: pr * T + (qt + 1) * 512],
                                Yu[0:64, qt * 512:(qt + 1) * 512], bcs[:]), "v")
        yu_mark = kb.n("v")
    kb.wait(SY, "v")
    for hh in range(4):
        for s in range(8):
            kb.inc(SY.dma_start(out=y_slot(Yi, s)[64 * hh:64 * hh + 64, :],
                                in_=Ysb[64 * (hh % 2):64 * (hh % 2) + 64,
                                        (hh // 2) * T + 1024 * s: (hh // 2) * T + 1024 * s + 1024]),
                   "st", 16)


def emit_PS(kb, c, x, X2o, Wp, bp, xo):
    kb.phase("ps")
    xsb = kb.sb("xsb", [128, ND * TOK], F32)
    ysb = kb.sb("ysb", [128, ND * TOK], BF16)
    Wsb = kb.sb("Wsb", [128, ND * D], BF16)
    bpsb = kb.sb("bpsb", [128, 8], F32)
    zb = kb.sb("zb", [128, ND * 512], BF16)
    w = LNW(kb, "ln")

    P_s = kb.ps("P_s", [1, 512])
    P_q = kb.ps("P_q", [1, 512])
    P_b1 = kb.ps("P_b1", [128, 512])
    P_b2 = kb.ps("P_b2", [128, 512])
    P_m = [kb.ps(f"P_m{i}", [128, 512]) for i in range(2)]

    SY, V, S, Tn, G = kb.SY, kb.V, kb.S, kb.T, kb.G
    load_x_fm(kb, xsb, x)
    kb.wait(SY, "cc")
    for r in range(8):
        ys = y_slot(X2o, r)
        for half in range(2):
            kc = 2 * (r % 4) + half
            kb.inc(SY.dma_start(out=ysb[:, kc * TOK + 1024 * (r // 4): kc * TOK + 1024 * (r // 4) + 1024],
                                in_=ys[128 * half:128 * half + 128, :]), "ld", 16)
    for kc in range(ND):
        kb.inc(G.dma_start(out=Wsb[:, kc * D:(kc + 1) * D], in_=Wp[kc * 128:(kc + 1) * 128, :]), "gd", 16)
    kb.inc(SY.dma_start(out=bpsb[:], in_=bp), "ld", 16)
    kb.wait(Tn, "ld"); kb.wait(Tn, "gd"); kb.wait(S, "ld"); kb.wait(V, "ld")

    for t in range(NT):
        for oc in range(ND):
            pm = P_m[oc % 2]
            kb.wait(Tn, "s")
            for kc in range(ND):
                kb.inc(Tn.matmul(pm[:], Wsb[:, kc * D + oc * 128: kc * D + (oc + 1) * 128],
                                 ysb[:, kc * TOK + t * 512: kc * TOK + (t + 1) * 512],
                                 start=(kc == 0), stop=(kc == ND - 1)), "t")
            kb.wait(S, "t")
            kb.wait(S, "v")
            kb.inc(S.activation(zb[:, oc * 512:(oc + 1) * 512], pm[:], AF.Identity,
                                bias=bpsb[:, oc:oc + 1]), "s")
        kb.wait(V, "s")
        zt = lambda kc: zb[:, kc * 512:(kc + 1) * 512]
        emit_stats(kb, w, zt, c, P_s, P_q, src_bf16=True)
        emit_bcast(kb, c, w.nmb, P_b1)
        emit_bcast(kb, c, w.rsb, P_b2)
        for kc in range(ND):
            kb.wait(V, "t")
            xap = xsb[:, kc * TOK + t * 512: kc * TOK + (t + 1) * 512]
            kb.inc(V.tensor_add(w.t1[:], zt(kc), P_b1[:]), "v")
            kb.inc(V.tensor_mul(w.t1[:], w.t1[:], P_b2[:]), "v")
            kb.inc(V.tensor_add(xap, xap, w.t1[:]), "v")
        kb.wait(SY, "v")
        dst = xo.rearrange("(kc p) t -> p kc t", p=128)[:, :, t * 512:(t + 1) * 512]
        src = xsb[:].rearrange("p (kc t) -> p kc t", t=TOK)[:, :, t * 512:(t + 1) * 512]
        kb.inc(SY.dma_start(out=dst, in_=src), "st", 16)


def emit_B1(kb, c, x, Wq, bq, anw, ctab_d, stab_d, Psw_d, xno, Qo, Ko, Vo, X3i):
    kb.phase("b1")
    xsb = kb.sb("xsb", [128, ND * TOK], F32)
    Wsb = kb.sb("Wsb", [128, ND * 1536], BF16)
    bqsb = kb.sb("bqsb", [128, 12], F32)
    anwsb = kb.sb("anwsb", [128, 8], F32)
    ctab = kb.sb("ctab_s", [128, TOK], BF16)
    stab = kb.sb("stab_s", [128, TOK], BF16)
    Psw = kb.sb("Psw_s", [128, 128], BF16)
    xnf = kb.sb("xnf", [128, ND * 512], F32)
    hb = kb.sb("hb", [128, ND * 512], BF16)
    qtmp = kb.sb("qtmp", [128, 512], BF16)
    psb = kb.sb("psb", [128, 512], BF16)
    t2b = kb.sb("t2b", [128, 512], BF16)
    stg = [kb.sb(f"stg{i}", [128, 512], BF16) for i in range(2)]
    vt = [kb.sb(f"vt{i}", [128, 4 * 256], BF16) for i in range(2)]
    hal = kb.sb("hal", [128, 1024], BF16)
    w = LNW(kb, "ln")

    P_s = kb.ps("P_s", [1, 512])
    P_q = kb.ps("P_q", [1, 512])
    P_b1 = kb.ps("P_b1", [128, 512])
    P_b2 = kb.ps("P_b2", [128, 512])
    P_m = [kb.ps(f"P_m{i}", [128, 512]) for i in range(2)]
    P_r = kb.ps("P_r", [128, 512])
    P_t = kb.ps("P_t", [128, 64], BF16)

    SY, V, S, Tn, G = kb.SY, kb.V, kb.S, kb.T, kb.G
    load_x_fm(kb, xsb, x)
    for kc in range(ND):
        kb.inc(G.dma_start(out=Wsb[:, kc * 1536:(kc + 1) * 1536],
                           in_=Wq[kc * 128:(kc + 1) * 128, :]), "gd", 16)
    kb.inc(SY.dma_start(out=bqsb[:], in_=bq), "ld", 16)
    kb.inc(SY.dma_start(out=anwsb[:], in_=anw), "ld", 16)
    kb.inc(SY.dma_start(out=ctab[:], in_=ctab_d), "ld", 16)
    kb.inc(SY.dma_start(out=stab[:], in_=stab_d), "ld", 16)
    kb.inc(SY.dma_start(out=Psw[:], in_=Psw_d), "ld", 16)
    kb.wait(V, "ld"); kb.wait(S, "ld")

    st_marks = {}
    for t in range(NT):
        xt = lambda kc: xsb[:, kc * TOK + t * 512: kc * TOK + (t + 1) * 512]
        emit_stats(kb, w, xt, c, P_s, P_q)
        emit_bcast(kb, c, w.nmb, P_b1)
        emit_bcast(kb, c, w.rsb, P_b2)
        for kc in range(ND):
            kb.wait(V, "t")
            kb.inc(V.tensor_add(w.t1[:], xt(kc), P_b1[:]), "v")
            kb.inc(V.tensor_mul(xnf[:, kc * 512:(kc + 1) * 512], w.t1[:], P_b2[:]), "v")
            kb.inc(V.tensor_copy(hb[:, kc * 512:(kc + 1) * 512], xnf[:, kc * 512:(kc + 1) * 512]), "v")
        kb.wait(SY, "v")
        dst = xno.rearrange("(kc p) t -> p kc t", p=128)[:, :, t * 512:(t + 1) * 512]
        kb.inc(SY.dma_start(out=dst, in_=hb[:].rearrange("p (kc t) -> p kc t", t=512)), "st", 16)
        xn_st_mark = kb.n("st")
        xnt = lambda kc: xnf[:, kc * 512:(kc + 1) * 512]
        emit_stats(kb, w, xnt, c, P_s, P_q, rms=True)
        emit_bcast(kb, c, w.rsb, P_b2)
        kb.wait(V, "st", xn_st_mark)
        for kc in range(ND):
            kb.wait(V, "t")
            kb.inc(V.tensor_mul(w.t1[:], xnt(kc), P_b2[:]), "v")
            kb.inc(V.tensor_scalar_mul(hb[:, kc * 512:(kc + 1) * 512], w.t1[:], anwsb[:, kc:kc + 1]), "v")
        kb.wait(Tn, "v")
        for oc in range(12):
            pm = P_m[oc % 2]
            kb.wait(Tn, "s")
            for kc in range(ND):
                kb.inc(Tn.matmul(pm[:], Wsb[:, kc * 1536 + oc * 128: kc * 1536 + (oc + 1) * 128],
                                 hb[:, kc * 512:(kc + 1) * 512],
                                 start=(kc == 0), stop=(kc == ND - 1)), "t")
            kb.wait(S, "t")
            kb.inc(S.activation(qtmp[:], pm[:], AF.Identity, bias=bqsb[:, oc:oc + 1]), "s")
            if oc < 10:
                kb.wait(Tn, "s")
                kb.wait(Tn, "v")
                kb.inc(Tn.matmul(P_r[:], Psw[:], qtmp[:], start=True, stop=True), "t")
                sg = stg[oc % 2]
                if id(sg) in st_marks:
                    kb.wait(V, "st", st_marks[id(sg)])
                kb.wait(V, "t")
                kb.inc(V.tensor_copy(psb[:], P_r[:]), "v")
                kb.wait(V, "s")
                kb.inc(V.tensor_mul(t2b[:], qtmp[:], ctab[:, t * 512:(t + 1) * 512]), "v")
                kb.inc(V.tensor_mul(psb[:], psb[:], stab[:, t * 512:(t + 1) * 512]), "v")
                kb.inc(V.tensor_add(sg[:], t2b[:], psb[:]), "v")
                kb.wait(SY, "v")
                if oc < 8:
                    kb.inc(SY.dma_start(out=Qo[oc * 128:(oc + 1) * 128, t * 512:(t + 1) * 512],
                                        in_=sg[:]), "st", 16)
                else:
                    kb.inc(SY.dma_start(out=Ko[(oc - 8) * 128:(oc - 7) * 128, t * 512:(t + 1) * 512],
                                        in_=sg[:]), "st", 16)
                st_marks[id(sg)] = kb.n("st")
            else:
                vc = oc - 10
                vbuf = vt[t % 2]
                if (vc == 0) and (id(vbuf) in st_marks):
                    kb.wait(S, "st", st_marks[id(vbuf)])
                for hh in range(2):
                    for p4 in range(4):
                        kb.wait(Tn, "s")
                        kb.wait(Tn, "v")
                        kb.inc(Tn.transpose(P_t[:], qtmp[64 * hh:64 * hh + 64, 128 * p4:128 * (p4 + 1)],
                                            c["idn2"][64 * hh:64 * hh + 64, :]), "t")
                        kb.wait(V, "t")
                        col = 256 * p4 + 64 * (2 * vc + hh)
                        kb.inc(V.tensor_copy(vbuf[:, col:col + 64], P_t[:]), "v")
                if vc == 1:
                    kb.wait(SY, "v")
                    vdst = Vo[t * 512:(t + 1) * 512, :].rearrange("(p4 p) ch -> p p4 ch", p=128)
                    kb.inc(SY.dma_start(out=vdst,
                                        in_=vbuf[:].rearrange("p (p4 ch) -> p p4 ch", ch=256)), "st", 16)
                    st_marks[id(vbuf)] = kb.n("st")
    kb.wait(SY, "st")
    for ch in range(2):
        koff = ch * 512
        kb.inc(SY.dma_start(out=hal[:, koff:koff + 256].rearrange("p (h tt) -> p h tt", h=2),
                            in_=Ko[:, 1024 * ch + 896:1024 * ch + 1024]
                            .rearrange("(h p) tt -> p h tt", p=128)), "ld", 16)
        kb.inc(SY.dma_start(out=hal[:, koff + 256:koff + 512],
                            in_=Vo[1024 * ch + 896:1024 * ch + 1024, :]), "ld", 16)
    kb.wait(SY, "ld")
    for ch in range(2):
        koff = ch * 512
        kb.inc(SY.dma_start(out=X3i[2 * ch:2 * ch + 1, :].rearrange("a (h p tt) -> p (a h) tt", p=128, tt=128),
                            in_=hal[:, koff:koff + 256].rearrange("p (h tt) -> p h tt", h=2)), "st", 16)
        kb.inc(SY.dma_start(out=X3i[2 * ch + 1:2 * ch + 2, :].rearrange("a (p f) -> p (a f)", p=128),
                            in_=hal[:, koff + 256:koff + 512]), "st", 16)


def emit_B2(kb, c, Qd, Kod, Vod, X3o, msel_d, bmd, snd, xnd, Wpd, bpd, fnwd, gwd, gbd,
            xmo, h2o, ceo):
    kb.phase("b2")
    Qsb = kb.sb("Qsb", [128, ND * TOK], BF16)
    Ksb = kb.sb("Ksb2", [128, 8 * 1152], BF16)
    Vsl = kb.sb("Vsl2", [128, 8 * 9 * 65], BF16)
    bm = kb.sb("bm_s", [128, 7 * 512], BF16)
    snsb = kb.sb("snsb", [1, 16], F32)
    snexp = kb.sb("snexp", [1, 16], F32)
    xnsb = kb.sb("xnsb", [128, ND * 512], BF16)
    Wsb = kb.sb("Wsb2", [128, ND * D], BF16)
    bpsb = kb.sb("bpsb2", [128, 8], F32)
    fnwsb = kb.sb("fnwsb", [128, 8], F32)
    gwsb = kb.sb("gwsb", [128, ND * 32], BF16)
    gbsb = kb.sb("gbsb", [32, 1], F32)
    Ysb = kb.sb("Ysb2", [128, ND * TOK], BF16)
    esT = [kb.sb(f"esT{i}", [128, 3 * 512], BF16) for i in range(2)]
    dr = kb.sb("dr", [1, 512], F32)
    drb = kb.sb("drb", [1, 512], BF16)
    bcs = kb.sb("bcs", [64, 512], F32)
    zb = kb.sb("zb", [128, ND * 512], BF16)
    xm_f = kb.sb("xm_f", [128, ND * 512], F32)
    h2b = kb.sb("h2b", [128, ND * 512], BF16)
    glf = kb.sb("glf", [32, 512], F32)
    I32f = kb.sb("I32f", [32, 32], F32)
    I128f = kb.sb("I128f", [128, 128], F32)
    gtm = kb.sb("gtm", [128, 32], F32)
    cur = kb.sb("cur", [128, 32], F32)
    eqb = kb.sb("eqb", [128, 32], F32)
    mrow = kb.sb("mrow", [128, 5], F32)
    nm0 = kb.sb("nm0", [128, 1], F32)
    rd_ = kb.sb("rd_", [128, 1], F32)
    et = kb.sb("et", [128, 32], F32)
    sel = kb.sb("sel", [128, 32], F32)
    wgt = kb.sb("wgt", [128, 32], F32)
    cetm = kb.sb("cetm", [128, 32], F32)
    cef = kb.sb("cef", [32, 512], F32)
    tsel = kb.sb("tsel", [128, 8 * 256], BF16)
    Ks = kb.sb("Ks", [128, 2 * 256], BF16)
    Vs = kb.sb("Vs", [128, 2 * 256], BF16)
    tmpsel = kb.sb("tmpsel", [128, 256], BF16)
    mselsb = kb.sb("mselsb", [128, 8], F32)
    w = LNW(kb, "ln")

    P_S = [kb.ps(f"P_Sw{i}", [128, 3 * 512]) for i in range(2)]
    P_y = kb.ps("P_y", [128, 512])
    P_x = kb.ps("P_x", [128, 512])
    P_m = [P_S[0][:, 0:512], P_S[1][:, 0:512]]
    P_a = [P_S[0][0:1, 1024:1536], P_S[1][0:1, 1024:1536]]
    P_t32 = P_S[0][:, 512:1024]

    SY, V, S, Tn, G = kb.SY, kb.V, kb.S, kb.T, kb.G
    kb.inc(G.memset(I32f[:], 1.0), "g")
    kb.inc(G.affine_select(out=I32f[:], in_=I32f[:], compare_op=ALU.is_equal,
                           fill=0.0, base=0, pattern=[[-1, 32]], channel_multiplier=1), "g")
    kb.inc(G.memset(I128f[:], 1.0), "g")
    kb.inc(G.affine_select(out=I128f[:], in_=I128f[:], compare_op=ALU.is_equal,
                           fill=0.0, base=0, pattern=[[-1, 128]], channel_multiplier=1), "g")
    load_x_fm(kb, Qsb, Qd)
    kb.inc(SY.dma_start(out=bm[:], in_=bmd), "ld", 16)
    kb.inc(SY.dma_start(out=snsb[:], in_=snd), "ld", 16)
    kb.inc(SY.dma_start(out=bpsb[:], in_=bpd), "ld", 16)
    kb.inc(SY.dma_start(out=fnwsb[:], in_=fnwd), "ld", 16)
    kb.inc(SY.dma_start(out=gbsb[:], in_=gbd), "ld", 16)
    kb.inc(SY.dma_start(out=mselsb[:], in_=msel_d), "ld", 16)
    for kc in range(ND):
        kb.inc(G.dma_start(out=Wsb[:, kc * D:(kc + 1) * D], in_=Wpd[kc * 128:(kc + 1) * 128, :]), "gd", 16)
        kb.inc(G.dma_start(out=gwsb[:, kc * 32:(kc + 1) * 32], in_=gwd[kc * 128:(kc + 1) * 128, :]), "gd", 16)
    for ch in range(2):
        for kh in range(4):
            bi = ch * 4 + kh
            src = Kod[64 * kh:64 * kh + 64, 1024 * ch:1024 * ch + 1024]
            for rh in range(2):
                kb.inc(SY.dma_start(out=Ksb[64 * rh:64 * rh + 64, bi * 1152 + 128: bi * 1152 + 1152],
                                    in_=src), "ld", 16)
            kb.inc(SY.dma_start(
                out=Vsl[:, bi * 585:(bi + 1) * 585].rearrange("p (j cc) -> p j cc", cc=65)[:, 1:9, 0:64],
                in_=Vod[1024 * ch:1024 * ch + 1024, 64 * kh:64 * kh + 64].rearrange("(j p) cc -> p j cc", p=128)),
                "ld", 16)
    kb.wait(SY, "cc")
    for ch in range(2):
        for kind in range(2):
            kb.wait(SY, "v")
            for s in range(8):
                if kind == 0:
                    kb.inc(SY.dma_start(
                        out=tsel[:, 256 * s:256 * (s + 1)].rearrange("p (h tt) -> p h tt", h=2),
                        in_=X3o[s:s + 1, 2 * ch:2 * ch + 1, :].rearrange("a b (h p tt) -> p (a h) tt",
                                                                        p=128, tt=128)), "ld", 16)
                else:
                    kb.inc(SY.dma_start(
                        out=tsel[:, 256 * s:256 * (s + 1)],
                        in_=X3o[s:s + 1, 2 * ch + 1:2 * ch + 2, :].rearrange("a b (p f) -> p (a b f)",
                                                                            p=128)), "ld", 16)
            kb.wait(V, "ld")
            dst = (Ks if kind == 0 else Vs)
            for s in range(8):
                blk = tsel[:, 256 * s:256 * (s + 1)]
                if s == 0:
                    kb.inc(V.tensor_scalar(dst[:, 256 * ch:256 * (ch + 1)], blk, mselsb[:, 0:1], None, ALU.mult), "v")
                else:
                    kb.inc(V.tensor_scalar(tmpsel[:], blk, mselsb[:, s:s + 1], None, ALU.mult), "v")
                    kb.inc(V.tensor_add(dst[:, 256 * ch:256 * (ch + 1)], dst[:, 256 * ch:256 * (ch + 1)],
                                        tmpsel[:]), "v")
    for ch in range(2):
        for kh in range(4):
            bi = ch * 4 + kh
            ksrc = Ks[64 * (kh % 2):64 * (kh % 2) + 64, 256 * ch + 128 * (kh // 2): 256 * ch + 128 * (kh // 2) + 128]
            for rh in range(2):
                kb.inc(V.tensor_copy(Ksb[64 * rh:64 * rh + 64, bi * 1152: bi * 1152 + 128], ksrc), "v")
            kb.inc(V.tensor_copy(Vsl[:, bi * 585: bi * 585 + 64],
                                 Vs[:, 256 * ch + 64 * kh: 256 * ch + 64 * kh + 64]), "v")
    onescol = Vsl[:].rearrange("p (j cc) -> p j cc", cc=65)[:, :, 64:65]
    kb.inc(V.memset(onescol, 1.0), "v")
    kb.wait(S, "ld")
    kb.inc(S.activation(snexp[:], snsb[:], AF.Exp), "s")

    epi_mark = 0
    exp_last = [0, 0]
    kb.wait(Tn, "ld")
    kb.wait(Tn, "v")
    kb.wait(Tn, "g")
    for h in range(16):
        kh = h // 4
        qrow = 64 * (h % 2)
        qpr = h // 2
        for qt in range(NT):
            chx = qt // 2
            qoff = (qt % 2) * 512
            qap = Qsb[qrow:qrow + 64, qpr * TOK + qt * 512: qpr * TOK + (qt + 1) * 512]
            wgroups = [(0, 3), (3, 2)]
            def emit_qkw(gi):
                k0, gs = wgroups[gi]
                par = gi % 2
                ps = P_S[par]
                kb.wait(Tn, "s", exp_last[par])
                for i4 in range(gs):
                    kt = k0 + i4
                    kcol = (chx * 4 + kh) * 1152 + qoff + kt * 128
                    ins = Tn.matmul(ps[:, i4 * 512:(i4 + 1) * 512],
                                    Ksb[qrow:qrow + 64, kcol:kcol + 128],
                                    qap, start=True, stop=True)
                    if i4 == gs - 1:
                        kb.inc(ins, "t")
                qkw_marks[gi] = kb.n("t")
            qkw_marks = {}
            emit_qkw(0)
            for gi, (k0, gs) in enumerate(wgroups):
                par = gi % 2
                ps = P_S[par]
                eb = esT[par]
                kb.wait(S, "t", qkw_marks[gi])
                kb.inc(S.activation(eb[:, 0:gs * 512], ps[:, 0:gs * 512], AF.Exp, scale=0.125), "s")
                exp_last[par] = kb.n("s")
                kb.wait(V, "s", exp_last[par])
                kb.wait(V, "g")
                for i4 in range(gs):
                    kt = k0 + i4
                    if qoff == 0 and kt == 0:
                        mi = 5 if chx == 0 else 6
                    else:
                        mi = kt
                    kb.inc(V.tensor_mul(eb[:, i4 * 512:(i4 + 1) * 512],
                                        eb[:, i4 * 512:(i4 + 1) * 512],
                                        bm[:, mi * 512:(mi + 1) * 512]), "v")
                if gi + 1 < len(wgroups):
                    emit_qkw(gi + 1)
                kb.wait(Tn, "v", kb.n("v"))
                if gi == 0:
                    kb.wait(Tn, "v", epi_mark)
                for i4 in range(gs):
                    kt = k0 + i4
                    vcol = (chx * 4 + kh) * 585 + (qoff // 128 + kt) * 65
                    ins = Tn.matmul(P_y[0:65, :], Vsl[:, vcol:vcol + 65],
                                    eb[:, i4 * 512:(i4 + 1) * 512],
                                    start=(kt == 0), stop=(kt == 4))
                    if i4 == gs - 1:
                        kb.inc(ins, "t")
            kb.wait(V, "t")
            kb.inc(V.tensor_scalar(dr[:], P_y[64:65, :], snexp[0:1, h:h + 1], None, ALU.add), "v")
            kb.inc(V.reciprocal(dr[:], dr[:]), "v")
            kb.inc(V.tensor_copy(drb[:], dr[:]), "v")
            kb.wait(Tn, "v")
            kb.inc(Tn.matmul(P_x[0:64, :], c["ones_row"][0:1, 0:64], drb[:], start=True, stop=True), "t")
            kb.wait(V, "t")
            kb.inc(V.tensor_copy(bcs[:], P_x[0:64, :]), "v")
            kb.inc(V.tensor_mul(Ysb[qrow:qrow + 64, qpr * TOK + qt * 512: qpr * TOK + (qt + 1) * 512],
                                P_y[0:64, :], bcs[:]), "v")
            epi_mark = kb.n("v")

    kb.wait(Tn, "gd")
    for t in range(NT):
        for oc in range(ND):
            pm = P_m[oc % 2]
            kb.wait(Tn, "s")
            kb.wait(Tn, "v")
            for kc in range(ND):
                kb.inc(Tn.matmul(pm[:], Wsb[:, kc * D + oc * 128: kc * D + (oc + 1) * 128],
                                 Ysb[:, kc * TOK + t * 512: kc * TOK + (t + 1) * 512],
                                 start=(kc == 0), stop=(kc == ND - 1)), "t")
            kb.wait(S, "t")
            kb.wait(S, "v")
            kb.inc(S.activation(zb[:, oc * 512:(oc + 1) * 512], pm[:], AF.Identity,
                                bias=bpsb[:, oc:oc + 1]), "s")
        kb.wait(SY, "v")
        xnsrc = xnd.rearrange("(kc p) tt -> p kc tt", p=128)[:, :, t * 512:(t + 1) * 512]
        kb.inc(SY.dma_start(out=xnsb[:].rearrange("p (kc tt) -> p kc tt", tt=512), in_=xnsrc), "ld", 16)
        xn_mark = kb.n("ld")
        kb.wait(V, "ld", xn_mark)
        for kc in range(ND):
            kb.wait(V, "s")
            kb.wait(V, "st")
            xnap = xnsb[:, kc * 512:(kc + 1) * 512]
            kb.inc(V.tensor_add(xm_f[:, kc * 512:(kc + 1) * 512], xnap, zb[:, kc * 512:(kc + 1) * 512]), "v")
        kb.wait(G, "v")
        dstx = xmo.rearrange("(kc p) tt -> p kc tt", p=128)[:, :, t * 512:(t + 1) * 512]
        kb.inc(G.dma_start(out=dstx, in_=xm_f[:].rearrange("p (kc tt) -> p kc tt", tt=512)), "st", 16)
        xmt = lambda kc: xm_f[:, kc * 512:(kc + 1) * 512]
        emit_stats(kb, w, xmt, c, P_a[0], P_a[1], rms=True)
        emit_bcast(kb, c, w.rsb, P_y)
        for kc in range(ND):
            kb.wait(V, "t")
            kb.wait(V, "st")
            kb.inc(V.tensor_mul(w.t1[:], xmt(kc), P_y[:]), "v")
            kb.inc(V.tensor_scalar_mul(h2b[:, kc * 512:(kc + 1) * 512],
                                       w.t1[:], fnwsb[:, kc:kc + 1]), "v")
        kb.wait(SY, "v")
        dsth = h2o.rearrange("(kc p) tt -> p kc tt", p=128)[:, :, t * 512:(t + 1) * 512]
        kb.inc(SY.dma_start(out=dsth, in_=h2b[:].rearrange("p (kc tt) -> p kc tt", tt=512)), "st", 16)
        kb.wait(Tn, "v")
        for kc in range(ND):
            kb.inc(Tn.matmul(P_x[0:32, :], gwsb[:, kc * 32:(kc + 1) * 32],
                             h2b[:, kc * 512:(kc + 1) * 512],
                             start=(kc == 0), stop=(kc == ND - 1)), "t")
        kb.wait(S, "t")
        kb.inc(S.activation(glf[:], P_x[0:32, :], AF.Identity, bias=gbsb[:, 0:1]), "s")
        for p4 in range(4):
            kb.wait(Tn, "s")
            kb.wait(Tn, "v")
            kb.inc(Tn.transpose(P_m[0][:, 0:32], glf[0:32, 128 * p4:128 * (p4 + 1)], I32f[:]), "t")
            kb.wait(V, "t")
            kb.incw(V, V.tensor_copy(gtm[:], P_m[0][:, 0:32]), "v")
            kb.incw(V, V.tensor_copy(cur[:], gtm[:]), "v")
            for r in range(4):
                kb.incw(V, V.reduce_max(mrow[:, r:r + 1], cur[:], axis=AX.X), "v")
                if r < 3:
                    kb.incw(V, V.tensor_scalar(eqb[:], cur[:], mrow[:, r:r + 1], None, ALU.is_ge), "v")
                    kb.incw(V, V.scalar_tensor_tensor(cur[:], in0=eqb[:], scalar=-1e30, in1=cur[:],
                                                      op0=ALU.mult, op1=ALU.add), "v")
            kb.incw(V, V.tensor_scalar_mul(nm0[:], mrow[:, 0:1], -1.0), "v")
            kb.wait(S, "v")
            kb.inc(S.activation(et[:], gtm[:], AF.Exp, bias=nm0[:]), "s")
            kb.incw(V, V.tensor_scalar(sel[:], gtm[:], mrow[:, 3:4], None, ALU.is_ge), "v")
            kb.wait(V, "s")
            kb.incw(V, V.tensor_mul(wgt[:], et[:], sel[:]), "v")
            kb.incw(V, V.reduce_sum(mrow[:, 4:5], wgt[:], axis=AX.X), "v")
            kb.incw(V, V.reciprocal(rd_[:], mrow[:, 4:5]), "v")
            kb.incw(V, V.tensor_scalar_mul(cetm[:], wgt[:], rd_[:]), "v")
            kb.wait(Tn, "v")
            kb.inc(Tn.transpose(P_t32[0:32, 0:128], cetm[:], I128f[:]), "t")
            kb.wait(V, "t")
            kb.inc(V.tensor_copy(cef[0:32, 128 * p4: 128 * (p4 + 1)],
                                 P_t32[0:32, 0:128]), "v")
        kb.wait(SY, "v")
        kb.inc(SY.dma_start(out=ceo[:, t * 512:(t + 1) * 512], in_=cef[0:32, :]), "st", 16)


def emit_B3(kb, c, h2d, ced, xmd, xd, Wud, Wdd, xo):
    kb.phase("b3")
    h2sb = kb.sb("h2sb", [128, ND * TOK], BF16)
    cefb = kb.sb("cefb", [32, TOK], BF16)
    Ebig = kb.sb("Ebig", [32, 32 * 128], BF16)
    acc = kb.sb("acc", [128, ND * TOK], F32)
    wes = ExitStack()
    _wsb = lambda n, sh, dt: wes.enter_context(kb.nc.sbuf_tensor("b3_" + n, sh, dt))
    cb_all = _wsb("cb_all", [128, 4 * 512], F32)
    sg_sb = [_wsb(f"sg_sb{i}", [128, 512], F32) for i in range(2)]
    mdf = [_wsb(f"mdf{i}", [128, 512], F32) for i in range(2)]
    mdb = _wsb("mdb", [128, 4 * ND * 512], BF16)
    Wusb = [_wsb(f"Wusb{i}", [128, ND * 1024], BF16) for i in range(2)]
    Wdsb = _wsb("Wdsb", [128, ND * D], BF16)

    P_u = [kb.ps(f"P_u{i}", [128, 512]) for i in range(2)]
    P_g = [kb.ps(f"P_g{i}", [128, 512]) for i in range(2)]
    P_d = [kb.ps(f"P_d{i}", [128, 512]) for i in range(2)]
    P_c = kb.ps("P_c", [128, 512])

    SY, V, S, Tn, G = kb.SY, kb.V, kb.S, kb.T, kb.G
    gd0 = kb.n("gd")
    kb.inc(G.memset(Ebig[:], 1.0), "g")
    kb.inc(G.affine_select(out=Ebig[:].rearrange("p (e cc) -> p e cc", cc=128),
                           in_=Ebig[:].rearrange("p (e cc) -> p e cc", cc=128),
                           compare_op=ALU.is_equal, fill=0.0, base=0,
                           pattern=[[-1, 32], [0, 128]], channel_multiplier=1), "g")
    load_x_fm(kb, h2sb, h2d)
    load_x_fm(kb, acc, xmd, eng=G, sem="gd")
    kb.inc(G.dma_start(out=cefb[:], in_=ced), "gd", 16)

    wu_ready = {}
    wu_free = {}
    wd_ready = {}
    wd_free = 0

    def prefetch_wu(unit):
        e, ha = divmod(unit, 2)
        if e >= 32:
            return
        par = unit % 2
        kb.wait(G, "t", wu_free.get(par, 0))
        wsl = Wud[e * D:(e + 1) * D, :].rearrange("(kc p) cc -> p kc cc", p=128)
        dstu = Wusb[par][:].rearrange("p (kc two cc) -> p kc two cc", two=2, cc=512)
        kb.inc(G.dma_start(out=dstu[:, :, 0, :], in_=wsl[:, :, 512 * ha:512 * (ha + 1)]), "gd", 16)
        kb.inc(G.dma_start(out=dstu[:, :, 1, :], in_=wsl[:, :, 1024 + 512 * ha:1024 + 512 * (ha + 1)]), "gd", 16)
        wu_ready[unit] = kb.n("gd")

    def prefetch_wd(e):
        if e >= 32:
            return
        kb.wait(G, "t", wd_free)
        wsl = Wdd[e * D:(e + 1) * D, :].rearrange("(kc p) cc -> p kc cc", p=128)
        kb.inc(G.dma_start(out=Wdsb[:].rearrange("p (kc cc) -> p kc cc", cc=D), in_=wsl), "gd", 16)
        wd_ready[e] = kb.n("gd")

    prefetch_wu(0)
    prefetch_wu(1)
    prefetch_wd(0)
    pu_last = [0, 0]
    pg_last = [0, 0]
    pd_last = [0, 0]
    for e in range(32):
        kb.wait(Tn, "g")
        kb.wait(Tn, "gd", gd0 + 32)
        kb.wait(S, "v")
        for t in range(NT):
            kb.wait(Tn, "s")
            kb.inc(Tn.matmul(P_c[:], Ebig[:, e * 128:(e + 1) * 128],
                             cefb[0:32, t * 512:(t + 1) * 512], start=True, stop=True), "t")
            kb.wait(S, "t")
            kb.inc(S.activation(cb_all[:, t * 512:(t + 1) * 512], P_c[:], AF.Identity), "s")
        for ha in range(2):
            unit = 2 * e + ha
            par = unit % 2
            kb.wait(Tn, "gd", wu_ready[unit])
            for t in range(NT):
                h2t = lambda kc: h2sb[:, kc * TOK + t * 512: kc * TOK + (t + 1) * 512]
                for jj in range(4):
                    j = 4 * ha + jj
                    jpar = jj % 2
                    kb.wait(Tn, "v", pu_last[jpar])
                    for kc in range(ND):
                        kb.inc(Tn.matmul(P_u[jpar][:], Wusb[par][:, kc * 1024 + jj * 128: kc * 1024 + (jj + 1) * 128],
                                         h2t(kc), start=(kc == 0), stop=(kc == ND - 1)), "t")
                    kb.wait(Tn, "s", pg_last[jpar])
                    for kc in range(ND):
                        kb.inc(Tn.matmul(P_g[jpar][:], Wusb[par][:, kc * 1024 + 512 + jj * 128: kc * 1024 + 512 + (jj + 1) * 128],
                                         h2t(kc), start=(kc == 0), stop=(kc == ND - 1)), "t")
                    g_mark = kb.n("t")
                    kb.wait(S, "t", g_mark)
                    kb.wait(S, "v", pu_last[jpar])
                    kb.inc(S.activation(sg_sb[jpar][:], P_g[jpar][:], AF.Silu), "s")
                    pg_last[jpar] = kb.n("s")
                    kb.wait(V, "s", pg_last[jpar])
                    kb.inc(V.tensor_mul(mdf[jpar][:], sg_sb[jpar][:], P_u[jpar][:]), "v")
                    kb.inc(V.tensor_mul(mdb[:, (t * ND + j) * 512:(t * ND + j + 1) * 512],
                                        mdf[jpar][:], cb_all[:, t * 512:(t + 1) * 512]), "v")
                    pu_last[jpar] = kb.n("v")
            wu_free[par] = kb.n("t")
            if unit + 2 <= 63:
                prefetch_wu(unit + 2)
        kb.wait(Tn, "gd", wd_ready[e])
        kb.wait(Tn, "v")
        kb.wait(V, "gd", gd0 + 32)
        for t in range(NT):
            for oc in range(ND):
                pdp = oc % 2
                pd = P_d[pdp]
                kb.wait(Tn, "v", pd_last[pdp])
                for j in range(ND):
                    kb.inc(Tn.matmul(pd[:], Wdsb[:, j * D + oc * 128: j * D + (oc + 1) * 128],
                                     mdb[:, (t * ND + j) * 512:(t * ND + j + 1) * 512],
                                     start=(j == 0), stop=(j == ND - 1)), "t")
                kb.wait(V, "t")
                aap = acc[:, oc * TOK + t * 512: oc * TOK + (t + 1) * 512]
                kb.inc(V.tensor_add(aap, aap, pd[:]), "v")
                pd_last[pdp] = kb.n("v")
        wd_free = kb.n("t")
        prefetch_wd(e + 1)
    wes.close()
    kb.wait(SY, "t"); kb.wait(SY, "v"); kb.wait(SY, "s")
    xsb = kb.sb("xsb3", [128, ND * TOK], F32)
    w = LNW(kb, "ln3")
    load_x_fm(kb, xsb, xd)
    kb.wait(V, "ld")
    P_b1 = P_u[0]
    P_b2 = P_u[1]
    for t in range(NT):
        at = lambda kc: acc[:, kc * TOK + t * 512: kc * TOK + (t + 1) * 512]
        emit_stats(kb, w, at, c, P_g[0][0:1, :], P_g[1][0:1, :])
        emit_bcast(kb, c, w.nmb, P_b1)
        emit_bcast(kb, c, w.rsb, P_b2)
        for kc in range(ND):
            kb.wait(V, "t")
            xap = xsb[:, kc * TOK + t * 512: kc * TOK + (t + 1) * 512]
            kb.inc(V.tensor_add(w.t1[:], at(kc), P_b1[:]), "v")
            kb.inc(V.tensor_mul(w.t1[:], w.t1[:], P_b2[:]), "v")
            kb.inc(V.tensor_add(xap, xap, w.t1[:]), "v")
        kb.wait(SY, "v")
        dst = xo.rearrange("(kc p) tt -> p kc tt", p=128)[:, :, t * 512:(t + 1) * 512]
        src2 = xsb[:].rearrange("p (kc tt) -> p kc tt", tt=TOK)[:, :, t * 512:(t + 1) * 512]
        kb.inc(SY.dma_start(out=dst, in_=src2), "st", 16)


def emit_C1(kb, c, x, Wq, bq, w1, b1, xno, X4i):
    kb.phase("c1")
    xsb = kb.sb("xsb", [128, ND * TOK], F32)
    Wsb = kb.sb("Wsb", [128, ND * 3 * D], BF16)
    bqsb = kb.sb("bqsb", [128, 24], F32)
    w1sb = kb.sb("w1sb", [128, 8], F32)
    b1sb = kb.sb("b1sb", [128, 8], F32)
    xnf = kb.sb("xnf", [128, ND * 512], F32)
    xlb = kb.sb("xlb", [128, ND * 512], BF16)
    qtmp = kb.sb("qtmp", [128, 512], BF16)
    stg = [kb.sb(f"stg{i}", [128, 512], BF16) for i in range(2)]
    vt = [kb.sb(f"vt{i}", [128, 4 * D], BF16) for i in range(2)]
    w = LNW(kb, "ln")

    P_s = kb.ps("P_s", [1, 512])
    P_q = kb.ps("P_q", [1, 512])
    P_b1 = kb.ps("P_b1", [128, 512])
    P_b2 = kb.ps("P_b2", [128, 512])
    P_m = [kb.ps(f"P_m{i}", [128, 512]) for i in range(2)]
    P_t = kb.ps("P_t", [128, 64], BF16)

    SY, V, S, Tn, G = kb.SY, kb.V, kb.S, kb.T, kb.G
    load_x_fm(kb, xsb, x)
    for kc in range(ND):
        kb.inc(G.dma_start(out=Wsb[:, kc * 3 * D:(kc + 1) * 3 * D],
                           in_=Wq[kc * 128:(kc + 1) * 128, :]), "gd", 16)
    kb.inc(SY.dma_start(out=bqsb[:], in_=bq), "ld", 16)
    kb.inc(SY.dma_start(out=w1sb[:], in_=w1), "ld", 16)
    kb.inc(SY.dma_start(out=b1sb[:], in_=b1), "ld", 16)
    kb.wait(V, "ld"); kb.wait(S, "ld")

    st_marks = {}
    for t in range(NT):
        xt = lambda kc: xsb[:, kc * TOK + t * 512: kc * TOK + (t + 1) * 512]
        emit_stats(kb, w, xt, c, P_s, P_q)
        emit_bcast(kb, c, w.nmb, P_b1)
        emit_bcast(kb, c, w.rsb, P_b2)
        for kc in range(ND):
            kb.wait(V, "t")
            kb.inc(V.tensor_add(w.t1[:], xt(kc), P_b1[:]), "v")
            kb.inc(V.tensor_mul(xnf[:, kc * 512:(kc + 1) * 512], w.t1[:], P_b2[:]), "v")
            kb.inc(V.tensor_copy(xlb[:, kc * 512:(kc + 1) * 512], xnf[:, kc * 512:(kc + 1) * 512]), "v")
        kb.wait(SY, "v")
        dst = xno.rearrange("(kc p) tt -> p kc tt", p=128)[:, :, t * 512:(t + 1) * 512]
        kb.inc(SY.dma_start(out=dst, in_=xlb[:].rearrange("p (kc tt) -> p kc tt", tt=512)), "st", 16)
        xn_st = kb.n("st")
        xnt = lambda kc: xnf[:, kc * 512:(kc + 1) * 512]
        emit_stats(kb, w, xnt, c, P_s, P_q)
        emit_bcast(kb, c, w.nmb, P_b1)
        emit_bcast(kb, c, w.rsb, P_b2)
        kb.wait(V, "st", xn_st)
        for kc in range(ND):
            kb.wait(V, "t")
            kb.inc(V.tensor_add(w.t1[:], xnt(kc), P_b1[:]), "v")
            kb.inc(V.tensor_mul(w.t1[:], w.t1[:], P_b2[:]), "v")
            kb.inc(V.tensor_scalar(xlb[:, kc * 512:(kc + 1) * 512], w.t1[:],
                                   w1sb[:, kc:kc + 1], b1sb[:, kc:kc + 1], ALU.mult, ALU.add), "v")
        kb.wait(Tn, "v")
        kb.wait(Tn, "gd")
        for oc in range(24):
            pm = P_m[oc % 2]
            kb.wait(Tn, "s")
            for kc in range(ND):
                kb.inc(Tn.matmul(pm[:], Wsb[:, kc * 3 * D + oc * 128: kc * 3 * D + (oc + 1) * 128],
                                 xlb[:, kc * 512:(kc + 1) * 512],
                                 start=(kc == 0), stop=(kc == ND - 1)), "t")
            kb.wait(S, "t")
            kb.inc(S.activation(qtmp[:], pm[:], AF.Identity, bias=bqsb[:, oc:oc + 1]), "s")
            if oc < 16:
                sg = stg[oc % 2]
                if id(sg) in st_marks:
                    kb.wait(V, "st", st_marks[id(sg)])
                kb.wait(V, "s")
                kb.inc(V.tensor_copy(sg[:], qtmp[:]), "v")
                kb.wait(SY, "v")
                k = 0 if oc < 8 else 1
                j = oc % 8
                s = 4 * (t // 2) + j // 2
                slot = qk_slot(X4i, s, k)
                kb.inc(SY.dma_start(out=slot[128 * (j % 2):128 * (j % 2) + 128,
                                             512 * (t % 2):512 * (t % 2) + 512], in_=sg[:]), "st", 16)
                st_marks[id(sg)] = kb.n("st")
            else:
                vc = oc - 16
                vbuf = vt[t % 2]
                if (vc == 0) and (id(vbuf) in st_marks):
                    kb.wait(V, "st", st_marks[id(vbuf)])
                for hh in range(2):
                    for p4 in range(4):
                        kb.wait(Tn, "s")
                        kb.wait(Tn, "v")
                        kb.inc(Tn.transpose(P_t[:], qtmp[64 * hh:64 * hh + 64, 128 * p4:128 * (p4 + 1)],
                                            c["idn2"][64 * hh:64 * hh + 64, :]), "t")
                        kb.wait(V, "t")
                        col = 1024 * p4 + 64 * (2 * vc + hh)
                        kb.inc(V.tensor_copy(vbuf[:, col:col + 64], P_t[:]), "v")
                if vc == 7:
                    kb.wait(SY, "v")
                    for p4 in range(4):
                        for q in range(4):
                            s = 4 * (t // 2) + q
                            r0 = 512 * (t % 2) + 128 * p4
                            kb.inc(SY.dma_start(out=v_slot(X4i, s)[r0:r0 + 128, :],
                                                in_=vbuf[:, 1024 * p4 + 256 * q: 1024 * p4 + 256 * q + 256]),
                                   "st", 16)
                    st_marks[id(vbuf)] = kb.n("st")


def emit_C3(kb, c, x, X5o, xnd, Wo, bo, w2, b2, xo, xn3o):
    kb.phase("c3")
    xsb = kb.sb("xsb", [128, ND * TOK], F32)
    ysb = kb.sb("ysb", [128, ND * TOK], BF16)
    xnsb = kb.sb("xnsb", [128, ND * TOK], BF16)
    Wsb = kb.sb("Wsb", [128, ND * D], BF16)
    bosb = kb.sb("bosb", [128, 8], F32)
    w2sb = kb.sb("w2sb", [128, 8], F32)
    b2sb = kb.sb("b2sb", [128, 8], F32)
    zb = kb.sb("zb", [128, ND * 512], BF16)
    rm_f = kb.sb("rm_f", [128, ND * 512], F32)
    rp_f = kb.sb("rp_f", [128, ND * 512], F32)
    w = LNW(kb, "ln")

    P_s = kb.ps("P_s", [1, 512])
    P_q = kb.ps("P_q", [1, 512])
    P_b1 = kb.ps("P_b1", [128, 512])
    P_b2 = kb.ps("P_b2", [128, 512])
    P_m = [kb.ps(f"P_m{i}", [128, 512]) for i in range(2)]

    SY, V, S, Tn, G = kb.SY, kb.V, kb.S, kb.T, kb.G
    load_x_fm(kb, xsb, x)
    load_x_fm(kb, xnsb, xnd)
    kb.wait(SY, "cc")
    for r in range(8):
        ys = y_slot(X5o, r)
        for half in range(2):
            kc = 2 * (r % 4) + half
            kb.inc(SY.dma_start(out=ysb[:, kc * TOK + 1024 * (r // 4): kc * TOK + 1024 * (r // 4) + 1024],
                                in_=ys[128 * half:128 * half + 128, :]), "ld", 16)
    for kc in range(ND):
        kb.inc(G.dma_start(out=Wsb[:, kc * D:(kc + 1) * D], in_=Wo[kc * 128:(kc + 1) * 128, :]), "gd", 16)
    kb.inc(SY.dma_start(out=bosb[:], in_=bo), "ld", 16)
    kb.inc(SY.dma_start(out=w2sb[:], in_=w2), "ld", 16)
    kb.inc(SY.dma_start(out=b2sb[:], in_=b2), "ld", 16)
    kb.wait(Tn, "ld"); kb.wait(Tn, "gd"); kb.wait(S, "ld"); kb.wait(V, "ld")

    for t in range(NT):
        for oc in range(ND):
            pm = P_m[oc % 2]
            kb.wait(Tn, "s")
            for kc in range(ND):
                kb.inc(Tn.matmul(pm[:], Wsb[:, kc * D + oc * 128: kc * D + (oc + 1) * 128],
                                 ysb[:, kc * TOK + t * 512: kc * TOK + (t + 1) * 512],
                                 start=(kc == 0), stop=(kc == ND - 1)), "t")
            kb.wait(S, "t")
            kb.wait(S, "v")
            kb.inc(S.activation(zb[:, oc * 512:(oc + 1) * 512], pm[:], AF.Identity,
                                bias=bosb[:, oc:oc + 1]), "s")
        for kc in range(ND):
            kb.wait(V, "s")
            xnap = xnsb[:, kc * TOK + t * 512: kc * TOK + (t + 1) * 512]
            kb.inc(V.tensor_add(rm_f[:, kc * 512:(kc + 1) * 512], xnap, zb[:, kc * 512:(kc + 1) * 512]), "v")
        rmt = lambda kc: rm_f[:, kc * 512:(kc + 1) * 512]
        emit_stats(kb, w, rmt, c, P_s, P_q)
        emit_bcast(kb, c, w.nmb, P_b1)
        emit_bcast(kb, c, w.rsb, P_b2)
        for kc in range(ND):
            kb.wait(V, "t")
            kb.inc(V.tensor_add(w.t1[:], rmt(kc), P_b1[:]), "v")
            kb.inc(V.tensor_mul(w.t1[:], w.t1[:], P_b2[:]), "v")
            kb.inc(V.tensor_scalar(w.t1[:], w.t1[:], w2sb[:, kc:kc + 1], b2sb[:, kc:kc + 1],
                                   ALU.mult, ALU.add), "v")
            kb.inc(V.tensor_add(rp_f[:, kc * 512:(kc + 1) * 512], rmt(kc), w.t1[:]), "v")
        rpt = lambda kc: rp_f[:, kc * 512:(kc + 1) * 512]
        emit_stats(kb, w, rpt, c, P_s, P_q)
        emit_bcast(kb, c, w.nmb, P_b1)
        emit_bcast(kb, c, w.rsb, P_b2)
        for kc in range(ND):
            kb.wait(V, "t")
            xap = xsb[:, kc * TOK + t * 512: kc * TOK + (t + 1) * 512]
            kb.inc(V.tensor_add(w.t1[:], rpt(kc), P_b1[:]), "v")
            kb.inc(V.tensor_mul(w.t1[:], w.t1[:], P_b2[:]), "v")
            kb.inc(V.tensor_add(xap, xap, w.t1[:]), "v")
        kb.wait(SY, "v")
        dst = xo.rearrange("(kc p) tt -> p kc tt", p=128)[:, :, t * 512:(t + 1) * 512]
        src = xsb[:].rearrange("p (kc tt) -> p kc tt", tt=TOK)[:, :, t * 512:(t + 1) * 512]
        kb.inc(SY.dma_start(out=dst, in_=src), "st", 16)
        xt3 = lambda kc: xsb[:, kc * TOK + t * 512: kc * TOK + (t + 1) * 512]
        emit_stats(kb, w, xt3, c, P_s, P_q)
        emit_bcast(kb, c, w.nmb, P_b1)
        emit_bcast(kb, c, w.rsb, P_b2)
        for kc in range(ND):
            kb.wait(V, "t")
            kb.inc(V.tensor_add(w.t1[:], xt3(kc), P_b1[:]), "v")
            kb.inc(V.tensor_mul(rp_f[:, kc * 512:(kc + 1) * 512], w.t1[:], P_b2[:]), "v")
        kb.wait(SY, "v")
        dst2 = xn3o.rearrange("(kc p) tt -> p kc tt", p=128)[:, :, t * 512:(t + 1) * 512]
        kb.inc(SY.dma_start(out=dst2, in_=rp_f[:].rearrange("p (kc tt) -> p kc tt", tt=512)), "st", 16)


def emit_D(kb, c, knum, h_in, qA, qB, srA, srB, bA, bB, laA, laB, lbA, lbB, gam, bet,
           final, out_ap, xd=None):
    kb.phase(f"d{knum}")
    hsb = kb.sb("hsb", [128, ND * TOK], F32)
    hbb = kb.sb("hbb", [128, ND * TOK], BF16)
    yA = kb.sb("yA", [128, ND * TOK], BF16)
    lasb = kb.sb("lasb", [128, ND * 64], BF16)
    lbsb = kb.sb("lbsb", [32, 2 * D], BF16)
    bAsb = kb.sb("bAsb", [128, 8], F32)
    bBsb = kb.sb("bBsb", [128, 8], F32)
    gamsb = kb.sb("gamsb", [128, 8], F32)
    betsb = kb.sb("betsb", [128, 8], F32)
    lsf = kb.sb("lsf", [32, D], F32)
    qf = [kb.sb(f"qf{i}", [128, 512], F32) for i in range(2)]
    srf = [kb.sb(f"srf{i}", [128, 512], F32) for i in range(2)]
    qaf = kb.sb("qaf", [128, 512], F32)
    wsbd = [kb.sb(f"wsb_d{i}", [128, ND * D], BF16) for i in range(2)]
    xrb = kb.sb("xrb", [32, 512], BF16)
    w = LNW(kb, "ln")

    P_s = kb.ps("P_s", [1, 512])
    P_q = kb.ps("P_q", [1, 512])
    P_b1 = kb.ps("P_b1", [128, 512])
    P_b2 = kb.ps("P_b2", [128, 512])
    P_m = [kb.ps(f"P_m{i}", [128, 512]) for i in range(2)]
    P_l = kb.ps("P_l", [32, 512])

    SY, V, S, Tn, G = kb.SY, kb.V, kb.S, kb.T, kb.G
    load_x_fm(kb, hsb, h_in)
    kb.inc(SY.dma_start(out=bAsb[:], in_=bA), "ld", 16)
    kb.inc(SY.dma_start(out=bBsb[:], in_=bB), "ld", 16)
    kb.inc(SY.dma_start(out=gamsb[:], in_=gam), "ld", 16)
    kb.inc(SY.dma_start(out=betsb[:], in_=bet), "ld", 16)
    for kc in range(ND):
        kb.inc(G.dma_start(out=lasb[:, kc * 32:(kc + 1) * 32], in_=laA[kc * 128:(kc + 1) * 128, :]), "gd", 16)
        kb.inc(G.dma_start(out=lasb[:, 256 + kc * 32:256 + (kc + 1) * 32],
                           in_=laB[kc * 128:(kc + 1) * 128, :]), "gd", 16)
    kb.inc(SY.dma_start(out=lsf[:], in_=lbA), "ld", 16)
    kb.wait(V, "ld")
    kb.inc(V.tensor_copy(lbsb[:, 0:D], lsf[:]), "v")
    lb_mark = kb.n("v")
    kb.wait(SY, "v", lb_mark)
    kb.inc(SY.dma_start(out=lsf[:], in_=lbB), "ld", 16)
    kb.wait(V, "ld")
    kb.inc(V.tensor_copy(lbsb[:, D:2 * D], lsf[:]), "v")
    kb.wait(V, "ld")
    for kc in range(ND):
        kb.inc(V.tensor_copy(hbb[:, kc * TOK:(kc + 1) * TOK], hsb[:, kc * TOK:(kc + 1) * TOK]), "v")

    deq_marks = [0, 0]

    def dequant(qd, srd, dst):
        marks_s = [deq_marks[0], deq_marks[0]]
        marks_v = [deq_marks[1], deq_marks[1]]
        for kk in range(2 * ND):
            kc, hc = divmod(kk, 2)
            par = kk % 2
            pqf = qf[par]
            psr = srf[par]
            kb.wait(G, "s", marks_s[par])
            kb.wait(G, "v", marks_v[par])
            kb.inc(G.dma_start(out=pqf[:], in_=qd[kc * 128:(kc + 1) * 128,
                                                  hc * 512:(hc + 1) * 512]), "gd", 16)
            kb.inc(G.dma_start(out=psr[:], in_=srd[kc * 128:(kc + 1) * 128,
                                                   hc * 512:(hc + 1) * 512]), "gd", 16)
            bias_c = kb.fconst(-1.0, 128)
            kb.wait(S, "gd")
            kb.wait(S, "v")
            kb.wait(S, "g")
            kb.inc(S.activation(qaf[:], pqf[:], AF.Identity, scale=1.0 / 7.5, bias=bias_c), "s")
            marks_s[par] = kb.n("s")
            kb.wait(V, "s")
            kb.inc(V.tensor_mul(dst[:, kc * D + hc * 512: kc * D + (hc + 1) * 512],
                                qaf[:], psr[:]), "v")
            marks_v[par] = kb.n("v")
        deq_marks[0] = kb.n("s")
        deq_marks[1] = kb.n("v")

    def qlora(src_sb, bias_sb, la_off, lb_off, out_writer, wsb):
        for t in range(NT):
            kb.wait(Tn, "v")
            kb.wait(Tn, "gd")
            kb.wait(Tn, "s")
            for kc in range(ND):
                kb.inc(Tn.matmul(P_l[:], lasb[:, la_off + kc * 32: la_off + (kc + 1) * 32],
                                 src_sb[:, kc * TOK + t * 512: kc * TOK + (t + 1) * 512],
                                 start=(kc == 0), stop=(kc == ND - 1)), "t")
            kb.wait(S, "t")
            kb.inc(S.activation(xrb[:], P_l[0:32, :], AF.Identity), "s")
            kb.wait(Tn, "s")
            for oc in range(ND):
                pm = P_m[oc % 2]
                kb.wait(Tn, "s")
                kb.wait(Tn, "v")
                kb.inc(Tn.matmul(pm[:], lbsb[0:32, lb_off + oc * 128: lb_off + (oc + 1) * 128],
                                 xrb[:], start=True, stop=False), "t")
                for kc in range(ND):
                    kb.inc(Tn.matmul(pm[:], wsb[:, kc * D + oc * 128: kc * D + (oc + 1) * 128],
                                     src_sb[:, kc * TOK + t * 512: kc * TOK + (t + 1) * 512],
                                     start=False, stop=(kc == ND - 1)), "t")
                out_writer(oc, t, pm, bias_sb)

    dequant(qA, srA, wsbd[0])
    def wA(oc, t, pm, bias_sb):
        kb.wait(S, "t")
        kb.inc(S.activation(yA[:, oc * TOK + t * 512: oc * TOK + (t + 1) * 512], pm[:],
                            AF.Identity, bias=bias_sb[:, oc:oc + 1]), "s")
    qlora(hbb, bAsb, 0, 0, wA, wsbd[0])
    gb_sb = yA
    for t in range(NT):
        yt = lambda kc: yA[:, kc * TOK + t * 512: kc * TOK + (t + 1) * 512]
        emit_stats(kb, w, yt, c, P_s, P_q, src_bf16=True)
        emit_bcast(kb, c, w.nmb, P_b1)
        emit_bcast(kb, c, w.rsb, P_b2)
        for kc in range(ND):
            kb.wait(V, "t")
            kb.wait(V, "s")
            kb.inc(V.tensor_add(w.t1[:], yt(kc), P_b1[:]), "v")
            kb.inc(V.tensor_mul(w.t1[:], w.t1[:], P_b2[:]), "v")
            kb.inc(V.tensor_scalar(w.t1[:], w.t1[:], gamsb[:, kc:kc + 1], betsb[:, kc:kc + 1],
                                   ALU.mult, ALU.add), "v")
            kb.wait(S, "v")
            kb.inc(S.activation(yA[:, kc * TOK + t * 512: kc * TOK + (t + 1) * 512],
                                w.t1[:], AF.Gelu), "s")
    dequant(qB, srB, wsbd[1])
    def wB(oc, t, pm, bias_sb):
        kb.wait(V, "t")
        hap = hsb[:, oc * TOK + t * 512: oc * TOK + (t + 1) * 512]
        kb.inc(V.tensor_scalar_add(w.t1[:], pm[:], bias_sb[:, oc:oc + 1]), "v")
        kb.inc(V.tensor_add(hap, hap, w.t1[:]), "v")
    qlora(gb_sb, bBsb, 256, D, wB, wsbd[1])

    if not final:
        kb.wait(SY, "v")
        dst = out_ap.rearrange("(kc p) tt -> p kc tt", p=128)
        kb.inc(SY.dma_start(out=dst, in_=hsb[:].rearrange("p (kc tt) -> p kc tt", tt=TOK)), "st", 16)
    else:
        xtile = [kb.sb(f"xtileD{i}", [128, 512], F32) for i in range(2)]
        xt_war = [0, 0]
        for t in range(NT):
            ht = lambda kc: hsb[:, kc * TOK + t * 512: kc * TOK + (t + 1) * 512]
            emit_stats(kb, w, ht, c, P_s, P_q)
            emit_bcast(kb, c, w.nmb, P_b1)
            emit_bcast(kb, c, w.rsb, P_b2)
            for kc in range(ND):
                xb_ = xtile[kc % 2]
                kb.wait(SY, "v", xt_war[kc % 2])
                kb.inc(SY.dma_start(out=xb_[:], in_=xd[kc * 128:(kc + 1) * 128,
                                                       t * 512:(t + 1) * 512]), "ld", 16)
                ld_mark = kb.n("ld")
                kb.wait(V, "t")
                kb.wait(V, "ld", ld_mark)
                kb.inc(V.tensor_add(w.t1[:], ht(kc), P_b1[:]), "v")
                kb.inc(V.tensor_mul(w.t1[:], w.t1[:], P_b2[:]), "v")
                kb.inc(V.tensor_add(ht(kc), xb_[:], w.t1[:]), "v")
                xt_war[kc % 2] = kb.n("v")
            kb.wait(SY, "v")
            dst = out_ap.rearrange("(kc p) tt -> p kc tt", p=128)[:, :, t * 512:(t + 1) * 512]
            hsrc = hsb[:].rearrange("p (kc tt) -> p kc tt", tt=TOK)[:, :, t * 512:(t + 1) * 512]
            kb.inc(SY.dma_start(out=dst, in_=hsrc), "st", 16)


DEBUG_TAPS = False


def build_mega():
    kb = KB()
    dtap = (lambda n, s, d: kb.dout(n, s, d)) if DEBUG_TAPS else (lambda n, s, d: kb.dint(n, s, d))
    x_in = kb.din("x", [D, TOK], F32)
    Wa = kb.din("Wa", [D, 3 * D], F32)
    ba = kb.din("ba", [128, 24], F32)
    Rq = kb.din("Rq", [128, D], BF16)
    Wp_a = kb.din("Wp_a", [D, D], F32)
    bp_a = kb.din("bp_a", [128, 8], F32)
    WqB = kb.din("WqB", [D, 1536], F32)
    bqB = kb.din("bqB", [128, 12], F32)
    anwB = kb.din("anwB", [128, 8], F32)
    ctabB = kb.din("ctabB", [128, TOK], BF16)
    stabB = kb.din("stabB", [128, TOK], BF16)
    PswB = kb.din("PswB", [128, 128], BF16)
    mselB = kb.din("mselB", [128, 8], F32)
    bmB = kb.din("bmB", [128, 7 * 512], BF16)
    snB = kb.din("snB", [1, 16], F32)
    WpB = kb.din("WpB", [D, D], F32)
    bpB = kb.din("bpB", [128, 8], F32)
    fnwB = kb.din("fnwB", [128, 8], F32)
    gwB = kb.din("gwB", [D, 32], F32)
    gbB = kb.din("gbB", [32, 1], F32)
    WuB = kb.din("WuB", [32 * D, 2048], F32)
    WdB = kb.din("WdB", [32 * D, D], F32)
    WqC = kb.din("WqC", [D, 3 * D], F32)
    bqC = kb.din("bqC", [128, 24], F32)
    w1C = kb.din("w1C", [128, 8], F32)
    b1C = kb.din("b1C", [128, 8], F32)
    WoC = kb.din("WoC", [D, D], F32)
    boC = kb.din("boC", [128, 8], F32)
    w2C = kb.din("w2C", [128, 8], F32)
    b2C = kb.din("b2C", [128, 8], F32)
    Dk = []
    for knum in range(3):
        Dk.append({nm: kb.din(f"{nm}{knum}",
                              [D, D] if nm in ("qA", "qB") else
                              ([D, D] if nm in ("srA", "srB") else
                               ([128, 8] if nm in ("bA", "bB", "gam", "bet") else
                                ([D, 32] if nm in ("laA", "laB") else [32, D]))),
                              I32 if nm in ("qA", "qB") else F32)
                   for nm in ("qA", "qB", "srA", "srB", "bA", "bB", "laA", "laB",
                              "lbA", "lbB", "gam", "bet")})
    xo_final = kb.dout("xo", [D, TOK], F32)

    X1i = kb.dint("X1i", [8, 3, SLOT], BF16)
    X1o = kb.dint("X1o", [8, 3, SLOT], BF16)
    X2i = kb.dint("X2i", [8, SLOT], BF16)
    X2o = kb.dint("X2o", [8, SLOT], BF16)
    X3i = kb.dint("X3i", [4, 32768], BF16)
    X3o = kb.dint("X3o", [8, 4, 32768], BF16)
    X4i = kb.dint("X4i", [8, 3, SLOT], BF16)
    X4o = kb.dint("X4o", [8, 3, SLOT], BF16)
    X5i = kb.dint("X5i", [8, SLOT], BF16)
    X5o = kb.dint("X5o", [8, SLOT], BF16)
    x1d = dtap("x1d", [D, TOK], F32)
    x2d = dtap("x2d", [D, TOK], F32)
    x3d = dtap("x3d", [D, TOK], F32)
    xn3d = dtap("xn3d", [D, TOK], F32)
    xnoB = dtap("xnoB", [D, TOK], BF16)
    QfB = dtap("QfB", [D, TOK], BF16)
    KoB = dtap("KoB", [256, TOK], BF16)
    VoB = dtap("VoB", [TOK, 256], BF16)
    xmB = dtap("xmB", [D, TOK], BF16)
    h2B = dtap("h2B", [D, TOK], BF16)
    ceB = dtap("ceB", [32, TOK], F32)
    xnoC = dtap("xnoC", [D, TOK], BF16)
    hD = [dtap(f"hD{i}", [D, TOK], F32) for i in range(2)]

    c = emit_consts(kb)
    kb.fconst(EPS)
    kb.fconst(-1.0, 128)

    emit_A1(kb, c, x_in, Wa, ba, Rq, X1i)
    kb.collective("AllToAll", X1i, X1o)
    emit_flash(kb, c, X1o, X2i, l1_mode=False, tag="f0")
    kb.collective("AllToAll", X2i, X2o)
    emit_PS(kb, c, x_in, X2o, Wp_a, bp_a, x1d)
    emit_B1(kb, c, x1d, WqB, bqB, anwB, ctabB, stabB, PswB, xnoB, QfB, KoB, VoB, X3i)
    kb.collective("AllGather", X3i, X3o)
    emit_B2(kb, c, QfB, KoB, VoB, X3o, mselB, bmB, snB, xnoB, WpB, bpB, fnwB, gwB, gbB,
            xmB, h2B, ceB)
    emit_B3(kb, c, h2B, ceB, xmB, x1d, WuB, WdB, x2d)
    emit_C1(kb, c, x2d, WqC, bqC, w1C, b1C, xnoC, X4i)
    kb.collective("AllToAll", X4i, X4o)
    emit_flash(kb, c, X4o, X5i, l1_mode=True, tag="f1")
    kb.collective("AllToAll", X5i, X5o)
    emit_C3(kb, c, x2d, X5o, xnoC, WoC, boC, w2C, b2C, x3d, xn3d)
    emit_D(kb, c, 0, xn3d, **{k: Dk[0][k] for k in Dk[0]}, final=False, out_ap=hD[0])
    emit_D(kb, c, 1, hD[0], **{k: Dk[1][k] for k in Dk[1]}, final=False, out_ap=hD[1])
    emit_D(kb, c, 2, hD[1], **{k: Dk[2][k] for k in Dk[2]}, final=True, out_ap=xo_final, xd=x3d)
    return kb.fin()


def rope_mats_l0():
    hd = 64
    theta = 1.0 / 10000.0 ** (np.arange(0, hd, 2, dtype=np.float64) / hd)
    R = np.zeros((128, D), np.float32)
    for ch in range(8):
        for hh in range(2):
            h = 2 * ch + hh
            base = 64 * hh
            for j in range(32):
                cth = np.cos(h * theta[j]); sth = np.sin(h * theta[j])
                R[base + 2 * j, ch * 128 + base + 2 * j] = cth
                R[base + 2 * j + 1, ch * 128 + base + 2 * j] = -sth
                R[base + 2 * j, ch * 128 + base + 2 * j + 1] = sth
                R[base + 2 * j + 1, ch * 128 + base + 2 * j + 1] = cth
    return R.astype(BF)


def l0f_rope_tables(positions):
    DH = 64
    pos = positions.astype(np.float64) / 32.0
    invf = 1.0 / (150000.0 ** (np.arange(0, DH, 2, dtype=np.float64) / DH))
    fr = pos[:, None] * invf[None, :]
    sn, cs = np.sin(fr), np.cos(fr)
    ct = np.zeros((64, len(positions)), np.float32)
    st = np.zeros((64, len(positions)), np.float32)
    for i in range(32):
        ct[2 * i] = cs[:, i]; ct[2 * i + 1] = cs[:, i]
        st[2 * i] = -sn[:, i]; st[2 * i + 1] = sn[:, i]
    return (np.ascontiguousarray(np.tile(ct, (2, 1))).astype(BF),
            np.ascontiguousarray(np.tile(st, (2, 1))).astype(BF))


def pair_swap_mat():
    P = np.zeros((128, 128), np.float32)
    for i in range(64):
        P[2 * i, 2 * i + 1] = 1.0
        P[2 * i + 1, 2 * i] = 1.0
    return P.astype(BF)


def bm_masks(core):
    bmm = np.zeros((128, 7 * 512), np.float32)
    ql = np.arange(512)[None, :]
    kl = np.arange(128)[:, None]
    for kt in range(5):
        dmm = ql - kl + 128 - 128 * kt
        bmm[:, kt * 512:(kt + 1) * 512] = ((dmm >= 0) & (dmm <= 128)).astype(np.float32)
    if core == 0:
        bmm[:, 5 * 512:] = 0.0
    else:
        bmm[:, 5 * 512:6 * 512] = bmm[:, 0:512]
        bmm[:, 6 * 512:7 * 512] = bmm[:, 0:512]
    return bmm.astype(BF)


_nc_cache = []


def kernel(**I):
    x = np.ascontiguousarray(I["L0_attn_x"].astype(np.float32).transpose(0, 2, 1))
    common = {}
    common["Wa"] = np.ascontiguousarray(I["L0_attn_W_attn"].astype(np.float32))
    common["ba"] = np.ascontiguousarray(I["L0_attn_b_attn"].astype(np.float32).reshape(24, 128).T)
    common["Rq"] = rope_mats_l0()
    common["Wp_a"] = np.ascontiguousarray(I["L0_attn_W_proj"].astype(np.float32))
    common["bp_a"] = np.ascontiguousarray(I["L0_attn_b_proj"].astype(np.float32).reshape(8, 128).T)
    common["WqB"] = np.ascontiguousarray(I["L0_ffn_qkv_w"].astype(np.float32))
    common["bqB"] = np.ascontiguousarray(I["L0_ffn_qkv_b"].astype(np.float32).reshape(12, 128).T)
    common["anwB"] = np.ascontiguousarray(I["L0_ffn_attn_norm_w"].astype(np.float32).reshape(8, 128).T)
    common["PswB"] = pair_swap_mat()
    common["snB"] = np.ascontiguousarray(I["L0_ffn_attn_sink"].astype(np.float32).reshape(1, 16))
    common["WpB"] = np.ascontiguousarray(I["L0_ffn_proj_w"].astype(np.float32))
    common["bpB"] = np.ascontiguousarray(I["L0_ffn_proj_b"].astype(np.float32).reshape(8, 128).T)
    common["fnwB"] = np.ascontiguousarray(I["L0_ffn_ffn_norm_w"].astype(np.float32).reshape(8, 128).T)
    common["gwB"] = np.ascontiguousarray(I["L0_ffn_gate_w"].astype(np.float32))
    common["gbB"] = np.ascontiguousarray(I["L0_ffn_gate_b"].astype(np.float32).reshape(32, 1))
    common["WuB"] = np.ascontiguousarray(I["L0_ffn_W_up"].astype(np.float32).reshape(32 * D, 2048))
    common["WdB"] = np.ascontiguousarray(I["L0_ffn_W_down"].astype(np.float32).reshape(32 * D, D))
    WQ = np.ascontiguousarray(I["L1_attn_W_Q"].astype(np.float32).transpose(1, 0, 2).reshape(D, D))
    WK = np.ascontiguousarray(I["L1_attn_W_K"].astype(np.float32).transpose(1, 0, 2).reshape(D, D))
    WV = np.ascontiguousarray(I["L1_attn_W_V"].astype(np.float32).transpose(1, 0, 2).reshape(D, D))
    common["WqC"] = np.ascontiguousarray(np.concatenate([WQ, WK, WV], axis=1))
    bqC = np.concatenate([I["L1_attn_b_Q"].reshape(-1), I["L1_attn_b_K"].reshape(-1),
                          I["L1_attn_b_V"].reshape(-1)]).astype(np.float32)
    common["bqC"] = np.ascontiguousarray(bqC.reshape(24, 128).T)
    common["w1C"] = np.ascontiguousarray(I["L1_attn_ln1_w"].astype(np.float32).reshape(8, 128).T)
    common["b1C"] = np.ascontiguousarray(I["L1_attn_ln1_b"].astype(np.float32).reshape(8, 128).T)
    common["WoC"] = np.ascontiguousarray(I["L1_attn_W_O"].astype(np.float32).reshape(D, D))
    common["boC"] = np.ascontiguousarray(I["L1_attn_b_O"].astype(np.float32).reshape(8, 128).T)
    common["w2C"] = np.ascontiguousarray(I["L1_attn_ln2_w"].astype(np.float32).reshape(8, 128).T)
    common["b2C"] = np.ascontiguousarray(I["L1_attn_ln2_b"].astype(np.float32).reshape(8, 128).T)
    for knum in range(3):
        i0, i1 = 2 * knum, 2 * knum + 1
        common[f"qA{knum}"] = np.ascontiguousarray(I["L1_ffn_qweights"][i0].astype(np.int32).T)
        common[f"qB{knum}"] = np.ascontiguousarray(I["L1_ffn_qweights"][i1].astype(np.int32).T)
        common[f"srA{knum}"] = np.ascontiguousarray(
            np.repeat(I["L1_ffn_scales"][i0].astype(np.float32), 16, axis=1).T)
        common[f"srB{knum}"] = np.ascontiguousarray(
            np.repeat(I["L1_ffn_scales"][i1].astype(np.float32), 16, axis=1).T)
        common[f"bA{knum}"] = np.ascontiguousarray(I["L1_ffn_biases"][i0].astype(np.float32).reshape(8, 128).T)
        common[f"bB{knum}"] = np.ascontiguousarray(I["L1_ffn_biases"][i1].astype(np.float32).reshape(8, 128).T)
        common[f"laA{knum}"] = np.ascontiguousarray(I["L1_ffn_lora_a"][i0].astype(np.float32).T)
        common[f"laB{knum}"] = np.ascontiguousarray(I["L1_ffn_lora_a"][i1].astype(np.float32).T)
        common[f"lbA{knum}"] = np.ascontiguousarray(I["L1_ffn_lora_b"][i0].astype(np.float32).T)
        common[f"lbB{knum}"] = np.ascontiguousarray(I["L1_ffn_lora_b"][i1].astype(np.float32).T)
        common[f"gam{knum}"] = np.ascontiguousarray(I["L1_ffn_ln_gamma"][knum].astype(np.float32).reshape(8, 128).T)
        common[f"bet{knum}"] = np.ascontiguousarray(I["L1_ffn_ln_beta"][knum].astype(np.float32).reshape(8, 128).T)

    in_maps = []
    for cc in range(NCORE):
        cols = slice(1024 * cc, 1024 * (cc + 1))
        xs = np.concatenate([x[0][:, cols], x[1][:, cols]], axis=1)
        pos = np.concatenate([np.arange(1024 * cc, 1024 * (cc + 1))] * 2)
        ct, st = l0f_rope_tables(pos)
        msel = np.zeros((128, 8), np.float32)
        if cc > 0:
            msel[:, cc - 1] = 1.0
        m = dict(common)
        m["x"] = np.ascontiguousarray(xs)
        m["ctabB"] = ct
        m["stabB"] = st
        m["mselB"] = msel
        m["bmB"] = bm_masks(cc)
        in_maps.append(m)

    if not _nc_cache:
        _nc_cache.append(build_mega())
    nc = _nc_cache[0]
    res = run_bass_kernel_spmd(nc, in_maps, core_ids=list(range(NCORE)))
    globals()["LAST"] = res
    out = np.zeros((B, D, T), np.float32)
    for cc in range(NCORE):
        r = res.results[cc]["xo"]
        out[0][:, 1024 * cc:1024 * (cc + 1)] = r[:, 0:1024]
        out[1][:, 1024 * cc:1024 * (cc + 1)] = r[:, 1024:2048]
    return np.ascontiguousarray(out.transpose(0, 2, 1))


# version 07 (final)
# speedup vs baseline: 1.0550x; 1.0504x over previous
import numpy as np
from contextlib import ExitStack
import concourse.bass as bass
import concourse.mybir as mybir
from concourse.bass_utils import run_bass_kernel_spmd
from concourse.aot_env import allow_ici_all_to_all
import ml_dtypes

F32 = mybir.dt.float32
BF16 = mybir.dt.bfloat16
I32 = mybir.dt.int32
AF = mybir.ActivationFunctionType
ALU = mybir.AluOpType
AX = mybir.AxisListType
BF = ml_dtypes.bfloat16

NCORE = 8
B = 2
T = 8192
D = 1024
TOK = 2048
TT = 512
NT = TOK // TT
ND = D // 128
EPS = 1e-5
SLOT = 262144
RG = [[0, 1, 2, 3, 4, 5, 6, 7]]


class KB:
    def __init__(self):
        self.nc = bass.Bass("TRN2", target_bir_lowering=False)
        nc = self.nc
        self.es = ExitStack()
        self.ps_es = self.es
        self.V, self.S, self.T, self.SY, self.G = nc.vector, nc.scalar, nc.tensor, nc.sync, nc.gpsimd
        self.counts = {}
        self.sems = {}
        self.waited = {}
        self.tag = "g"
        self.cur = self.es
        for s in ("ld", "st", "gd", "t", "v", "s", "g", "cc"):
            self.sems[s] = self.es.enter_context(nc.semaphore("sem_" + s))
            self.counts[s] = 0

    def inc(self, ins, name, k=1):
        self.counts[name] += k
        ins.then_inc(self.sems[name], k)
        return self.counts[name]

    def n(self, name):
        return self.counts[name]

    def incw(self, eng, ins, name, k=1):
        self.inc(ins, name, k)
        self.wait(eng, name)

    def wait(self, eng, name, val=None):
        if val is None:
            val = self.counts[name]
        if val <= 0:
            return
        key = (id(eng), name)
        if self.waited.get(key, 0) >= val:
            return
        eng.wait_ge(self.sems[name], val)
        self.waited[key] = val

    def sb(self, name, shape, dt):
        return self.cur.enter_context(self.nc.sbuf_tensor(self.tag + "_" + name, shape, dt))

    def ps(self, name, shape, dt=F32):
        return self.cur.enter_context(self.nc.psum_tensor(self.tag + "_" + name, shape, dt))

    def din(self, name, shape, dt):
        return self.nc.dram_tensor(name, shape, dt, kind="ExternalInput").ap()

    def dout(self, name, shape, dt):
        return self.nc.dram_tensor(name, shape, dt, kind="ExternalOutput").ap()

    def dint(self, name, shape, dt):
        return self.nc.dram_tensor(name, shape, dt).ap()

    def barrier(self):
        for eng in (self.SY, self.V, self.S, self.T, self.G):
            for s in self.counts:
                self.wait(eng, s)

    def phase(self, tag):
        self.barrier()
        if self.cur is not self.es:
            self.cur.close()
        self.cur = ExitStack()
        self.tag = tag

    def collective(self, kind, src_ap, dst_ap):
        G = self.G
        self.wait(G, "st")
        self.wait(G, "gd")
        if kind == "AllToAll":
            with allow_ici_all_to_all():
                ins = G.collective_compute(kind, mybir.AluOpType.bypass,
                                           replica_groups=RG, ins=[src_ap.opt()],
                                           outs=[dst_ap.opt()])
        else:
            ins = G.collective_compute(kind, mybir.AluOpType.bypass,
                                       replica_groups=RG, ins=[src_ap.opt()],
                                       outs=[dst_ap.opt()])
        self.inc(ins, "cc", 1)

    def fin(self):
        self.barrier()
        if self.cur is not self.es:
            self.cur.close()
            self.cur = self.es
        self.es.close()
        return self.nc

    _fc_n = 0
    def fconst(self, val, parts=1):
        key = ("fc", val, parts)
        if not hasattr(self, "_fcache"):
            self._fcache = {}
        if key not in self._fcache:
            KB._fc_n += 1
            sv = self.cur
            self.cur = self.es
            tns = self.sb(f"fc{KB._fc_n}_{parts}", [parts, 1], F32)
            self.cur = sv
            self.inc(self.G.memset(tns[:], float(val)), "g")
            self._fcache[key] = tns
        return self._fcache[key][:]


def qk_slot(X, s, k):
    return X[s:s + 1, k:k + 1, :].rearrange("a b (p c) -> (a b p) c", p=256)


def v_slot(X, s):
    return X[s:s + 1, 2:3, :].rearrange("a b (t f) -> (a b t) f", t=1024)


def y_slot(X, s):
    return X[s:s + 1, :].rearrange("a (p c) -> (a p) c", p=256)


def emit_consts(kb):
    c = {}
    sv = kb.cur
    kb.cur = kb.es
    c["ones_col"] = kb.sb("c_ones_col", [128, 1], BF16)
    c["ones_row"] = kb.sb("c_ones_row", [1, 128], BF16)
    c["idn2"] = kb.sb("c_idn2", [128, 64], BF16)
    kb.cur = sv
    G = kb.G
    kb.inc(G.memset(c["ones_col"][:], 1.0), "g")
    kb.inc(G.memset(c["ones_row"][:], 1.0), "g")
    kb.inc(G.memset(c["idn2"][0:64, :], 1.0), "g")
    kb.inc(G.affine_select(out=c["idn2"][0:64, :], in_=c["idn2"][0:64, :], compare_op=ALU.is_equal,
                           fill=0.0, base=0, pattern=[[-1, 64]], channel_multiplier=1), "g")
    kb.wait(kb.V, "g")
    kb.inc(kb.V.tensor_copy(c["idn2"][64:128, :], c["idn2"][0:64, :]), "v")
    return c


def emit_diag_masks(kb, with_anti):
    G = kb.G
    dm = kb.sb("dmask", [128, 4 * 512], BF16)
    kb.inc(G.memset(dm[:], 1.0), "g")
    for r in range(4):
        kb.inc(G.affine_select(out=dm[:, r * 512:(r + 1) * 512], in_=dm[:, r * 512:(r + 1) * 512],
                               compare_op=ALU.is_ge, fill=0.0, base=-128 * r,
                               pattern=[[1, 512]], channel_multiplier=-1), "g")
    am = None
    if with_anti:
        am = kb.sb("amask", [128, 4 * 512], BF16)
        kb.inc(G.memset(am[:], 1.0), "g")
        for r in range(4):
            kb.inc(G.affine_select(out=am[:, r * 512:(r + 1) * 512], in_=am[:, r * 512:(r + 1) * 512],
                                   compare_op=ALU.is_ge, fill=0.0, base=128 * r - 1,
                                   pattern=[[-1, 512]], channel_multiplier=1), "g")
    return dm, am


class LNW:
    def __init__(self, kb, tag):
        self.xb = [kb.sb(tag + f"_xb{i}", [128, 512], BF16) for i in range(2)]
        self.xsq = [kb.sb(tag + f"_xsq{i}", [128, 512], BF16) for i in range(2)]
        self.ms_mark = [0, 0]
        self.mq_mark = [0, 0]
        self.pread_mark = 0
        self.nm = kb.sb(tag + "_nm", [1, 512], F32)
        self.ex2 = kb.sb(tag + "_ex2", [1, 512], F32)
        self.m2 = kb.sb(tag + "_m2", [1, 512], F32)
        self.var = kb.sb(tag + "_var", [1, 512], F32)
        self.rs = kb.sb(tag + "_rs", [1, 512], F32)
        self.nmb = kb.sb(tag + "_nmb", [1, 512], BF16)
        self.rsb = kb.sb(tag + "_rsb", [1, 512], BF16)
        self.t1 = kb.sb(tag + "_t1", [128, 512], F32)


def emit_stats(kb, w, get_src, c, P_s, P_q, rms=False, src_bf16=False, ncols=512):
    V, S, Tn = kb.V, kb.S, kb.T
    kb.wait(Tn, "v", w.pread_mark)
    kb.wait(Tn, "s")
    kb.wait(S, "v")
    for kc in range(ND):
        src = get_src(kc)
        par = kc % 2
        if not rms:
            if src_bf16:
                xb = src
            else:
                kb.wait(V, "t", w.ms_mark[par])
                kb.inc(V.tensor_copy(w.xb[par][:, :ncols], src), "v")
                xb = w.xb[par][:, :ncols]
                kb.wait(Tn, "v")
            kb.wait(Tn, "g")
            kb.inc(Tn.matmul(P_s[0:1, :ncols], c["ones_col"][:], xb,
                             start=(kc == 0), stop=(kc == ND - 1)), "t")
            w.ms_mark[par] = kb.n("t")
        kb.wait(S, "t", w.mq_mark[par])
        kb.inc(S.activation(w.xsq[par][:, :ncols], src, AF.Square), "s")
        kb.wait(Tn, "s")
        kb.wait(Tn, "g")
        kb.inc(Tn.matmul(P_q[0:1, :ncols], c["ones_col"][:], w.xsq[par][:, :ncols],
                         start=(kc == 0), stop=(kc == ND - 1)), "t")
        w.mq_mark[par] = kb.n("t")
    kb.wait(V, "t")
    kb.inc(V.tensor_scalar_mul(w.ex2[:, :ncols], P_q[0:1, :ncols], 1.0 / D), "v")
    if rms:
        w.pread_mark = kb.n("v")
        kb.wait(S, "v")
        kb.wait(S, "g")
        kb.inc(S.activation(w.var[:, :ncols], w.ex2[:, :ncols], AF.Ln, bias=kb.fconst(EPS)), "s")
        kb.inc(S.activation(w.rs[:, :ncols], w.var[:, :ncols], AF.Exp, scale=-0.5), "s")
        kb.wait(V, "s")
        kb.inc(V.tensor_copy(w.rsb[:, :ncols], w.rs[:, :ncols]), "v")
    else:
        kb.inc(V.tensor_scalar_mul(w.nm[:, :ncols], P_s[0:1, :ncols], -1.0 / D), "v")
        w.pread_mark = kb.n("v")
        kb.inc(V.tensor_mul(w.m2[:, :ncols], w.nm[:, :ncols], w.nm[:, :ncols]), "v")
        kb.inc(V.tensor_sub(w.var[:, :ncols], w.ex2[:, :ncols], w.m2[:, :ncols]), "v")
        kb.wait(S, "v")
        kb.wait(S, "g")
        kb.inc(S.activation(w.m2[:, :ncols], w.var[:, :ncols], AF.Ln, bias=kb.fconst(EPS)), "s")
        kb.inc(S.activation(w.rs[:, :ncols], w.m2[:, :ncols], AF.Exp, scale=-0.5), "s")
        kb.wait(V, "s")
        kb.inc(V.tensor_copy(w.nmb[:, :ncols], w.nm[:, :ncols]), "v")
        kb.inc(V.tensor_copy(w.rsb[:, :ncols], w.rs[:, :ncols]), "v")


def emit_bcast(kb, c, row_b, P_dst, ncols=512):
    kb.wait(kb.T, "v")
    kb.wait(kb.T, "g")
    kb.inc(kb.T.matmul(P_dst[:, :ncols], c["ones_row"][:], row_b[0:1, :ncols],
                       start=True, stop=True), "t")


def load_x_fm(kb, dst_sb, src_dram, sem="ld", eng=None):
    ncols = src_dram.shape[1]
    src = src_dram.rearrange("(kc p) t -> p kc t", p=128)
    dst = dst_sb[:].rearrange("p (kc t) -> p kc t", t=ncols)
    kb.inc((eng or kb.SY).dma_start(out=dst, in_=src), sem, 16)


def emit_A1(kb, c, x, Wa, ba, Rq, X1i):
    kb.phase("a1")
    xsb = kb.sb("xsb", [128, ND * TOK], F32)
    Wsb = kb.sb("Wsb", [128, ND * 3 * D], BF16)
    basb = kb.sb("basb", [128, 24], F32)
    Rsb = kb.sb("Rsb", [128, D], BF16)
    xnb2 = [kb.sb(f"xnb{i}", [128, ND * 512], BF16) for i in range(2)]
    qtmp = kb.sb("qtmp", [128, 512], BF16)
    stg = [kb.sb(f"stg{i}", [128, 512], BF16) for i in range(2)]
    vt = [kb.sb(f"vt{i}", [128, 4 * D], BF16) for i in range(2)]
    w = LNW(kb, "ln")

    P_s = kb.ps("P_s", [1, 512])
    P_q = kb.ps("P_q", [1, 512])
    P_b1 = kb.ps("P_b1", [128, 512])
    P_b2 = kb.ps("P_b2", [128, 512])
    P_m = [kb.ps(f"P_m{i}", [128, 512]) for i in range(2)]
    P_r = kb.ps("P_r", [128, 512])
    P_t = kb.ps("P_t", [128, 64], BF16)

    SY, V, S, Tn, G = kb.SY, kb.V, kb.S, kb.T, kb.G
    load_x_fm(kb, xsb, x)
    kb.inc(SY.dma_start(out=basb[:], in_=ba), "ld", 16)
    kb.inc(SY.dma_start(out=Rsb[:], in_=Rq), "ld", 16)
    for kc in range(ND):
        kb.inc(G.dma_start(out=Wsb[:, kc * 3 * D:(kc + 1) * 3 * D],
                           in_=Wa[kc * 128:(kc + 1) * 128, :]), "gd", 16)
    kb.wait(V, "ld"); kb.wait(S, "ld")

    st_marks = {}
    proj_mark = [0, 0]
    for t in range(NT):
        tp = t % 2
        xnb = xnb2[tp]
        xt = lambda kc: xsb[:, kc * TOK + t * 512: kc * TOK + (t + 1) * 512]
        emit_stats(kb, w, xt, c, P_s, P_q)
        emit_bcast(kb, c, w.nmb, P_b1)
        emit_bcast(kb, c, w.rsb, P_b2)
        bc_mark = kb.n("t")
        kb.wait(V, "t", max(bc_mark, proj_mark[tp]))
        for kc in range(ND):
            kb.inc(V.tensor_add(w.t1[:], xt(kc), P_b1[:]), "v")
            kb.inc(V.tensor_mul(xnb[:, kc * 512:(kc + 1) * 512], w.t1[:], P_b2[:]), "v")
        kb.wait(Tn, "v")
        kb.wait(Tn, "gd")
        for oc in range(24):
            pm = P_m[oc % 2]
            kb.wait(Tn, "s")
            for kc in range(ND):
                kb.inc(Tn.matmul(pm[:], Wsb[:, kc * 3 * D + oc * 128: kc * 3 * D + (oc + 1) * 128],
                                 xnb[:, kc * 512:(kc + 1) * 512],
                                 start=(kc == 0), stop=(kc == ND - 1)), "t")
            kb.wait(S, "t")
            if oc < 16:
                kb.inc(S.activation(qtmp[:], pm[:], AF.Identity, bias=basb[:, oc:oc + 1]), "s")
                kb.wait(Tn, "s")
                kb.wait(Tn, "v")
                kb.inc(Tn.matmul(P_r[:], Rsb[:, (oc % 8) * 128:(oc % 8 + 1) * 128], qtmp[:],
                                 start=True, stop=True), "t")
                sg = stg[oc % 2]
                if id(sg) in st_marks:
                    kb.wait(V, "st", st_marks[id(sg)])
                kb.wait(V, "t")
                kb.inc(V.tensor_copy(sg[:], P_r[:]), "v")
                kb.wait(SY, "v")
                k = 0 if oc < 8 else 1
                j = oc % 8
                s = 4 * (t // 2) + j // 2
                slot = qk_slot(X1i, s, k)
                kb.inc(SY.dma_start(out=slot[128 * (j % 2):128 * (j % 2) + 128,
                                             512 * (t % 2):512 * (t % 2) + 512], in_=sg[:]), "st", 16)
                st_marks[id(sg)] = kb.n("st")
            else:
                vc = oc - 16
                vbuf = vt[t % 2]
                if (vc == 0) and (id(vbuf) in st_marks):
                    kb.wait(S, "st", st_marks[id(vbuf)])
                    kb.wait(V, "st", st_marks[id(vbuf)])
                kb.inc(S.activation(qtmp[:], pm[:], AF.Identity, bias=basb[:, oc:oc + 1]), "s")
                for hh in range(2):
                    for p4 in range(4):
                        kb.wait(Tn, "s")
                        kb.wait(Tn, "v")
                        kb.inc(Tn.transpose(P_t[:], qtmp[64 * hh:64 * hh + 64, 128 * p4:128 * (p4 + 1)],
                                            c["idn2"][64 * hh:64 * hh + 64, :]), "t")
                        kb.wait(V, "t")
                        col = 1024 * p4 + 64 * (2 * vc + hh)
                        kb.inc(V.tensor_copy(vbuf[:, col:col + 64], P_t[:]), "v")
                if vc == 7:
                    kb.wait(SY, "v")
                    for p4 in range(4):
                        for q in range(4):
                            s = 4 * (t // 2) + q
                            r0 = 512 * (t % 2) + 128 * p4
                            kb.inc(SY.dma_start(out=v_slot(X1i, s)[r0:r0 + 128, :],
                                                in_=vbuf[:, 1024 * p4 + 256 * q: 1024 * p4 + 256 * q + 256]),
                                   "st", 16)
                    st_marks[id(vbuf)] = kb.n("st")
        proj_mark[tp] = kb.n("t")


def emit_flash(kb, c, Xo, Yi, l1_mode, tag):
    kb.phase(tag)
    dm, am = emit_diag_masks(kb, with_anti=l1_mode)

    Qsb = kb.sb("Qsb", [128, 2 * T], BF16)
    Ksb = kb.sb("Ksb", [128, 2 * T], BF16)
    Vsb = [kb.sb(f"Vsb{i}", [128, 64 * 65], BF16) for i in range(4)]
    Ysb = kb.sb("Ysb", [128, 2 * T], BF16)
    esT = [kb.sb(f"esT{i}", [128, 4 * 512], BF16) for i in range(2)]
    Yu = kb.sb("Yu", [65, 16 * 512], BF16)
    drS = kb.sb("drS", [1, 16 * 512], BF16)
    bcs = kb.sb("bcs", [64, 512], F32)
    tsb = kb.sb("tsb", [65, 64], F32) if l1_mode else None
    tstb = kb.sb("tstb", [64, 65], F32) if l1_mode else None
    utri = kb.sb("utri", [64, 64], F32) if l1_mode else None
    sufsb = kb.sb("sufsb", [65, 64], F32) if l1_mode else None
    idn65 = kb.sb("idn65", [65, 65], F32) if l1_mode else None

    P_S = [kb.ps(f"P_S{i}", [128, 3 * 512]) for i in range(2)]
    P_y = kb.ps("P_y", [128, 512])
    P_bc = kb.ps("P_bc", [128, 512])
    P_ts = P_bc[0:65, 0:64]
    P_tt = P_y[0:64, 0:65]

    SY, V, S, Tn, G = kb.SY, kb.V, kb.S, kb.T, kb.G
    if l1_mode:
        kb.inc(G.memset(utri[:], 1.0), "g")
        kb.inc(G.affine_select(out=utri[:], in_=utri[:], compare_op=ALU.is_ge,
                               fill=0.0, base=0, pattern=[[-1, 64]], channel_multiplier=1), "g")
        kb.inc(G.memset(idn65[:], 1.0), "g")
        kb.inc(G.affine_select(out=idn65[:], in_=idn65[:], compare_op=ALU.is_equal,
                               fill=0.0, base=0, pattern=[[-1, 65]], channel_multiplier=1), "g")

    kb.wait(SY, "cc")
    for pr in range(2):
        off = 131072 * pr
        kb.inc(SY.dma_start(out=Qsb[0:128, pr * T:(pr + 1) * T].rearrange("p (s cc) -> p s cc", cc=1024),
                            in_=Xo[:, 0:1, off:off + 131072].rearrange("s b (p cc) -> p s (b cc)", p=128)),
               "ld", 16)
        kb.inc(SY.dma_start(out=Ksb[0:128, pr * T:(pr + 1) * T].rearrange("p (s cc) -> p s cc", cc=1024),
                            in_=Xo[:, 1:2, off:off + 131072].rearrange("s b (p cc) -> p s (b cc)", p=128)),
               "ld", 16)
    for hh in range(4):
        for s in range(8):
            kb.inc(SY.dma_start(
                out=Vsb[hh][:].rearrange("p (j cc) -> p j cc", cc=65)[:, 8 * s:8 * s + 8, 0:64],
                in_=v_slot(Xo, s)[:, 64 * hh:64 * hh + 64].rearrange("(j p) cc -> p j cc", p=128)),
                "ld", 16)
    kb.wait(V, "ld")
    for hh in range(4):
        onescol = Vsb[hh][:].rearrange("p (j cc) -> p j cc", cc=65)[:, :, 64:65]
        kb.inc(V.memset(onescol, 1.0), "v")

    yu_mark = 0
    exp_mark = [0, 0]
    av_mark = [0, 0]
    for hh in range(4):
        pr = hh // 2
        qrow = 64 * (hh % 2)
        kb.wait(Tn, "ld")
        kb.wait(Tn, "v")
        if l1_mode:
            for j in range(64):
                kb.inc(Tn.matmul(P_ts[:, j:j + 1], Vsb[hh][:, j * 65:(j + 1) * 65],
                                 c["ones_col"][:], start=(j == 0), stop=(j == 63)), "t")
            kb.wait(V, "t")
            kb.inc(V.tensor_copy(tsb[:], P_ts[:, :]), "v")
            kb.wait(Tn, "v")
            kb.inc(Tn.transpose(P_tt[:, :], tsb[:], idn65[:]), "t")
            kb.wait(V, "t")
            kb.inc(V.tensor_copy(tstb[:], P_tt[:, :]), "v")
            kb.wait(Tn, "v")
            kb.inc(Tn.matmul(P_ts[:, :], tstb[:], utri[:], start=True, stop=True), "t")
            kb.wait(V, "t")
            kb.inc(V.tensor_copy(sufsb[:], P_ts[:, :]), "v")
            kb.wait(Tn, "v")
        for qt in range(16):
            g0 = 512 * qt
            nkt = 4 * (qt + 1)
            qap = Qsb[qrow:qrow + 64, pr * T + g0: pr * T + g0 + 512]
            groups = []
            k0 = 0
            while k0 < nkt:
                gs = min(3, nkt - k0)
                groups.append((k0, gs))
                k0 += gs
            def emit_qk(gi):
                k0, gs = groups[gi]
                par = gi % 2
                ps = P_S[par]
                kb.wait(Tn, "s", exp_mark[par])
                for i4 in range(gs):
                    kt = k0 + i4
                    ins = Tn.matmul(ps[:, i4 * 512:(i4 + 1) * 512],
                                    Ksb[qrow:qrow + 64, pr * T + kt * 128: pr * T + (kt + 1) * 128],
                                    qap, start=True, stop=True)
                    if i4 == gs - 1:
                        kb.inc(ins, "t")
                qk_marks[gi] = kb.n("t")
            qk_marks = {}
            emit_qk(0)
            for gi, (k0, gs) in enumerate(groups):
                par = gi % 2
                eb = esT[par]
                ps = P_S[par]
                kb.wait(S, "t", qk_marks[gi])
                kb.inc(S.activation(eb[:, 0:gs * 512], ps[:, 0:gs * 512], AF.Exp, scale=0.125), "s")
                exp_mark[par] = kb.n("s")
                need_v = False
                for i4 in range(gs):
                    kt = k0 + i4
                    r = kt - (nkt - 4)
                    if r >= 0:
                        if not need_v:
                            kb.wait(V, "s", exp_mark[par])
                            kb.wait(V, "g")
                            need_v = True
                        kb.inc(V.tensor_mul(eb[:, i4 * 512:(i4 + 1) * 512],
                                            eb[:, i4 * 512:(i4 + 1) * 512],
                                            dm[:, r * 512:(r + 1) * 512]), "v")
                        if l1_mode:
                            kb.inc(V.tensor_add(eb[:, i4 * 512:(i4 + 1) * 512],
                                                eb[:, i4 * 512:(i4 + 1) * 512],
                                                am[:, r * 512:(r + 1) * 512]), "v")
                if gi + 1 < len(groups):
                    emit_qk(gi + 1)
                if need_v:
                    kb.wait(Tn, "v", kb.n("v"))
                else:
                    kb.wait(Tn, "s", exp_mark[par])
                if gi == 0:
                    kb.wait(Tn, "v", yu_mark)
                for i4 in range(gs):
                    kt = k0 + i4
                    ins = Tn.matmul(P_y[0:65, :], Vsb[hh][:, kt * 65:(kt + 1) * 65],
                                    eb[:, i4 * 512:(i4 + 1) * 512],
                                    start=(kt == 0), stop=(kt == nkt - 1))
                    if i4 == gs - 1:
                        kb.inc(ins, "t")
                av_mark[par] = kb.n("t")
            kb.wait(V, "t")
            ys = Yu[:, qt * 512:(qt + 1) * 512]
            if l1_mode:
                jq = (g0 + 512) // 128
                if jq < 64:
                    kb.inc(V.tensor_scalar(ys, P_y[0:65, :], sufsb[:, jq:jq + 1], None, ALU.add), "v")
                else:
                    kb.inc(V.tensor_copy(ys, P_y[0:65, :]), "v")
            else:
                kb.inc(V.tensor_copy(ys, P_y[0:65, :]), "v")
            yu_mark = kb.n("v")
        with kb.nc.allow_low_precision(reason="bf16 denominator reciprocal feeding bf16 bcast matmul"):
            for qt in range(16):
                kb.inc(V.reciprocal(drS[:, qt * 512:(qt + 1) * 512],
                                    Yu[64:65, qt * 512:(qt + 1) * 512]), "v")
        rec_mark = kb.n("v")
        bc_banks = [P_y[0:64, :], P_bc[0:64, :]]
        for qt in range(16):
            pb = bc_banks[qt % 2]
            kb.wait(Tn, "v", rec_mark)
            if qt >= 2:
                kb.wait(Tn, "v")
            kb.inc(Tn.matmul(pb, c["ones_row"][0:1, 0:64],
                             drS[0:1, qt * 512:(qt + 1) * 512], start=True, stop=True), "t")
            kb.wait(V, "t")
            kb.inc(V.tensor_copy(bcs[:], pb), "v")
            kb.inc(V.tensor_mul(Ysb[qrow:qrow + 64, pr * T + qt * 512: pr * T + (qt + 1) * 512],
                                Yu[0:64, qt * 512:(qt + 1) * 512], bcs[:]), "v")
        yu_mark = kb.n("v")
    kb.wait(SY, "v")
    for hh in range(4):
        for s in range(8):
            kb.inc(SY.dma_start(out=y_slot(Yi, s)[64 * hh:64 * hh + 64, :],
                                in_=Ysb[64 * (hh % 2):64 * (hh % 2) + 64,
                                        (hh // 2) * T + 1024 * s: (hh // 2) * T + 1024 * s + 1024]),
                   "st", 16)


def emit_PS(kb, c, x, X2o, Wp, bp, xo):
    kb.phase("ps")
    xsb = kb.sb("xsb", [128, ND * TOK], F32)
    ysb = kb.sb("ysb", [128, ND * TOK], BF16)
    Wsb = kb.sb("Wsb", [128, ND * D], BF16)
    bpsb = kb.sb("bpsb", [128, 8], F32)
    zb = kb.sb("zb", [128, ND * 512], BF16)
    w = LNW(kb, "ln")

    P_s = kb.ps("P_s", [1, 512])
    P_q = kb.ps("P_q", [1, 512])
    P_b1 = kb.ps("P_b1", [128, 512])
    P_b2 = kb.ps("P_b2", [128, 512])
    P_m = [kb.ps(f"P_m{i}", [128, 512]) for i in range(2)]

    SY, V, S, Tn, G = kb.SY, kb.V, kb.S, kb.T, kb.G
    load_x_fm(kb, xsb, x)
    kb.wait(SY, "cc")
    for r in range(8):
        ys = y_slot(X2o, r)
        for half in range(2):
            kc = 2 * (r % 4) + half
            kb.inc(SY.dma_start(out=ysb[:, kc * TOK + 1024 * (r // 4): kc * TOK + 1024 * (r // 4) + 1024],
                                in_=ys[128 * half:128 * half + 128, :]), "ld", 16)
    for kc in range(ND):
        kb.inc(G.dma_start(out=Wsb[:, kc * D:(kc + 1) * D], in_=Wp[kc * 128:(kc + 1) * 128, :]), "gd", 16)
    kb.inc(SY.dma_start(out=bpsb[:], in_=bp), "ld", 16)
    kb.wait(Tn, "ld"); kb.wait(Tn, "gd"); kb.wait(S, "ld"); kb.wait(V, "ld")

    for t in range(NT):
        for oc in range(ND):
            pm = P_m[oc % 2]
            kb.wait(Tn, "s")
            for kc in range(ND):
                kb.inc(Tn.matmul(pm[:], Wsb[:, kc * D + oc * 128: kc * D + (oc + 1) * 128],
                                 ysb[:, kc * TOK + t * 512: kc * TOK + (t + 1) * 512],
                                 start=(kc == 0), stop=(kc == ND - 1)), "t")
            kb.wait(S, "t")
            kb.wait(S, "v")
            kb.inc(S.activation(zb[:, oc * 512:(oc + 1) * 512], pm[:], AF.Identity,
                                bias=bpsb[:, oc:oc + 1]), "s")
        kb.wait(V, "s")
        zt = lambda kc: zb[:, kc * 512:(kc + 1) * 512]
        emit_stats(kb, w, zt, c, P_s, P_q, src_bf16=True)
        emit_bcast(kb, c, w.nmb, P_b1)
        emit_bcast(kb, c, w.rsb, P_b2)
        for kc in range(ND):
            kb.wait(V, "t")
            xap = xsb[:, kc * TOK + t * 512: kc * TOK + (t + 1) * 512]
            kb.inc(V.tensor_add(w.t1[:], zt(kc), P_b1[:]), "v")
            kb.inc(V.tensor_mul(w.t1[:], w.t1[:], P_b2[:]), "v")
            kb.inc(V.tensor_add(xap, xap, w.t1[:]), "v")
        kb.wait(SY, "v")
        dst = xo.rearrange("(kc p) t -> p kc t", p=128)[:, :, t * 512:(t + 1) * 512]
        src = xsb[:].rearrange("p (kc t) -> p kc t", t=TOK)[:, :, t * 512:(t + 1) * 512]
        kb.inc(SY.dma_start(out=dst, in_=src), "st", 16)


def emit_B1(kb, c, x, Wq, bq, anw, ctab_d, stab_d, Psw_d, xno, Qo, Ko, Vo, X3i):
    kb.phase("b1")
    xsb = kb.sb("xsb", [128, ND * TOK], F32)
    Wsb = kb.sb("Wsb", [128, ND * 1536], BF16)
    bqsb = kb.sb("bqsb", [128, 12], F32)
    anwsb = kb.sb("anwsb", [128, 8], F32)
    ctab = kb.sb("ctab_s", [128, TOK], BF16)
    stab = kb.sb("stab_s", [128, TOK], BF16)
    Psw = kb.sb("Psw_s", [128, 128], BF16)
    xnf = kb.sb("xnf", [128, ND * 512], F32)
    hb = kb.sb("hb", [128, ND * 512], BF16)
    qtmp = kb.sb("qtmp", [128, 512], BF16)
    psb = kb.sb("psb", [128, 512], BF16)
    t2b = kb.sb("t2b", [128, 512], BF16)
    stg = [kb.sb(f"stg{i}", [128, 512], BF16) for i in range(2)]
    vt = [kb.sb(f"vt{i}", [128, 4 * 256], BF16) for i in range(2)]
    hal = kb.sb("hal", [128, 1024], BF16)
    w = LNW(kb, "ln")

    P_s = kb.ps("P_s", [1, 512])
    P_q = kb.ps("P_q", [1, 512])
    P_b1 = kb.ps("P_b1", [128, 512])
    P_b2 = kb.ps("P_b2", [128, 512])
    P_m = [kb.ps(f"P_m{i}", [128, 512]) for i in range(2)]
    P_r = kb.ps("P_r", [128, 512])
    P_t = kb.ps("P_t", [128, 64], BF16)

    SY, V, S, Tn, G = kb.SY, kb.V, kb.S, kb.T, kb.G
    load_x_fm(kb, xsb, x)
    for kc in range(ND):
        kb.inc(G.dma_start(out=Wsb[:, kc * 1536:(kc + 1) * 1536],
                           in_=Wq[kc * 128:(kc + 1) * 128, :]), "gd", 16)
    kb.inc(SY.dma_start(out=bqsb[:], in_=bq), "ld", 16)
    kb.inc(SY.dma_start(out=anwsb[:], in_=anw), "ld", 16)
    kb.inc(SY.dma_start(out=ctab[:], in_=ctab_d), "ld", 16)
    kb.inc(SY.dma_start(out=stab[:], in_=stab_d), "ld", 16)
    kb.inc(SY.dma_start(out=Psw[:], in_=Psw_d), "ld", 16)
    kb.wait(V, "ld"); kb.wait(S, "ld")

    st_marks = {}
    for t in range(NT):
        xt = lambda kc: xsb[:, kc * TOK + t * 512: kc * TOK + (t + 1) * 512]
        emit_stats(kb, w, xt, c, P_s, P_q)
        emit_bcast(kb, c, w.nmb, P_b1)
        emit_bcast(kb, c, w.rsb, P_b2)
        for kc in range(ND):
            kb.wait(V, "t")
            kb.inc(V.tensor_add(w.t1[:], xt(kc), P_b1[:]), "v")
            kb.inc(V.tensor_mul(xnf[:, kc * 512:(kc + 1) * 512], w.t1[:], P_b2[:]), "v")
            kb.inc(V.tensor_copy(hb[:, kc * 512:(kc + 1) * 512], xnf[:, kc * 512:(kc + 1) * 512]), "v")
        kb.wait(SY, "v")
        dst = xno.rearrange("(kc p) t -> p kc t", p=128)[:, :, t * 512:(t + 1) * 512]
        kb.inc(SY.dma_start(out=dst, in_=hb[:].rearrange("p (kc t) -> p kc t", t=512)), "st", 16)
        xn_st_mark = kb.n("st")
        xnt = lambda kc: xnf[:, kc * 512:(kc + 1) * 512]
        emit_stats(kb, w, xnt, c, P_s, P_q, rms=True)
        emit_bcast(kb, c, w.rsb, P_b2)
        kb.wait(V, "st", xn_st_mark)
        for kc in range(ND):
            kb.wait(V, "t")
            kb.inc(V.tensor_mul(w.t1[:], xnt(kc), P_b2[:]), "v")
            kb.inc(V.tensor_scalar_mul(hb[:, kc * 512:(kc + 1) * 512], w.t1[:], anwsb[:, kc:kc + 1]), "v")
        kb.wait(Tn, "v")
        for oc in range(12):
            pm = P_m[oc % 2]
            kb.wait(Tn, "s")
            for kc in range(ND):
                kb.inc(Tn.matmul(pm[:], Wsb[:, kc * 1536 + oc * 128: kc * 1536 + (oc + 1) * 128],
                                 hb[:, kc * 512:(kc + 1) * 512],
                                 start=(kc == 0), stop=(kc == ND - 1)), "t")
            kb.wait(S, "t")
            kb.inc(S.activation(qtmp[:], pm[:], AF.Identity, bias=bqsb[:, oc:oc + 1]), "s")
            if oc < 10:
                kb.wait(Tn, "s")
                kb.wait(Tn, "v")
                kb.inc(Tn.matmul(P_r[:], Psw[:], qtmp[:], start=True, stop=True), "t")
                sg = stg[oc % 2]
                if id(sg) in st_marks:
                    kb.wait(V, "st", st_marks[id(sg)])
                kb.wait(V, "t")
                kb.inc(V.tensor_copy(psb[:], P_r[:]), "v")
                kb.wait(V, "s")
                kb.inc(V.tensor_mul(t2b[:], qtmp[:], ctab[:, t * 512:(t + 1) * 512]), "v")
                kb.inc(V.tensor_mul(psb[:], psb[:], stab[:, t * 512:(t + 1) * 512]), "v")
                kb.inc(V.tensor_add(sg[:], t2b[:], psb[:]), "v")
                kb.wait(SY, "v")
                if oc < 8:
                    kb.inc(SY.dma_start(out=Qo[oc * 128:(oc + 1) * 128, t * 512:(t + 1) * 512],
                                        in_=sg[:]), "st", 16)
                else:
                    kb.inc(SY.dma_start(out=Ko[(oc - 8) * 128:(oc - 7) * 128, t * 512:(t + 1) * 512],
                                        in_=sg[:]), "st", 16)
                st_marks[id(sg)] = kb.n("st")
            else:
                vc = oc - 10
                vbuf = vt[t % 2]
                if (vc == 0) and (id(vbuf) in st_marks):
                    kb.wait(S, "st", st_marks[id(vbuf)])
                for hh in range(2):
                    for p4 in range(4):
                        kb.wait(Tn, "s")
                        kb.wait(Tn, "v")
                        kb.inc(Tn.transpose(P_t[:], qtmp[64 * hh:64 * hh + 64, 128 * p4:128 * (p4 + 1)],
                                            c["idn2"][64 * hh:64 * hh + 64, :]), "t")
                        kb.wait(V, "t")
                        col = 256 * p4 + 64 * (2 * vc + hh)
                        kb.inc(V.tensor_copy(vbuf[:, col:col + 64], P_t[:]), "v")
                if vc == 1:
                    kb.wait(SY, "v")
                    vdst = Vo[t * 512:(t + 1) * 512, :].rearrange("(p4 p) ch -> p p4 ch", p=128)
                    kb.inc(SY.dma_start(out=vdst,
                                        in_=vbuf[:].rearrange("p (p4 ch) -> p p4 ch", ch=256)), "st", 16)
                    st_marks[id(vbuf)] = kb.n("st")
    kb.wait(SY, "st")
    for ch in range(2):
        koff = ch * 512
        kb.inc(SY.dma_start(out=hal[:, koff:koff + 256].rearrange("p (h tt) -> p h tt", h=2),
                            in_=Ko[:, 1024 * ch + 896:1024 * ch + 1024]
                            .rearrange("(h p) tt -> p h tt", p=128)), "ld", 16)
        kb.inc(SY.dma_start(out=hal[:, koff + 256:koff + 512],
                            in_=Vo[1024 * ch + 896:1024 * ch + 1024, :]), "ld", 16)
    kb.wait(SY, "ld")
    for ch in range(2):
        koff = ch * 512
        kb.inc(SY.dma_start(out=X3i[2 * ch:2 * ch + 1, :].rearrange("a (h p tt) -> p (a h) tt", p=128, tt=128),
                            in_=hal[:, koff:koff + 256].rearrange("p (h tt) -> p h tt", h=2)), "st", 16)
        kb.inc(SY.dma_start(out=X3i[2 * ch + 1:2 * ch + 2, :].rearrange("a (p f) -> p (a f)", p=128),
                            in_=hal[:, koff + 256:koff + 512]), "st", 16)


def emit_B2(kb, c, Qd, Kod, Vod, X3o, msel_d, bmd, snd, xnd, Wpd, bpd, fnwd, gwd, gbd,
            xmo, h2o, ceo):
    kb.phase("b2")
    Qsb = kb.sb("Qsb", [128, ND * TOK], BF16)
    Ksb = kb.sb("Ksb2", [128, 8 * 1152], BF16)
    Vsl = kb.sb("Vsl2", [128, 8 * 9 * 65], BF16)
    bm = kb.sb("bm_s", [128, 7 * 512], BF16)
    snsb = kb.sb("snsb", [1, 16], F32)
    snexp = kb.sb("snexp", [1, 16], F32)
    xnsb = kb.sb("xnsb", [128, ND * 512], BF16)
    Wsb = kb.sb("Wsb2", [128, ND * D], BF16)
    bpsb = kb.sb("bpsb2", [128, 8], F32)
    fnwsb = kb.sb("fnwsb", [128, 8], F32)
    gwsb = kb.sb("gwsb", [128, ND * 32], BF16)
    gbsb = kb.sb("gbsb", [32, 1], F32)
    Ysb = kb.sb("Ysb2", [128, ND * TOK], BF16)
    esT = [kb.sb(f"esT{i}", [128, 3 * 512], BF16) for i in range(2)]
    dr = kb.sb("dr", [1, 512], F32)
    drb = kb.sb("drb", [1, 512], BF16)
    bcs = kb.sb("bcs", [64, 512], F32)
    zb = kb.sb("zb", [128, ND * 512], BF16)
    xm_f = kb.sb("xm_f", [128, ND * 512], F32)
    h2b = kb.sb("h2b", [128, ND * 512], BF16)
    glf = kb.sb("glf", [32, 512], F32)
    I32f = kb.sb("I32f", [32, 32], F32)
    I128f = kb.sb("I128f", [128, 128], F32)
    gtm = kb.sb("gtm", [128, 32], F32)
    cur = kb.sb("cur", [128, 32], F32)
    eqb = kb.sb("eqb", [128, 32], F32)
    mrow = kb.sb("mrow", [128, 5], F32)
    nm0 = kb.sb("nm0", [128, 1], F32)
    rd_ = kb.sb("rd_", [128, 1], F32)
    et = kb.sb("et", [128, 32], F32)
    sel = kb.sb("sel", [128, 32], F32)
    wgt = kb.sb("wgt", [128, 32], F32)
    cetm = kb.sb("cetm", [128, 32], F32)
    cef = kb.sb("cef", [32, 512], F32)
    tsel = kb.sb("tsel", [128, 8 * 256], BF16)
    Ks = kb.sb("Ks", [128, 2 * 256], BF16)
    Vs = kb.sb("Vs", [128, 2 * 256], BF16)
    tmpsel = kb.sb("tmpsel", [128, 256], BF16)
    mselsb = kb.sb("mselsb", [128, 8], F32)
    w = LNW(kb, "ln")

    P_S = [kb.ps(f"P_Sw{i}", [128, 3 * 512]) for i in range(2)]
    P_y = kb.ps("P_y", [128, 512])
    P_x = kb.ps("P_x", [128, 512])
    P_m = [P_S[0][:, 0:512], P_S[1][:, 0:512]]
    P_a = [P_S[0][0:1, 1024:1536], P_S[1][0:1, 1024:1536]]
    P_t32 = P_S[0][:, 512:1024]

    SY, V, S, Tn, G = kb.SY, kb.V, kb.S, kb.T, kb.G
    kb.inc(G.memset(I32f[:], 1.0), "g")
    kb.inc(G.affine_select(out=I32f[:], in_=I32f[:], compare_op=ALU.is_equal,
                           fill=0.0, base=0, pattern=[[-1, 32]], channel_multiplier=1), "g")
    kb.inc(G.memset(I128f[:], 1.0), "g")
    kb.inc(G.affine_select(out=I128f[:], in_=I128f[:], compare_op=ALU.is_equal,
                           fill=0.0, base=0, pattern=[[-1, 128]], channel_multiplier=1), "g")
    load_x_fm(kb, Qsb, Qd)
    kb.inc(SY.dma_start(out=bm[:], in_=bmd), "ld", 16)
    kb.inc(SY.dma_start(out=snsb[:], in_=snd), "ld", 16)
    kb.inc(SY.dma_start(out=bpsb[:], in_=bpd), "ld", 16)
    kb.inc(SY.dma_start(out=fnwsb[:], in_=fnwd), "ld", 16)
    kb.inc(SY.dma_start(out=gbsb[:], in_=gbd), "ld", 16)
    kb.inc(SY.dma_start(out=mselsb[:], in_=msel_d), "ld", 16)
    for kc in range(ND):
        kb.inc(G.dma_start(out=Wsb[:, kc * D:(kc + 1) * D], in_=Wpd[kc * 128:(kc + 1) * 128, :]), "gd", 16)
        kb.inc(G.dma_start(out=gwsb[:, kc * 32:(kc + 1) * 32], in_=gwd[kc * 128:(kc + 1) * 128, :]), "gd", 16)
    for ch in range(2):
        for kh in range(4):
            bi = ch * 4 + kh
            src = Kod[64 * kh:64 * kh + 64, 1024 * ch:1024 * ch + 1024]
            for rh in range(2):
                kb.inc(SY.dma_start(out=Ksb[64 * rh:64 * rh + 64, bi * 1152 + 128: bi * 1152 + 1152],
                                    in_=src), "ld", 16)
            kb.inc(SY.dma_start(
                out=Vsl[:, bi * 585:(bi + 1) * 585].rearrange("p (j cc) -> p j cc", cc=65)[:, 1:9, 0:64],
                in_=Vod[1024 * ch:1024 * ch + 1024, 64 * kh:64 * kh + 64].rearrange("(j p) cc -> p j cc", p=128)),
                "ld", 16)
    kb.wait(SY, "cc")
    for ch in range(2):
        for kind in range(2):
            kb.wait(SY, "v")
            for s in range(8):
                if kind == 0:
                    kb.inc(SY.dma_start(
                        out=tsel[:, 256 * s:256 * (s + 1)].rearrange("p (h tt) -> p h tt", h=2),
                        in_=X3o[s:s + 1, 2 * ch:2 * ch + 1, :].rearrange("a b (h p tt) -> p (a h) tt",
                                                                        p=128, tt=128)), "ld", 16)
                else:
                    kb.inc(SY.dma_start(
                        out=tsel[:, 256 * s:256 * (s + 1)],
                        in_=X3o[s:s + 1, 2 * ch + 1:2 * ch + 2, :].rearrange("a b (p f) -> p (a b f)",
                                                                            p=128)), "ld", 16)
            kb.wait(V, "ld")
            dst = (Ks if kind == 0 else Vs)
            for s in range(8):
                blk = tsel[:, 256 * s:256 * (s + 1)]
                if s == 0:
                    kb.inc(V.tensor_scalar(dst[:, 256 * ch:256 * (ch + 1)], blk, mselsb[:, 0:1], None, ALU.mult), "v")
                else:
                    kb.inc(V.tensor_scalar(tmpsel[:], blk, mselsb[:, s:s + 1], None, ALU.mult), "v")
                    kb.inc(V.tensor_add(dst[:, 256 * ch:256 * (ch + 1)], dst[:, 256 * ch:256 * (ch + 1)],
                                        tmpsel[:]), "v")
    for ch in range(2):
        for kh in range(4):
            bi = ch * 4 + kh
            ksrc = Ks[64 * (kh % 2):64 * (kh % 2) + 64, 256 * ch + 128 * (kh // 2): 256 * ch + 128 * (kh // 2) + 128]
            for rh in range(2):
                kb.inc(V.tensor_copy(Ksb[64 * rh:64 * rh + 64, bi * 1152: bi * 1152 + 128], ksrc), "v")
            kb.inc(V.tensor_copy(Vsl[:, bi * 585: bi * 585 + 64],
                                 Vs[:, 256 * ch + 64 * kh: 256 * ch + 64 * kh + 64]), "v")
    onescol = Vsl[:].rearrange("p (j cc) -> p j cc", cc=65)[:, :, 64:65]
    kb.inc(V.memset(onescol, 1.0), "v")
    kb.wait(S, "ld")
    kb.inc(S.activation(snexp[:], snsb[:], AF.Exp), "s")

    epi_mark = 0
    exp_last = [0, 0]
    kb.wait(Tn, "ld")
    kb.wait(Tn, "v")
    kb.wait(Tn, "g")
    for h in range(16):
        kh = h // 4
        qrow = 64 * (h % 2)
        qpr = h // 2
        for qt in range(NT):
            chx = qt // 2
            qoff = (qt % 2) * 512
            qap = Qsb[qrow:qrow + 64, qpr * TOK + qt * 512: qpr * TOK + (qt + 1) * 512]
            wgroups = [(0, 3), (3, 2)]
            def emit_qkw(gi):
                k0, gs = wgroups[gi]
                par = gi % 2
                ps = P_S[par]
                kb.wait(Tn, "s", exp_last[par])
                for i4 in range(gs):
                    kt = k0 + i4
                    kcol = (chx * 4 + kh) * 1152 + qoff + kt * 128
                    ins = Tn.matmul(ps[:, i4 * 512:(i4 + 1) * 512],
                                    Ksb[qrow:qrow + 64, kcol:kcol + 128],
                                    qap, start=True, stop=True)
                    if i4 == gs - 1:
                        kb.inc(ins, "t")
                qkw_marks[gi] = kb.n("t")
            qkw_marks = {}
            emit_qkw(0)
            for gi, (k0, gs) in enumerate(wgroups):
                par = gi % 2
                ps = P_S[par]
                eb = esT[par]
                kb.wait(S, "t", qkw_marks[gi])
                kb.inc(S.activation(eb[:, 0:gs * 512], ps[:, 0:gs * 512], AF.Exp, scale=0.125), "s")
                exp_last[par] = kb.n("s")
                kb.wait(V, "s", exp_last[par])
                kb.wait(V, "g")
                for i4 in range(gs):
                    kt = k0 + i4
                    if qoff == 0 and kt == 0:
                        mi = 5 if chx == 0 else 6
                    else:
                        mi = kt
                    kb.inc(V.tensor_mul(eb[:, i4 * 512:(i4 + 1) * 512],
                                        eb[:, i4 * 512:(i4 + 1) * 512],
                                        bm[:, mi * 512:(mi + 1) * 512]), "v")
                if gi + 1 < len(wgroups):
                    emit_qkw(gi + 1)
                kb.wait(Tn, "v", kb.n("v"))
                if gi == 0:
                    kb.wait(Tn, "v", epi_mark)
                for i4 in range(gs):
                    kt = k0 + i4
                    vcol = (chx * 4 + kh) * 585 + (qoff // 128 + kt) * 65
                    ins = Tn.matmul(P_y[0:65, :], Vsl[:, vcol:vcol + 65],
                                    eb[:, i4 * 512:(i4 + 1) * 512],
                                    start=(kt == 0), stop=(kt == 4))
                    if i4 == gs - 1:
                        kb.inc(ins, "t")
            kb.wait(V, "t")
            kb.inc(V.tensor_scalar(dr[:], P_y[64:65, :], snexp[0:1, h:h + 1], None, ALU.add), "v")
            kb.inc(V.reciprocal(dr[:], dr[:]), "v")
            kb.inc(V.tensor_copy(drb[:], dr[:]), "v")
            kb.wait(Tn, "v")
            kb.inc(Tn.matmul(P_x[0:64, :], c["ones_row"][0:1, 0:64], drb[:], start=True, stop=True), "t")
            kb.wait(V, "t")
            kb.inc(V.tensor_copy(bcs[:], P_x[0:64, :]), "v")
            kb.inc(V.tensor_mul(Ysb[qrow:qrow + 64, qpr * TOK + qt * 512: qpr * TOK + (qt + 1) * 512],
                                P_y[0:64, :], bcs[:]), "v")
            epi_mark = kb.n("v")

    kb.wait(Tn, "gd")
    for t in range(NT):
        for oc in range(ND):
            pm = P_m[oc % 2]
            kb.wait(Tn, "s")
            kb.wait(Tn, "v")
            for kc in range(ND):
                kb.inc(Tn.matmul(pm[:], Wsb[:, kc * D + oc * 128: kc * D + (oc + 1) * 128],
                                 Ysb[:, kc * TOK + t * 512: kc * TOK + (t + 1) * 512],
                                 start=(kc == 0), stop=(kc == ND - 1)), "t")
            kb.wait(S, "t")
            kb.wait(S, "v")
            kb.inc(S.activation(zb[:, oc * 512:(oc + 1) * 512], pm[:], AF.Identity,
                                bias=bpsb[:, oc:oc + 1]), "s")
        kb.wait(SY, "v")
        xnsrc = xnd.rearrange("(kc p) tt -> p kc tt", p=128)[:, :, t * 512:(t + 1) * 512]
        kb.inc(SY.dma_start(out=xnsb[:].rearrange("p (kc tt) -> p kc tt", tt=512), in_=xnsrc), "ld", 16)
        xn_mark = kb.n("ld")
        kb.wait(V, "ld", xn_mark)
        for kc in range(ND):
            kb.wait(V, "s")
            kb.wait(V, "st")
            xnap = xnsb[:, kc * 512:(kc + 1) * 512]
            kb.inc(V.tensor_add(xm_f[:, kc * 512:(kc + 1) * 512], xnap, zb[:, kc * 512:(kc + 1) * 512]), "v")
        kb.wait(G, "v")
        dstx = xmo.rearrange("(kc p) tt -> p kc tt", p=128)[:, :, t * 512:(t + 1) * 512]
        kb.inc(G.dma_start(out=dstx, in_=xm_f[:].rearrange("p (kc tt) -> p kc tt", tt=512)), "st", 16)
        xmt = lambda kc: xm_f[:, kc * 512:(kc + 1) * 512]
        emit_stats(kb, w, xmt, c, P_a[0], P_a[1], rms=True)
        emit_bcast(kb, c, w.rsb, P_y)
        for kc in range(ND):
            kb.wait(V, "t")
            kb.wait(V, "st")
            kb.inc(V.tensor_mul(w.t1[:], xmt(kc), P_y[:]), "v")
            kb.inc(V.tensor_scalar_mul(h2b[:, kc * 512:(kc + 1) * 512],
                                       w.t1[:], fnwsb[:, kc:kc + 1]), "v")
        kb.wait(SY, "v")
        dsth = h2o.rearrange("(kc p) tt -> p kc tt", p=128)[:, :, t * 512:(t + 1) * 512]
        kb.inc(SY.dma_start(out=dsth, in_=h2b[:].rearrange("p (kc tt) -> p kc tt", tt=512)), "st", 16)
        kb.wait(Tn, "v")
        for kc in range(ND):
            kb.inc(Tn.matmul(P_x[0:32, :], gwsb[:, kc * 32:(kc + 1) * 32],
                             h2b[:, kc * 512:(kc + 1) * 512],
                             start=(kc == 0), stop=(kc == ND - 1)), "t")
        kb.wait(S, "t")
        kb.inc(S.activation(glf[:], P_x[0:32, :], AF.Identity, bias=gbsb[:, 0:1]), "s")
        for p4 in range(4):
            kb.wait(Tn, "s")
            kb.wait(Tn, "v")
            kb.inc(Tn.transpose(P_m[0][:, 0:32], glf[0:32, 128 * p4:128 * (p4 + 1)], I32f[:]), "t")
            kb.wait(V, "t")
            kb.incw(V, V.tensor_copy(gtm[:], P_m[0][:, 0:32]), "v")
            kb.incw(V, V.tensor_copy(cur[:], gtm[:]), "v")
            for r in range(4):
                kb.incw(V, V.reduce_max(mrow[:, r:r + 1], cur[:], axis=AX.X), "v")
                if r < 3:
                    kb.incw(V, V.tensor_scalar(eqb[:], cur[:], mrow[:, r:r + 1], None, ALU.is_ge), "v")
                    kb.incw(V, V.scalar_tensor_tensor(cur[:], in0=eqb[:], scalar=-1e30, in1=cur[:],
                                                      op0=ALU.mult, op1=ALU.add), "v")
            kb.incw(V, V.tensor_scalar_mul(nm0[:], mrow[:, 0:1], -1.0), "v")
            kb.wait(S, "v")
            kb.inc(S.activation(et[:], gtm[:], AF.Exp, bias=nm0[:]), "s")
            kb.incw(V, V.tensor_scalar(sel[:], gtm[:], mrow[:, 3:4], None, ALU.is_ge), "v")
            kb.wait(V, "s")
            kb.incw(V, V.tensor_mul(wgt[:], et[:], sel[:]), "v")
            kb.incw(V, V.reduce_sum(mrow[:, 4:5], wgt[:], axis=AX.X), "v")
            kb.incw(V, V.reciprocal(rd_[:], mrow[:, 4:5]), "v")
            kb.incw(V, V.tensor_scalar_mul(cetm[:], wgt[:], rd_[:]), "v")
            kb.wait(Tn, "v")
            kb.inc(Tn.transpose(P_t32[0:32, 0:128], cetm[:], I128f[:]), "t")
            kb.wait(V, "t")
            kb.inc(V.tensor_copy(cef[0:32, 128 * p4: 128 * (p4 + 1)],
                                 P_t32[0:32, 0:128]), "v")
        kb.wait(SY, "v")
        kb.inc(SY.dma_start(out=ceo[:, t * 512:(t + 1) * 512], in_=cef[0:32, :]), "st", 16)


def emit_B3(kb, c, h2d, ced, xmd, xd, Wud, Wdd, xo):
    kb.phase("b3")
    h2sb = kb.sb("h2sb", [128, ND * TOK], BF16)
    cefb = kb.sb("cefb", [32, TOK], BF16)
    Ebig = kb.sb("Ebig", [32, 32 * 128], BF16)
    acc = kb.sb("acc", [128, ND * TOK], F32)
    wes = ExitStack()
    _wsb = lambda n, sh, dt: wes.enter_context(kb.nc.sbuf_tensor("b3_" + n, sh, dt))
    cb_all = _wsb("cb_all", [128, 4 * 512], F32)
    sg_sb = [_wsb(f"sg_sb{i}", [128, 512], F32) for i in range(2)]
    mdf = [_wsb(f"mdf{i}", [128, 512], F32) for i in range(2)]
    mdb = _wsb("mdb", [128, 4 * ND * 512], BF16)
    Wusb = [_wsb(f"Wusb{i}", [128, ND * 1024], BF16) for i in range(2)]
    Wdsb = _wsb("Wdsb", [128, ND * D], BF16)

    P_u = [kb.ps(f"P_u{i}", [128, 512]) for i in range(2)]
    P_g = [kb.ps(f"P_g{i}", [128, 512]) for i in range(2)]
    P_d = [kb.ps(f"P_d{i}", [128, 512]) for i in range(2)]
    P_c = kb.ps("P_c", [128, 512])

    SY, V, S, Tn, G = kb.SY, kb.V, kb.S, kb.T, kb.G
    gd0 = kb.n("gd")
    kb.inc(G.memset(Ebig[:], 1.0), "g")
    kb.inc(G.affine_select(out=Ebig[:].rearrange("p (e cc) -> p e cc", cc=128),
                           in_=Ebig[:].rearrange("p (e cc) -> p e cc", cc=128),
                           compare_op=ALU.is_equal, fill=0.0, base=0,
                           pattern=[[-1, 32], [0, 128]], channel_multiplier=1), "g")
    load_x_fm(kb, h2sb, h2d)
    load_x_fm(kb, acc, xmd, eng=G, sem="gd")
    kb.inc(G.dma_start(out=cefb[:], in_=ced), "gd", 16)

    wu_ready = {}
    wu_free = {}
    wd_ready = {}
    wd_free = 0

    def prefetch_wu(unit):
        e, ha = divmod(unit, 2)
        if e >= 32:
            return
        par = unit % 2
        kb.wait(G, "t", wu_free.get(par, 0))
        wsl = Wud[e * D:(e + 1) * D, :].rearrange("(kc p) cc -> p kc cc", p=128)
        dstu = Wusb[par][:].rearrange("p (kc two cc) -> p kc two cc", two=2, cc=512)
        kb.inc(G.dma_start(out=dstu[:, :, 0, :], in_=wsl[:, :, 512 * ha:512 * (ha + 1)]), "gd", 16)
        kb.inc(G.dma_start(out=dstu[:, :, 1, :], in_=wsl[:, :, 1024 + 512 * ha:1024 + 512 * (ha + 1)]), "gd", 16)
        wu_ready[unit] = kb.n("gd")

    def prefetch_wd(e):
        if e >= 32:
            return
        kb.wait(G, "t", wd_free)
        wsl = Wdd[e * D:(e + 1) * D, :].rearrange("(kc p) cc -> p kc cc", p=128)
        kb.inc(G.dma_start(out=Wdsb[:].rearrange("p (kc cc) -> p kc cc", cc=D), in_=wsl), "gd", 16)
        wd_ready[e] = kb.n("gd")

    prefetch_wu(0)
    prefetch_wu(1)
    prefetch_wd(0)
    pu_last = [0, 0]
    pg_last = [0, 0]
    pd_last = [0, 0]
    for e in range(32):
        kb.wait(Tn, "g")
        kb.wait(Tn, "gd", gd0 + 32)
        kb.wait(S, "v")
        for t in range(NT):
            kb.wait(Tn, "s")
            kb.inc(Tn.matmul(P_c[:], Ebig[:, e * 128:(e + 1) * 128],
                             cefb[0:32, t * 512:(t + 1) * 512], start=True, stop=True), "t")
            kb.wait(S, "t")
            kb.inc(S.activation(cb_all[:, t * 512:(t + 1) * 512], P_c[:], AF.Identity), "s")
        for ha in range(2):
            unit = 2 * e + ha
            par = unit % 2
            kb.wait(Tn, "gd", wu_ready[unit])
            for t in range(NT):
                h2t = lambda kc: h2sb[:, kc * TOK + t * 512: kc * TOK + (t + 1) * 512]
                for jj in range(4):
                    j = 4 * ha + jj
                    jpar = jj % 2
                    kb.wait(Tn, "v", pu_last[jpar])
                    for kc in range(ND):
                        kb.inc(Tn.matmul(P_u[jpar][:], Wusb[par][:, kc * 1024 + jj * 128: kc * 1024 + (jj + 1) * 128],
                                         h2t(kc), start=(kc == 0), stop=(kc == ND - 1)), "t")
                    kb.wait(Tn, "s", pg_last[jpar])
                    for kc in range(ND):
                        kb.inc(Tn.matmul(P_g[jpar][:], Wusb[par][:, kc * 1024 + 512 + jj * 128: kc * 1024 + 512 + (jj + 1) * 128],
                                         h2t(kc), start=(kc == 0), stop=(kc == ND - 1)), "t")
                    g_mark = kb.n("t")
                    kb.wait(S, "t", g_mark)
                    kb.wait(S, "v", pu_last[jpar])
                    kb.inc(S.activation(sg_sb[jpar][:], P_g[jpar][:], AF.Silu), "s")
                    pg_last[jpar] = kb.n("s")
                    kb.wait(V, "s", pg_last[jpar])
                    kb.inc(V.tensor_mul(mdf[jpar][:], sg_sb[jpar][:], P_u[jpar][:]), "v")
                    kb.inc(V.tensor_mul(mdb[:, (t * ND + j) * 512:(t * ND + j + 1) * 512],
                                        mdf[jpar][:], cb_all[:, t * 512:(t + 1) * 512]), "v")
                    pu_last[jpar] = kb.n("v")
            wu_free[par] = kb.n("t")
            if unit + 2 <= 63:
                prefetch_wu(unit + 2)
        kb.wait(Tn, "gd", wd_ready[e])
        kb.wait(Tn, "v")
        kb.wait(V, "gd", gd0 + 32)
        for t in range(NT):
            for oc in range(ND):
                pdp = oc % 2
                pd = P_d[pdp]
                kb.wait(Tn, "v", pd_last[pdp])
                for j in range(ND):
                    kb.inc(Tn.matmul(pd[:], Wdsb[:, j * D + oc * 128: j * D + (oc + 1) * 128],
                                     mdb[:, (t * ND + j) * 512:(t * ND + j + 1) * 512],
                                     start=(j == 0), stop=(j == ND - 1)), "t")
                kb.wait(V, "t")
                aap = acc[:, oc * TOK + t * 512: oc * TOK + (t + 1) * 512]
                kb.inc(V.tensor_add(aap, aap, pd[:]), "v")
                pd_last[pdp] = kb.n("v")
        wd_free = kb.n("t")
        prefetch_wd(e + 1)
    wes.close()
    kb.wait(SY, "t"); kb.wait(SY, "v"); kb.wait(SY, "s")
    xsb = kb.sb("xsb3", [128, ND * TOK], F32)
    w = LNW(kb, "ln3")
    load_x_fm(kb, xsb, xd)
    kb.wait(V, "ld")
    P_b1 = P_u[0]
    P_b2 = P_u[1]
    for t in range(NT):
        at = lambda kc: acc[:, kc * TOK + t * 512: kc * TOK + (t + 1) * 512]
        emit_stats(kb, w, at, c, P_g[0][0:1, :], P_g[1][0:1, :])
        emit_bcast(kb, c, w.nmb, P_b1)
        emit_bcast(kb, c, w.rsb, P_b2)
        for kc in range(ND):
            kb.wait(V, "t")
            xap = xsb[:, kc * TOK + t * 512: kc * TOK + (t + 1) * 512]
            kb.inc(V.tensor_add(w.t1[:], at(kc), P_b1[:]), "v")
            kb.inc(V.tensor_mul(w.t1[:], w.t1[:], P_b2[:]), "v")
            kb.inc(V.tensor_add(xap, xap, w.t1[:]), "v")
        kb.wait(SY, "v")
        dst = xo.rearrange("(kc p) tt -> p kc tt", p=128)[:, :, t * 512:(t + 1) * 512]
        src2 = xsb[:].rearrange("p (kc tt) -> p kc tt", tt=TOK)[:, :, t * 512:(t + 1) * 512]
        kb.inc(SY.dma_start(out=dst, in_=src2), "st", 16)


def emit_C1(kb, c, x, Wq, bq, w1, b1, xno, X4i):
    kb.phase("c1")
    xsb = kb.sb("xsb", [128, ND * TOK], F32)
    Wsb = kb.sb("Wsb", [128, ND * 3 * D], BF16)
    bqsb = kb.sb("bqsb", [128, 24], F32)
    w1sb = kb.sb("w1sb", [128, 8], F32)
    b1sb = kb.sb("b1sb", [128, 8], F32)
    xnf = kb.sb("xnf", [128, ND * 512], F32)
    xlb2 = [kb.sb(f"xlb{i}", [128, ND * 512], BF16) for i in range(2)]
    qtmp = kb.sb("qtmp", [128, 512], BF16)
    stg = [kb.sb(f"stg{i}", [128, 512], BF16) for i in range(2)]
    vt = [kb.sb(f"vt{i}", [128, 4 * D], BF16) for i in range(2)]
    w = LNW(kb, "ln")

    P_s = kb.ps("P_s", [1, 512])
    P_q = kb.ps("P_q", [1, 512])
    P_b1 = kb.ps("P_b1", [128, 512])
    P_b2 = kb.ps("P_b2", [128, 512])
    P_m = [kb.ps(f"P_m{i}", [128, 512]) for i in range(2)]
    P_t = kb.ps("P_t", [128, 64], BF16)

    SY, V, S, Tn, G = kb.SY, kb.V, kb.S, kb.T, kb.G
    load_x_fm(kb, xsb, x)
    for kc in range(ND):
        kb.inc(G.dma_start(out=Wsb[:, kc * 3 * D:(kc + 1) * 3 * D],
                           in_=Wq[kc * 128:(kc + 1) * 128, :]), "gd", 16)
    kb.inc(SY.dma_start(out=bqsb[:], in_=bq), "ld", 16)
    kb.inc(SY.dma_start(out=w1sb[:], in_=w1), "ld", 16)
    kb.inc(SY.dma_start(out=b1sb[:], in_=b1), "ld", 16)
    kb.wait(V, "ld"); kb.wait(S, "ld")

    st_marks = {}
    proj_mark = [0, 0]
    xst_mark = [0, 0]
    for t in range(NT):
        tp = t % 2
        xlb = xlb2[tp]
        xt = lambda kc: xsb[:, kc * TOK + t * 512: kc * TOK + (t + 1) * 512]
        emit_stats(kb, w, xt, c, P_s, P_q)
        emit_bcast(kb, c, w.nmb, P_b1)
        emit_bcast(kb, c, w.rsb, P_b2)
        bc_mark = kb.n("t")
        kb.wait(V, "t", max(bc_mark, proj_mark[tp]))
        kb.wait(V, "st", xst_mark[tp])
        for kc in range(ND):
            kb.inc(V.tensor_add(w.t1[:], xt(kc), P_b1[:]), "v")
            kb.inc(V.tensor_mul(xnf[:, kc * 512:(kc + 1) * 512], w.t1[:], P_b2[:]), "v")
            kb.inc(V.tensor_copy(xlb[:, kc * 512:(kc + 1) * 512], xnf[:, kc * 512:(kc + 1) * 512]), "v")
        kb.wait(SY, "v")
        dst = xno.rearrange("(kc p) tt -> p kc tt", p=128)[:, :, t * 512:(t + 1) * 512]
        kb.inc(SY.dma_start(out=dst, in_=xlb[:].rearrange("p (kc tt) -> p kc tt", tt=512)), "st", 16)
        xn_st = kb.n("st")
        xst_mark[tp] = xn_st
        xnt = lambda kc: xnf[:, kc * 512:(kc + 1) * 512]
        emit_stats(kb, w, xnt, c, P_s, P_q)
        emit_bcast(kb, c, w.nmb, P_b1)
        emit_bcast(kb, c, w.rsb, P_b2)
        bc_mark = kb.n("t")
        kb.wait(V, "st", xn_st)
        kb.wait(V, "t", max(bc_mark, proj_mark[tp]))
        for kc in range(ND):
            kb.inc(V.tensor_add(w.t1[:], xnt(kc), P_b1[:]), "v")
            kb.inc(V.tensor_mul(w.t1[:], w.t1[:], P_b2[:]), "v")
            kb.inc(V.tensor_scalar(xlb[:, kc * 512:(kc + 1) * 512], w.t1[:],
                                   w1sb[:, kc:kc + 1], b1sb[:, kc:kc + 1], ALU.mult, ALU.add), "v")
        kb.wait(Tn, "v")
        kb.wait(Tn, "gd")
        for oc in range(24):
            pm = P_m[oc % 2]
            kb.wait(Tn, "s")
            for kc in range(ND):
                kb.inc(Tn.matmul(pm[:], Wsb[:, kc * 3 * D + oc * 128: kc * 3 * D + (oc + 1) * 128],
                                 xlb[:, kc * 512:(kc + 1) * 512],
                                 start=(kc == 0), stop=(kc == ND - 1)), "t")
            kb.wait(S, "t")
            kb.inc(S.activation(qtmp[:], pm[:], AF.Identity, bias=bqsb[:, oc:oc + 1]), "s")
            if oc < 16:
                sg = stg[oc % 2]
                if id(sg) in st_marks:
                    kb.wait(V, "st", st_marks[id(sg)])
                kb.wait(V, "s")
                kb.inc(V.tensor_copy(sg[:], qtmp[:]), "v")
                kb.wait(SY, "v")
                k = 0 if oc < 8 else 1
                j = oc % 8
                s = 4 * (t // 2) + j // 2
                slot = qk_slot(X4i, s, k)
                kb.inc(SY.dma_start(out=slot[128 * (j % 2):128 * (j % 2) + 128,
                                             512 * (t % 2):512 * (t % 2) + 512], in_=sg[:]), "st", 16)
                st_marks[id(sg)] = kb.n("st")
            else:
                vc = oc - 16
                vbuf = vt[t % 2]
                if (vc == 0) and (id(vbuf) in st_marks):
                    kb.wait(V, "st", st_marks[id(vbuf)])
                for hh in range(2):
                    for p4 in range(4):
                        kb.wait(Tn, "s")
                        kb.wait(Tn, "v")
                        kb.inc(Tn.transpose(P_t[:], qtmp[64 * hh:64 * hh + 64, 128 * p4:128 * (p4 + 1)],
                                            c["idn2"][64 * hh:64 * hh + 64, :]), "t")
                        kb.wait(V, "t")
                        col = 1024 * p4 + 64 * (2 * vc + hh)
                        kb.inc(V.tensor_copy(vbuf[:, col:col + 64], P_t[:]), "v")
                if vc == 7:
                    kb.wait(SY, "v")
                    for p4 in range(4):
                        for q in range(4):
                            s = 4 * (t // 2) + q
                            r0 = 512 * (t % 2) + 128 * p4
                            kb.inc(SY.dma_start(out=v_slot(X4i, s)[r0:r0 + 128, :],
                                                in_=vbuf[:, 1024 * p4 + 256 * q: 1024 * p4 + 256 * q + 256]),
                                   "st", 16)
                    st_marks[id(vbuf)] = kb.n("st")
        proj_mark[tp] = kb.n("t")


def emit_C3(kb, c, x, X5o, xnd, Wo, bo, w2, b2, xo, xn3o):
    kb.phase("c3")
    xsb = kb.sb("xsb", [128, ND * TOK], F32)
    ysb = kb.sb("ysb", [128, ND * TOK], BF16)
    xnsb = kb.sb("xnsb", [128, ND * TOK], BF16)
    Wsb = kb.sb("Wsb", [128, ND * D], BF16)
    bosb = kb.sb("bosb", [128, 8], F32)
    w2sb = kb.sb("w2sb", [128, 8], F32)
    b2sb = kb.sb("b2sb", [128, 8], F32)
    zb = kb.sb("zb", [128, ND * 512], BF16)
    rm_f = kb.sb("rm_f", [128, ND * 512], F32)
    rp_f = kb.sb("rp_f", [128, ND * 512], F32)
    w = LNW(kb, "ln")

    P_s = kb.ps("P_s", [1, 512])
    P_q = kb.ps("P_q", [1, 512])
    P_b1 = kb.ps("P_b1", [128, 512])
    P_b2 = kb.ps("P_b2", [128, 512])
    P_m = [kb.ps(f"P_m{i}", [128, 512]) for i in range(2)]

    SY, V, S, Tn, G = kb.SY, kb.V, kb.S, kb.T, kb.G
    load_x_fm(kb, xsb, x)
    load_x_fm(kb, xnsb, xnd)
    kb.wait(SY, "cc")
    for r in range(8):
        ys = y_slot(X5o, r)
        for half in range(2):
            kc = 2 * (r % 4) + half
            kb.inc(SY.dma_start(out=ysb[:, kc * TOK + 1024 * (r // 4): kc * TOK + 1024 * (r // 4) + 1024],
                                in_=ys[128 * half:128 * half + 128, :]), "ld", 16)
    for kc in range(ND):
        kb.inc(G.dma_start(out=Wsb[:, kc * D:(kc + 1) * D], in_=Wo[kc * 128:(kc + 1) * 128, :]), "gd", 16)
    kb.inc(SY.dma_start(out=bosb[:], in_=bo), "ld", 16)
    kb.inc(SY.dma_start(out=w2sb[:], in_=w2), "ld", 16)
    kb.inc(SY.dma_start(out=b2sb[:], in_=b2), "ld", 16)
    kb.wait(Tn, "ld"); kb.wait(Tn, "gd"); kb.wait(S, "ld"); kb.wait(V, "ld")

    for t in range(NT):
        for oc in range(ND):
            pm = P_m[oc % 2]
            kb.wait(Tn, "s")
            for kc in range(ND):
                kb.inc(Tn.matmul(pm[:], Wsb[:, kc * D + oc * 128: kc * D + (oc + 1) * 128],
                                 ysb[:, kc * TOK + t * 512: kc * TOK + (t + 1) * 512],
                                 start=(kc == 0), stop=(kc == ND - 1)), "t")
            kb.wait(S, "t")
            kb.wait(S, "v")
            kb.inc(S.activation(zb[:, oc * 512:(oc + 1) * 512], pm[:], AF.Identity,
                                bias=bosb[:, oc:oc + 1]), "s")
        for kc in range(ND):
            kb.wait(V, "s")
            xnap = xnsb[:, kc * TOK + t * 512: kc * TOK + (t + 1) * 512]
            kb.inc(V.tensor_add(rm_f[:, kc * 512:(kc + 1) * 512], xnap, zb[:, kc * 512:(kc + 1) * 512]), "v")
        rmt = lambda kc: rm_f[:, kc * 512:(kc + 1) * 512]
        emit_stats(kb, w, rmt, c, P_s, P_q)
        emit_bcast(kb, c, w.nmb, P_b1)
        emit_bcast(kb, c, w.rsb, P_b2)
        for kc in range(ND):
            kb.wait(V, "t")
            kb.inc(V.tensor_add(w.t1[:], rmt(kc), P_b1[:]), "v")
            kb.inc(V.tensor_mul(w.t1[:], w.t1[:], P_b2[:]), "v")
            kb.inc(V.tensor_scalar(w.t1[:], w.t1[:], w2sb[:, kc:kc + 1], b2sb[:, kc:kc + 1],
                                   ALU.mult, ALU.add), "v")
            kb.inc(V.tensor_add(rp_f[:, kc * 512:(kc + 1) * 512], rmt(kc), w.t1[:]), "v")
        rpt = lambda kc: rp_f[:, kc * 512:(kc + 1) * 512]
        emit_stats(kb, w, rpt, c, P_s, P_q)
        emit_bcast(kb, c, w.nmb, P_b1)
        emit_bcast(kb, c, w.rsb, P_b2)
        for kc in range(ND):
            kb.wait(V, "t")
            xap = xsb[:, kc * TOK + t * 512: kc * TOK + (t + 1) * 512]
            kb.inc(V.tensor_add(w.t1[:], rpt(kc), P_b1[:]), "v")
            kb.inc(V.tensor_mul(w.t1[:], w.t1[:], P_b2[:]), "v")
            kb.inc(V.tensor_add(xap, xap, w.t1[:]), "v")
        kb.wait(SY, "v")
        dst = xo.rearrange("(kc p) tt -> p kc tt", p=128)[:, :, t * 512:(t + 1) * 512]
        src = xsb[:].rearrange("p (kc tt) -> p kc tt", tt=TOK)[:, :, t * 512:(t + 1) * 512]
        kb.inc(SY.dma_start(out=dst, in_=src), "st", 16)
        xt3 = lambda kc: xsb[:, kc * TOK + t * 512: kc * TOK + (t + 1) * 512]
        emit_stats(kb, w, xt3, c, P_s, P_q)
        emit_bcast(kb, c, w.nmb, P_b1)
        emit_bcast(kb, c, w.rsb, P_b2)
        for kc in range(ND):
            kb.wait(V, "t")
            kb.inc(V.tensor_add(w.t1[:], xt3(kc), P_b1[:]), "v")
            kb.inc(V.tensor_mul(rp_f[:, kc * 512:(kc + 1) * 512], w.t1[:], P_b2[:]), "v")
        kb.wait(SY, "v")
        dst2 = xn3o.rearrange("(kc p) tt -> p kc tt", p=128)[:, :, t * 512:(t + 1) * 512]
        kb.inc(SY.dma_start(out=dst2, in_=rp_f[:].rearrange("p (kc tt) -> p kc tt", tt=512)), "st", 16)


def emit_D(kb, c, knum, h_in, qA, qB, srA, srB, bA, bB, laA, laB, lbA, lbB, gam, bet,
           final, out_ap, xd=None):
    kb.phase(f"d{knum}")
    hsb = kb.sb("hsb", [128, ND * TOK], F32)
    hbb = kb.sb("hbb", [128, ND * TOK], BF16)
    yA = kb.sb("yA", [128, ND * TOK], BF16)
    lasb = kb.sb("lasb", [128, ND * 64], BF16)
    lbsb = kb.sb("lbsb", [32, 2 * D], BF16)
    bAsb = kb.sb("bAsb", [128, 8], F32)
    bBsb = kb.sb("bBsb", [128, 8], F32)
    gamsb = kb.sb("gamsb", [128, 8], F32)
    betsb = kb.sb("betsb", [128, 8], F32)
    lsf = kb.sb("lsf", [32, D], F32)
    qf = [kb.sb(f"qf{i}", [128, 512], F32) for i in range(2)]
    srf = [kb.sb(f"srf{i}", [128, 512], F32) for i in range(2)]
    qaf = kb.sb("qaf", [128, 512], F32)
    wsbd = [kb.sb(f"wsb_d{i}", [128, ND * D], BF16) for i in range(2)]
    xrb = kb.sb("xrb", [32, 512], BF16)
    w = LNW(kb, "ln")

    P_s = kb.ps("P_s", [1, 512])
    P_q = kb.ps("P_q", [1, 512])
    P_b1 = kb.ps("P_b1", [128, 512])
    P_b2 = kb.ps("P_b2", [128, 512])
    P_m = [kb.ps(f"P_m{i}", [128, 512]) for i in range(2)]
    P_l = kb.ps("P_l", [32, 512])

    SY, V, S, Tn, G = kb.SY, kb.V, kb.S, kb.T, kb.G
    load_x_fm(kb, hsb, h_in)
    kb.inc(SY.dma_start(out=bAsb[:], in_=bA), "ld", 16)
    kb.inc(SY.dma_start(out=bBsb[:], in_=bB), "ld", 16)
    kb.inc(SY.dma_start(out=gamsb[:], in_=gam), "ld", 16)
    kb.inc(SY.dma_start(out=betsb[:], in_=bet), "ld", 16)
    for kc in range(ND):
        kb.inc(G.dma_start(out=lasb[:, kc * 32:(kc + 1) * 32], in_=laA[kc * 128:(kc + 1) * 128, :]), "gd", 16)
        kb.inc(G.dma_start(out=lasb[:, 256 + kc * 32:256 + (kc + 1) * 32],
                           in_=laB[kc * 128:(kc + 1) * 128, :]), "gd", 16)
    kb.inc(SY.dma_start(out=lsf[:], in_=lbA), "ld", 16)
    kb.wait(V, "ld")
    kb.inc(V.tensor_copy(lbsb[:, 0:D], lsf[:]), "v")
    lb_mark = kb.n("v")
    kb.wait(SY, "v", lb_mark)
    kb.inc(SY.dma_start(out=lsf[:], in_=lbB), "ld", 16)
    kb.wait(V, "ld")
    kb.inc(V.tensor_copy(lbsb[:, D:2 * D], lsf[:]), "v")
    kb.wait(V, "ld")
    for kc in range(ND):
        kb.inc(V.tensor_copy(hbb[:, kc * TOK:(kc + 1) * TOK], hsb[:, kc * TOK:(kc + 1) * TOK]), "v")

    deq_marks = [0, 0]

    def dequant(qd, srd, dst):
        marks_s = [deq_marks[0], deq_marks[0]]
        marks_v = [deq_marks[1], deq_marks[1]]
        for kk in range(2 * ND):
            kc, hc = divmod(kk, 2)
            par = kk % 2
            pqf = qf[par]
            psr = srf[par]
            kb.wait(G, "s", marks_s[par])
            kb.wait(G, "v", marks_v[par])
            kb.inc(G.dma_start(out=pqf[:], in_=qd[kc * 128:(kc + 1) * 128,
                                                  hc * 512:(hc + 1) * 512]), "gd", 16)
            kb.inc(G.dma_start(out=psr[:], in_=srd[kc * 128:(kc + 1) * 128,
                                                   hc * 512:(hc + 1) * 512]), "gd", 16)
            bias_c = kb.fconst(-1.0, 128)
            kb.wait(S, "gd")
            kb.wait(S, "v")
            kb.wait(S, "g")
            kb.inc(S.activation(qaf[:], pqf[:], AF.Identity, scale=1.0 / 7.5, bias=bias_c), "s")
            marks_s[par] = kb.n("s")
            kb.wait(V, "s")
            kb.inc(V.tensor_mul(dst[:, kc * D + hc * 512: kc * D + (hc + 1) * 512],
                                qaf[:], psr[:]), "v")
            marks_v[par] = kb.n("v")
        deq_marks[0] = kb.n("s")
        deq_marks[1] = kb.n("v")

    def qlora(src_sb, bias_sb, la_off, lb_off, out_writer, wsb):
        pm_done = [None, None]
        for t in range(NT):
            kb.wait(Tn, "v")
            kb.wait(Tn, "gd")
            kb.wait(Tn, "s")
            for kc in range(ND):
                kb.inc(Tn.matmul(P_l[:], lasb[:, la_off + kc * 32: la_off + (kc + 1) * 32],
                                 src_sb[:, kc * TOK + t * 512: kc * TOK + (t + 1) * 512],
                                 start=(kc == 0), stop=(kc == ND - 1)), "t")
            kb.wait(S, "t")
            kb.inc(S.activation(xrb[:], P_l[0:32, :], AF.Identity), "s")
            xrb_mark = kb.n("s")
            for oc in range(ND):
                par = oc % 2
                pm = P_m[par]
                kb.wait(Tn, "s", xrb_mark)
                if pm_done[par] is not None:
                    kb.wait(Tn, pm_done[par][0], pm_done[par][1])
                kb.inc(Tn.matmul(pm[:], lbsb[0:32, lb_off + oc * 128: lb_off + (oc + 1) * 128],
                                 xrb[:], start=True, stop=False), "t")
                for kc in range(ND):
                    kb.inc(Tn.matmul(pm[:], wsb[:, kc * D + oc * 128: kc * D + (oc + 1) * 128],
                                     src_sb[:, kc * TOK + t * 512: kc * TOK + (t + 1) * 512],
                                     start=False, stop=(kc == ND - 1)), "t")
                pm_done[par] = out_writer(oc, t, pm, bias_sb)

    dequant(qA, srA, wsbd[0])
    def wA(oc, t, pm, bias_sb):
        kb.wait(S, "t")
        kb.inc(S.activation(yA[:, oc * TOK + t * 512: oc * TOK + (t + 1) * 512], pm[:],
                            AF.Identity, bias=bias_sb[:, oc:oc + 1]), "s")
        return ("s", kb.n("s"))
    qlora(hbb, bAsb, 0, 0, wA, wsbd[0])
    gb_sb = yA
    for t in range(NT):
        yt = lambda kc: yA[:, kc * TOK + t * 512: kc * TOK + (t + 1) * 512]
        emit_stats(kb, w, yt, c, P_s, P_q, src_bf16=True)
        emit_bcast(kb, c, w.nmb, P_b1)
        emit_bcast(kb, c, w.rsb, P_b2)
        for kc in range(ND):
            kb.wait(V, "t")
            kb.wait(V, "s")
            kb.inc(V.tensor_add(w.t1[:], yt(kc), P_b1[:]), "v")
            kb.inc(V.tensor_mul(w.t1[:], w.t1[:], P_b2[:]), "v")
            kb.inc(V.tensor_scalar(w.t1[:], w.t1[:], gamsb[:, kc:kc + 1], betsb[:, kc:kc + 1],
                                   ALU.mult, ALU.add), "v")
            kb.wait(S, "v")
            kb.inc(S.activation(yA[:, kc * TOK + t * 512: kc * TOK + (t + 1) * 512],
                                w.t1[:], AF.Gelu), "s")
    dequant(qB, srB, wsbd[1])
    def wB(oc, t, pm, bias_sb):
        kb.wait(V, "t")
        hap = hsb[:, oc * TOK + t * 512: oc * TOK + (t + 1) * 512]
        kb.inc(V.tensor_scalar_add(w.t1[:], pm[:], bias_sb[:, oc:oc + 1]), "v")
        kb.inc(V.tensor_add(hap, hap, w.t1[:]), "v")
        return ("v", kb.n("v"))
    qlora(gb_sb, bBsb, 256, D, wB, wsbd[1])

    if not final:
        kb.wait(SY, "v")
        dst = out_ap.rearrange("(kc p) tt -> p kc tt", p=128)
        kb.inc(SY.dma_start(out=dst, in_=hsb[:].rearrange("p (kc tt) -> p kc tt", tt=TOK)), "st", 16)
    else:
        xtile = [kb.sb(f"xtileD{i}", [128, 512], F32) for i in range(2)]
        xt_war = [0, 0]
        for t in range(NT):
            ht = lambda kc: hsb[:, kc * TOK + t * 512: kc * TOK + (t + 1) * 512]
            emit_stats(kb, w, ht, c, P_s, P_q)
            emit_bcast(kb, c, w.nmb, P_b1)
            emit_bcast(kb, c, w.rsb, P_b2)
            for kc in range(ND):
                xb_ = xtile[kc % 2]
                kb.wait(SY, "v", xt_war[kc % 2])
                kb.inc(SY.dma_start(out=xb_[:], in_=xd[kc * 128:(kc + 1) * 128,
                                                       t * 512:(t + 1) * 512]), "ld", 16)
                ld_mark = kb.n("ld")
                kb.wait(V, "t")
                kb.wait(V, "ld", ld_mark)
                kb.inc(V.tensor_add(w.t1[:], ht(kc), P_b1[:]), "v")
                kb.inc(V.tensor_mul(w.t1[:], w.t1[:], P_b2[:]), "v")
                kb.inc(V.tensor_add(ht(kc), xb_[:], w.t1[:]), "v")
                xt_war[kc % 2] = kb.n("v")
            kb.wait(SY, "v")
            dst = out_ap.rearrange("(kc p) tt -> p kc tt", p=128)[:, :, t * 512:(t + 1) * 512]
            hsrc = hsb[:].rearrange("p (kc tt) -> p kc tt", tt=TOK)[:, :, t * 512:(t + 1) * 512]
            kb.inc(SY.dma_start(out=dst, in_=hsrc), "st", 16)


DEBUG_TAPS = False


def build_mega():
    kb = KB()
    dtap = (lambda n, s, d: kb.dout(n, s, d)) if DEBUG_TAPS else (lambda n, s, d: kb.dint(n, s, d))
    x_in = kb.din("x", [D, TOK], F32)
    Wa = kb.din("Wa", [D, 3 * D], F32)
    ba = kb.din("ba", [128, 24], F32)
    Rq = kb.din("Rq", [128, D], BF16)
    Wp_a = kb.din("Wp_a", [D, D], F32)
    bp_a = kb.din("bp_a", [128, 8], F32)
    WqB = kb.din("WqB", [D, 1536], F32)
    bqB = kb.din("bqB", [128, 12], F32)
    anwB = kb.din("anwB", [128, 8], F32)
    ctabB = kb.din("ctabB", [128, TOK], BF16)
    stabB = kb.din("stabB", [128, TOK], BF16)
    PswB = kb.din("PswB", [128, 128], BF16)
    mselB = kb.din("mselB", [128, 8], F32)
    bmB = kb.din("bmB", [128, 7 * 512], BF16)
    snB = kb.din("snB", [1, 16], F32)
    WpB = kb.din("WpB", [D, D], F32)
    bpB = kb.din("bpB", [128, 8], F32)
    fnwB = kb.din("fnwB", [128, 8], F32)
    gwB = kb.din("gwB", [D, 32], F32)
    gbB = kb.din("gbB", [32, 1], F32)
    WuB = kb.din("WuB", [32 * D, 2048], F32)
    WdB = kb.din("WdB", [32 * D, D], F32)
    WqC = kb.din("WqC", [D, 3 * D], F32)
    bqC = kb.din("bqC", [128, 24], F32)
    w1C = kb.din("w1C", [128, 8], F32)
    b1C = kb.din("b1C", [128, 8], F32)
    WoC = kb.din("WoC", [D, D], F32)
    boC = kb.din("boC", [128, 8], F32)
    w2C = kb.din("w2C", [128, 8], F32)
    b2C = kb.din("b2C", [128, 8], F32)
    Dk = []
    for knum in range(3):
        Dk.append({nm: kb.din(f"{nm}{knum}",
                              [D, D] if nm in ("qA", "qB") else
                              ([D, D] if nm in ("srA", "srB") else
                               ([128, 8] if nm in ("bA", "bB", "gam", "bet") else
                                ([D, 32] if nm in ("laA", "laB") else [32, D]))),
                              I32 if nm in ("qA", "qB") else F32)
                   for nm in ("qA", "qB", "srA", "srB", "bA", "bB", "laA", "laB",
                              "lbA", "lbB", "gam", "bet")})
    xo_final = kb.dout("xo", [D, TOK], F32)

    X1i = kb.dint("X1i", [8, 3, SLOT], BF16)
    X1o = kb.dint("X1o", [8, 3, SLOT], BF16)
    X2i = kb.dint("X2i", [8, SLOT], BF16)
    X2o = kb.dint("X2o", [8, SLOT], BF16)
    X3i = kb.dint("X3i", [4, 32768], BF16)
    X3o = kb.dint("X3o", [8, 4, 32768], BF16)
    X4i = kb.dint("X4i", [8, 3, SLOT], BF16)
    X4o = kb.dint("X4o", [8, 3, SLOT], BF16)
    X5i = kb.dint("X5i", [8, SLOT], BF16)
    X5o = kb.dint("X5o", [8, SLOT], BF16)
    x1d = dtap("x1d", [D, TOK], F32)
    x2d = dtap("x2d", [D, TOK], F32)
    x3d = dtap("x3d", [D, TOK], F32)
    xn3d = dtap("xn3d", [D, TOK], F32)
    xnoB = dtap("xnoB", [D, TOK], BF16)
    QfB = dtap("QfB", [D, TOK], BF16)
    KoB = dtap("KoB", [256, TOK], BF16)
    VoB = dtap("VoB", [TOK, 256], BF16)
    xmB = dtap("xmB", [D, TOK], BF16)
    h2B = dtap("h2B", [D, TOK], BF16)
    ceB = dtap("ceB", [32, TOK], F32)
    xnoC = dtap("xnoC", [D, TOK], BF16)
    hD = [dtap(f"hD{i}", [D, TOK], F32) for i in range(2)]

    c = emit_consts(kb)
    kb.fconst(EPS)
    kb.fconst(-1.0, 128)

    emit_A1(kb, c, x_in, Wa, ba, Rq, X1i)
    kb.collective("AllToAll", X1i, X1o)
    emit_flash(kb, c, X1o, X2i, l1_mode=False, tag="f0")
    kb.collective("AllToAll", X2i, X2o)
    emit_PS(kb, c, x_in, X2o, Wp_a, bp_a, x1d)
    emit_B1(kb, c, x1d, WqB, bqB, anwB, ctabB, stabB, PswB, xnoB, QfB, KoB, VoB, X3i)
    kb.collective("AllGather", X3i, X3o)
    emit_B2(kb, c, QfB, KoB, VoB, X3o, mselB, bmB, snB, xnoB, WpB, bpB, fnwB, gwB, gbB,
            xmB, h2B, ceB)
    emit_B3(kb, c, h2B, ceB, xmB, x1d, WuB, WdB, x2d)
    emit_C1(kb, c, x2d, WqC, bqC, w1C, b1C, xnoC, X4i)
    kb.collective("AllToAll", X4i, X4o)
    emit_flash(kb, c, X4o, X5i, l1_mode=True, tag="f1")
    kb.collective("AllToAll", X5i, X5o)
    emit_C3(kb, c, x2d, X5o, xnoC, WoC, boC, w2C, b2C, x3d, xn3d)
    emit_D(kb, c, 0, xn3d, **{k: Dk[0][k] for k in Dk[0]}, final=False, out_ap=hD[0])
    emit_D(kb, c, 1, hD[0], **{k: Dk[1][k] for k in Dk[1]}, final=False, out_ap=hD[1])
    emit_D(kb, c, 2, hD[1], **{k: Dk[2][k] for k in Dk[2]}, final=True, out_ap=xo_final, xd=x3d)
    return kb.fin()


def rope_mats_l0():
    hd = 64
    theta = 1.0 / 10000.0 ** (np.arange(0, hd, 2, dtype=np.float64) / hd)
    R = np.zeros((128, D), np.float32)
    for ch in range(8):
        for hh in range(2):
            h = 2 * ch + hh
            base = 64 * hh
            for j in range(32):
                cth = np.cos(h * theta[j]); sth = np.sin(h * theta[j])
                R[base + 2 * j, ch * 128 + base + 2 * j] = cth
                R[base + 2 * j + 1, ch * 128 + base + 2 * j] = -sth
                R[base + 2 * j, ch * 128 + base + 2 * j + 1] = sth
                R[base + 2 * j + 1, ch * 128 + base + 2 * j + 1] = cth
    return R.astype(BF)


def l0f_rope_tables(positions):
    DH = 64
    pos = positions.astype(np.float64) / 32.0
    invf = 1.0 / (150000.0 ** (np.arange(0, DH, 2, dtype=np.float64) / DH))
    fr = pos[:, None] * invf[None, :]
    sn, cs = np.sin(fr), np.cos(fr)
    ct = np.zeros((64, len(positions)), np.float32)
    st = np.zeros((64, len(positions)), np.float32)
    for i in range(32):
        ct[2 * i] = cs[:, i]; ct[2 * i + 1] = cs[:, i]
        st[2 * i] = -sn[:, i]; st[2 * i + 1] = sn[:, i]
    return (np.ascontiguousarray(np.tile(ct, (2, 1))).astype(BF),
            np.ascontiguousarray(np.tile(st, (2, 1))).astype(BF))


def pair_swap_mat():
    P = np.zeros((128, 128), np.float32)
    for i in range(64):
        P[2 * i, 2 * i + 1] = 1.0
        P[2 * i + 1, 2 * i] = 1.0
    return P.astype(BF)


def bm_masks(core):
    bmm = np.zeros((128, 7 * 512), np.float32)
    ql = np.arange(512)[None, :]
    kl = np.arange(128)[:, None]
    for kt in range(5):
        dmm = ql - kl + 128 - 128 * kt
        bmm[:, kt * 512:(kt + 1) * 512] = ((dmm >= 0) & (dmm <= 128)).astype(np.float32)
    if core == 0:
        bmm[:, 5 * 512:] = 0.0
    else:
        bmm[:, 5 * 512:6 * 512] = bmm[:, 0:512]
        bmm[:, 6 * 512:7 * 512] = bmm[:, 0:512]
    return bmm.astype(BF)


_nc_cache = []


def kernel(**I):
    x = np.ascontiguousarray(I["L0_attn_x"].astype(np.float32).transpose(0, 2, 1))
    common = {}
    common["Wa"] = np.ascontiguousarray(I["L0_attn_W_attn"].astype(np.float32))
    common["ba"] = np.ascontiguousarray(I["L0_attn_b_attn"].astype(np.float32).reshape(24, 128).T)
    common["Rq"] = rope_mats_l0()
    common["Wp_a"] = np.ascontiguousarray(I["L0_attn_W_proj"].astype(np.float32))
    common["bp_a"] = np.ascontiguousarray(I["L0_attn_b_proj"].astype(np.float32).reshape(8, 128).T)
    common["WqB"] = np.ascontiguousarray(I["L0_ffn_qkv_w"].astype(np.float32))
    common["bqB"] = np.ascontiguousarray(I["L0_ffn_qkv_b"].astype(np.float32).reshape(12, 128).T)
    common["anwB"] = np.ascontiguousarray(I["L0_ffn_attn_norm_w"].astype(np.float32).reshape(8, 128).T)
    common["PswB"] = pair_swap_mat()
    common["snB"] = np.ascontiguousarray(I["L0_ffn_attn_sink"].astype(np.float32).reshape(1, 16))
    common["WpB"] = np.ascontiguousarray(I["L0_ffn_proj_w"].astype(np.float32))
    common["bpB"] = np.ascontiguousarray(I["L0_ffn_proj_b"].astype(np.float32).reshape(8, 128).T)
    common["fnwB"] = np.ascontiguousarray(I["L0_ffn_ffn_norm_w"].astype(np.float32).reshape(8, 128).T)
    common["gwB"] = np.ascontiguousarray(I["L0_ffn_gate_w"].astype(np.float32))
    common["gbB"] = np.ascontiguousarray(I["L0_ffn_gate_b"].astype(np.float32).reshape(32, 1))
    common["WuB"] = np.ascontiguousarray(I["L0_ffn_W_up"].astype(np.float32).reshape(32 * D, 2048))
    common["WdB"] = np.ascontiguousarray(I["L0_ffn_W_down"].astype(np.float32).reshape(32 * D, D))
    WQ = np.ascontiguousarray(I["L1_attn_W_Q"].astype(np.float32).transpose(1, 0, 2).reshape(D, D))
    WK = np.ascontiguousarray(I["L1_attn_W_K"].astype(np.float32).transpose(1, 0, 2).reshape(D, D))
    WV = np.ascontiguousarray(I["L1_attn_W_V"].astype(np.float32).transpose(1, 0, 2).reshape(D, D))
    common["WqC"] = np.ascontiguousarray(np.concatenate([WQ, WK, WV], axis=1))
    bqC = np.concatenate([I["L1_attn_b_Q"].reshape(-1), I["L1_attn_b_K"].reshape(-1),
                          I["L1_attn_b_V"].reshape(-1)]).astype(np.float32)
    common["bqC"] = np.ascontiguousarray(bqC.reshape(24, 128).T)
    common["w1C"] = np.ascontiguousarray(I["L1_attn_ln1_w"].astype(np.float32).reshape(8, 128).T)
    common["b1C"] = np.ascontiguousarray(I["L1_attn_ln1_b"].astype(np.float32).reshape(8, 128).T)
    common["WoC"] = np.ascontiguousarray(I["L1_attn_W_O"].astype(np.float32).reshape(D, D))
    common["boC"] = np.ascontiguousarray(I["L1_attn_b_O"].astype(np.float32).reshape(8, 128).T)
    common["w2C"] = np.ascontiguousarray(I["L1_attn_ln2_w"].astype(np.float32).reshape(8, 128).T)
    common["b2C"] = np.ascontiguousarray(I["L1_attn_ln2_b"].astype(np.float32).reshape(8, 128).T)
    for knum in range(3):
        i0, i1 = 2 * knum, 2 * knum + 1
        common[f"qA{knum}"] = np.ascontiguousarray(I["L1_ffn_qweights"][i0].astype(np.int32).T)
        common[f"qB{knum}"] = np.ascontiguousarray(I["L1_ffn_qweights"][i1].astype(np.int32).T)
        common[f"srA{knum}"] = np.ascontiguousarray(
            np.repeat(I["L1_ffn_scales"][i0].astype(np.float32), 16, axis=1).T)
        common[f"srB{knum}"] = np.ascontiguousarray(
            np.repeat(I["L1_ffn_scales"][i1].astype(np.float32), 16, axis=1).T)
        common[f"bA{knum}"] = np.ascontiguousarray(I["L1_ffn_biases"][i0].astype(np.float32).reshape(8, 128).T)
        common[f"bB{knum}"] = np.ascontiguousarray(I["L1_ffn_biases"][i1].astype(np.float32).reshape(8, 128).T)
        common[f"laA{knum}"] = np.ascontiguousarray(I["L1_ffn_lora_a"][i0].astype(np.float32).T)
        common[f"laB{knum}"] = np.ascontiguousarray(I["L1_ffn_lora_a"][i1].astype(np.float32).T)
        common[f"lbA{knum}"] = np.ascontiguousarray(I["L1_ffn_lora_b"][i0].astype(np.float32).T)
        common[f"lbB{knum}"] = np.ascontiguousarray(I["L1_ffn_lora_b"][i1].astype(np.float32).T)
        common[f"gam{knum}"] = np.ascontiguousarray(I["L1_ffn_ln_gamma"][knum].astype(np.float32).reshape(8, 128).T)
        common[f"bet{knum}"] = np.ascontiguousarray(I["L1_ffn_ln_beta"][knum].astype(np.float32).reshape(8, 128).T)

    in_maps = []
    for cc in range(NCORE):
        cols = slice(1024 * cc, 1024 * (cc + 1))
        xs = np.concatenate([x[0][:, cols], x[1][:, cols]], axis=1)
        pos = np.concatenate([np.arange(1024 * cc, 1024 * (cc + 1))] * 2)
        ct, st = l0f_rope_tables(pos)
        msel = np.zeros((128, 8), np.float32)
        if cc > 0:
            msel[:, cc - 1] = 1.0
        m = dict(common)
        m["x"] = np.ascontiguousarray(xs)
        m["ctabB"] = ct
        m["stabB"] = st
        m["mselB"] = msel
        m["bmB"] = bm_masks(cc)
        in_maps.append(m)

    if not _nc_cache:
        _nc_cache.append(build_mega())
    nc = _nc_cache[0]
    res = run_bass_kernel_spmd(nc, in_maps, core_ids=list(range(NCORE)))
    globals()["LAST"] = res
    out = np.zeros((B, D, T), np.float32)
    for cc in range(NCORE):
        r = res.results[cc]["xo"]
        out[0][:, 1024 * cc:1024 * (cc + 1)] = r[:, 0:1024]
        out[1][:, 1024 * cc:1024 * (cc + 1)] = r[:, 1024:2048]
    return np.ascontiguousarray(out.transpose(0, 2, 1))
